# version 07
# baseline (speedup 1.0000x reference)
import numpy as np
import ml_dtypes
import concourse.bass as bass
import concourse.mybir as mybir
from concourse.bass_utils import run_bass_kernel_spmd

F32 = mybir.dt.float32
BF16 = mybir.dt.bfloat16
F16 = mybir.dt.float16
AF = mybir.ActivationFunctionType
ALU = mybir.AluOpType
AX = mybir.AxisListType

NCORE = 8
TOK = 4096
D = 1024
NH = 16
NE = 8
HID = 4096
TILE = 512
NTILE = TOK // TILE
NSUB = TILE // 128
NCH = TOK // 128

DEBUG = []
ABLATE = set()


class Buf:
    __slots__ = ("name", "w", "readers")

    def __init__(self, name):
        self.name = name
        self.w = None
        self.readers = {}


class Op:
    __slots__ = ("eng", "fn", "waits", "dom", "inc", "flagged", "count_after", "i", "eng_seq", "raw")


RAW_K = 4


class Sched:
    def __init__(self):
        self.plan = []
        self.eng_n = {}

    def add(self, eng, fn, reads=(), writes=(), dma=False, cc=False, extra_waits=()):
        op = Op()
        op.eng = eng
        op.fn = fn
        op.dom = eng + "_dma" if dma else ("cc" if cc else eng)
        op.inc = 16 if dma else 1
        op.flagged = False
        op.count_after = None
        op.i = len(self.plan)
        op.eng_seq = self.eng_n.get(eng, 0)
        self.eng_n[eng] = op.eng_seq + 1
        ws = {}
        rw = {}

        def upd(d, w):
            c = d.get(w.dom)
            if c is None or w.i > c.i:
                d[w.dom] = w

        for b in reads:
            if b.w is not None:
                upd(ws, b.w)
                upd(rw, b.w)
        for b in writes:
            for r in b.readers.values():
                upd(ws, r)
            if b.w is not None:
                upd(ws, b.w)
        for w in extra_waits:
            if w is not None:
                upd(ws, w)
        waits = set(ws.values())
        raw = set(rw.values())
        waits.discard(op)
        raw.discard(op)
        for w in raw:
            waits.add(w)
        op.waits = sorted(waits, key=lambda o: o.i)
        op.raw = raw
        for b in reads:
            b.readers[op.dom] = op
        for b in writes:
            b.w = op
            b.readers = {}
        self.plan.append(op)
        return op

    def snapshot(self):
        last = {}
        for op in self.plan:
            last[op.dom] = op
        return list(last.values())

    def emit(self, nc):
        for op in self.plan:
            if op.dom.endswith("_dma") or op.dom == "cc":
                op.flagged = True
        def need_wait(op, w):
            if w.dom != op.eng:
                return True
            return (op.eng_seq - w.eng_seq) <= RAW_K

        for op in self.plan:
            for w in op.waits:
                if need_wait(op, w):
                    w.flagged = True
        counts = {}
        for op in self.plan:
            if op.flagged:
                counts[op.dom] = counts.get(op.dom, 0) + op.inc
                op.count_after = counts[op.dom]
        doms = set(counts.keys())
        sems = {}
        guards = []
        for d in sorted(doms):
            g = nc.semaphore("sem_" + d)
            sems[d] = g.__enter__()
            guards.append(g)

        engs = ("tensor", "vector", "scalar", "gpsimd", "sync")
        per_eng = {e: [op for op in self.plan if op.eng == e] for e in engs}

        def mk_body(eng):
            ops = per_eng[eng]

            def body(h):
                hw = {}
                for op in ops:
                    for w in op.waits:
                        if not need_wait(op, w):
                            continue
                        assert w.flagged, (w.i, op.i)
                        tgt = w.count_after
                        if hw.get(w.dom, 0) >= tgt:
                            continue
                        h.wait_ge(sems[w.dom], tgt)
                        hw[w.dom] = tgt
                    inst = op.fn(h)
                    if op.flagged:
                        inst.then_inc(sems[op.dom], op.inc)
            return body

        with nc.Block() as block:
            for eng in engs:
                if per_eng[eng]:
                    getattr(block, eng)(mk_body(eng))
        return guards


DT_SIZE = {F32: 4, BF16: 2, F16: 2}


def build_program(debug=(), ablate=frozenset()):
    nc = bass.Bass("TRN2", target_bir_lowering=False)
    S = Sched()

    x_in = nc.dram_tensor("x", [TOK, D], F32, kind="ExternalInput")
    W_in = {w: nc.dram_tensor(w, [D, D], F32, kind="ExternalInput") for w in ("Wq", "Wk", "Wv", "Wo")}
    bq_t_in = nc.dram_tensor("bq_t", [128, 8], F32, kind="ExternalInput")
    bo_t_in = nc.dram_tensor("bo_t", [128, 8], F32, kind="ExternalInput")
    bk_row_in = nc.dram_tensor("bk_row", [1, D], F32, kind="ExternalInput")
    bo_row_in = nc.dram_tensor("bo_row", [1, D], F32, kind="ExternalInput")
    bv_row_in = nc.dram_tensor("bv_row", [1, D], F32, kind="ExternalInput")
    wgred_in = nc.dram_tensor("wg_red", [16, D], F32, kind="ExternalInput")
    wg_in = nc.dram_tensor("wg", [NE, 16], F32, kind="ExternalInput")
    w1_in = nc.dram_tensor("w1", [NE, D, HID], F32, kind="ExternalInput")
    b1_t_in = nc.dram_tensor("b1_t", [128, 32 * NE], F32, kind="ExternalInput")
    w2_in = nc.dram_tensor("w2", [NE, HID, D], F32, kind="ExternalInput")
    b2row_in = nc.dram_tensor("b2row", [1, NE * D], F32, kind="ExternalInput")
    ident_in = nc.dram_tensor("ident", [128, 128], F32, kind="ExternalInput")
    ones_in = nc.dram_tensor("ones_row", [1, 512], F32, kind="ExternalInput")
    onesc_in = nc.dram_tensor("ones_col", [128, 1], F32, kind="ExternalInput")
    pw2_in = nc.dram_tensor("pw2", [1, NE], F32, kind="ExternalInput")
    bmask_in = nc.dram_tensor("bmask", [1, 2], F32, kind="ExternalInput")

    y_out = nc.dram_tensor("y", [TOK, D], F32, kind="ExternalOutput")

    xn_spill = nc.dram_tensor("xn_spill", [NTILE, D, TILE], BF16)
    q_spill = nc.dram_tensor("q_spill", [NTILE, D, TILE], BF16)
    cc_in = nc.dram_tensor("cc_in", [2 * 128, 520], F32)
    cc_out = nc.dram_tensor("cc_out", [2 * 128, 520], F32)

    dbg = {}
    if "xn" in debug:
        dbg["xn"] = nc.dram_tensor("dbg_xn", [NTILE, D, TILE], BF16, kind="ExternalOutput")
    if "q" in debug:
        dbg["q"] = nc.dram_tensor("dbg_q", [NTILE, D, TILE], BF16, kind="ExternalOutput")
    if "kv" in debug:
        dbg["kv"] = nc.dram_tensor("dbg_kv", [128, 520], F32, kind="ExternalOutput")
    if "kvfull" in debug:
        dbg["kvfull"] = nc.dram_tensor("dbg_kvfull", [128, 1040], F32, kind="ExternalOutput")
    if "C" in debug:
        dbg["C"] = nc.dram_tensor("dbg_C", [128, NCH * NE], F32, kind="ExternalOutput")
    if "Sc" in debug:
        dbg["Sc"] = nc.dram_tensor("dbg_Sc", [128, NCH], F32, kind="ExternalOutput")
    if "gpre" in debug:
        dbg["gpre"] = nc.dram_tensor("dbg_gpre", [NCH, 128, D], F16, kind="ExternalOutput")
    if "xr" in debug:
        dbg["xr"] = nc.dram_tensor("dbg_xr", [NTILE, 16, TILE], F32, kind="ExternalOutput")
    if "lg" in debug:
        dbg["lg"] = nc.dram_tensor("dbg_lg", [128, NCH * NE], F32, kind="ExternalOutput")
    if "lnraw" in debug:
        dbg["lnraw"] = nc.dram_tensor("dbg_lnraw", [128, D], F32, kind="ExternalOutput")
        dbg["lnst"] = nc.dram_tensor("dbg_lnst", [128, 8], F32, kind="ExternalOutput")
    if "ktm" in debug:
        dbg["ktm"] = nc.dram_tensor("dbg_ktm", [128, D], BF16, kind="ExternalOutput")
        dbg["vtm"] = nc.dram_tensor("dbg_vtm", [128, D], BF16, kind="ExternalOutput")

    SB_BASE = (nc.sbuf_base + 63) & ~63
    SB_TOP = nc.sbuf_top

    class Arena:
        def __init__(self, base):
            self.off = base
            self.hi = base

        def take(self, name, shape, dtype):
            nbytes = int(np.prod(shape[1:])) * DT_SIZE[dtype]
            off = (self.off + 63) & ~63
            t = nc.alloc_sbuf_tensor_at(name, list(shape), dtype, offset=off)
            self.off = off + nbytes
            self.hi = max(self.hi, self.off)
            assert self.off <= SB_TOP, (name, self.off, SB_TOP)
            return t, Buf(name)

    pers = Arena(SB_BASE)
    g_pre, b_gpre = pers.take("g_pre", [128, NCH * D], F16)
    b_gpre_ch = [Buf(f"g_pre{g}") for g in range(NCH)]
    C_all, b_C = pers.take("C_all", [128, NCH * NE], F32)
    S_all, b_S = pers.take("S_all", [128, NCH], F32)
    b1t, b_b1t = pers.take("b1t", [128, 32 * NE], F32)
    bqt, b_bqt = pers.take("bqt", [128, 8], F32)
    bot, b_bot = pers.take("bot", [128, 8], F32)
    idf, b_idf = pers.take("idf", [128, 128], F32)
    idb, b_idb = pers.take("idb", [128, 128], BF16)
    onesf, b_onesf = pers.take("onesf", [1, 512], F32)
    onesb, b_onesb = pers.take("onesb", [1, 512], BF16)
    onescb, b_onescb = pers.take("onescb", [128, 1], BF16)
    pw2b, b_pw2 = pers.take("pw2b", [128, NE], F32)
    bmb, b_bmb = pers.take("bmb", [128, 2], F32)
    ARENA_BASE = (pers.off + 63) & ~63

    b_ccin, b_ccout = Buf("cc_in"), Buf("cc_out")
    b_xnsp, b_qsp = Buf("xn_spill"), Buf("q_spill")
    b_y = Buf("y")

    P512 = []
    bP512 = []
    for i in range(6):
        t = nc.alloc_psum_tensor(f"ps{i}", [128, 512], F32)
        P512.append(t)
        bP512.append(Buf(f"ps{i}"))
    P1024 = nc.alloc_psum_tensor("ps1024", [128, 1024], F32)
    bP1024 = Buf("ps1024")

    def dma(eng, dst, src, reads=(), writes=(), extra=()):
        return S.add(eng, lambda h, d=dst, s=src: h.dma_start(out=d, in_=s),
                     reads=reads, writes=writes, dma=True, extra_waits=extra)

    dma("gpsimd", idb[:, :], ident_in[:, :], writes=[b_idb])
    dma("gpsimd", onesb[:, :], ones_in[:, :], writes=[b_onesb])
    dma("gpsimd", onescb[:, :], onesc_in[:, :], writes=[b_onescb])
    dma("sync", b1t[:, :], b1_t_in[:, :], writes=[b_b1t])
    dma("sync", bqt[:, :], bq_t_in[:, :], writes=[b_bqt])
    dma("sync", bot[:, :], bo_t_in[:, :], writes=[b_bot])
    dma("sync", idf[:, :], ident_in[:, :], writes=[b_idf])
    dma("sync", onesf[:, :], ones_in[:, :], writes=[b_onesf])
    dma("sync", pw2b[:, :], pw2_in.ap().broadcast_to([128, NE]), writes=[b_pw2])
    dma("sync", bmb[:, :], bmask_in.ap().broadcast_to([128, 2]), writes=[b_bmb])

    def mm(out, lhsT, rhs, start, stop, reads, writes, skip_gc=True, extra=()):
        return S.add("tensor",
                     lambda h, o=out, l=lhsT, r=rhs, st=start, sp=stop, sg=skip_gc:
                     h.matmul(o, l, r, start=st, stop=sp, skip_group_check=sg),
                     reads=reads, writes=writes, extra_waits=extra)

    def tr(out, in_, ident, reads, writes, extra=()):
        return S.add("tensor", lambda h, o=out, i=in_, d=ident: h.transpose(o, i, d),
                     reads=reads, writes=writes, extra_waits=extra)

    def act(out, in_, func, bias=0.0, scale=1.0, reads=(), writes=(), extra=()):
        return S.add("scalar",
                     lambda h, o=out, i=in_, f=func, b=bias, sc=scale: h.activation(o, i, f, bias=b, scale=sc),
                     reads=reads, writes=writes, extra_waits=extra)

    def vop(name, *aps, reads=(), writes=(), extra=(), eng="vector"):
        def fn(h, n=name, a=aps):
            return getattr(h, n)(*a)
        return S.add(eng, fn, reads=reads, writes=writes, extra_waits=extra)

    a1 = Arena(ARENA_BASE)
    WqT, b_WqT = a1.take("WqT", [128, 8 * D], BF16)
    WkT, b_WkT = a1.take("WkT", [128, 8 * D], BF16)
    WvT, b_WvT = a1.take("WvT", [128, 8 * D], BF16)
    wstage, b_wstage = a1.take("wstage", [128, D], BF16)
    wgredT, b_wgredT = a1.take("wgredT", [128, 8 * 16], F32)
    wgr_st, b_wgr_st = a1.take("wgr_st", [16, D], F32)
    wgu, b_wgu = a1.take("wgu", [NE, 16], F32)
    wguT, b_wguT = a1.take("wguT", [16, NE], F32)
    wgtmp, b_wgtmp = a1.take("wgtmp", [NE, 20], F32)
    bkr, b_bkr = a1.take("bkr", [1, D], BF16)
    bvr, b_bvr = a1.take("bvr", [1, D], BF16)
    xh, _ = a1.take("xh", [128, 2 * D], F32)
    b_xh = [Buf("xh0"), Buf("xh1")]
    ctr2, _ = a1.take("ctr2", [128, 2 * D], F32)
    b_ctr2 = [Buf("ctr0"), Buf("ctr1")]
    scr1, b_scr1 = a1.take("scr1", [128, D], F32)
    scr22, _ = a1.take("scr22", [128, 2 * D], F32)
    b_scr22 = [Buf("xnl0"), Buf("xnl1")]
    scr3, b_scr3 = a1.take("scr3", [128, D], F32)
    lnst2, _ = a1.take("lnst2", [128, 16], F32)
    b_lnst2 = [Buf("lnst0"), Buf("lnst1")]
    xnf_bf, b_xnbf = a1.take("xnf_bf", [128, 8 * TILE], BF16)
    xnf_f32, b_xnf32 = a1.take("xnf_f32", [128, 8 * 256], F32)
    q_tile, b_q = a1.take("q_tile", [128, 8 * TILE], BF16)
    k_tm, b_ktm = a1.take("k_tm", [128, D], BF16)
    v_tm, b_vtm = a1.take("v_tm", [128, D], BF16)
    xr_sb, b_xr = a1.take("xr_sb", [16, TILE], F32)
    gsc, b_gsc = a1.take("gsc", [128, 8 * 8], F32)
    qsc, b_qsc = a1.take("qsc", [128, D], F32)
    if "lg" in debug:
        lgbuf, b_lg = a1.take("lgbuf", [128, NCH * NE], F32)
    kvacc, b_kvacc = a1.take("kvacc", [128, 520], F32)
    contrib, b_contrib = a1.take("contrib", [128, 2 * 520], F32)

    ps_kvh, b_pkvh = P512[0], bP512[0]
    ps_ksum, b_pksum = P512[1], bP512[1]
    ps_trA, b_trA = P512[2], bP512[2]
    ps_trB, b_trB = P512[3], bP512[3]
    ps_mmA, b_mmA = P512[4], bP512[4]
    ps_mmB, b_mmB = P512[5], bP512[5]
    ps_kv2, b_pkv2 = P1024, bP1024

    dma("gpsimd", bkr[:, :], bk_row_in[:, :], writes=[b_bkr])
    dma("gpsimd", bvr[:, :], bv_row_in[:, :], writes=[b_bvr])

    def prep_wT(nm, WT, b_WT):
        for oc in range(8):
            dma("gpsimd", wstage[:, :], W_in[nm][oc * 128:(oc + 1) * 128, :],
                reads=(), writes=[b_wstage])
            for ic in range(8):
                psb, bps = (ps_trA, b_trA) if ic % 2 == 0 else (ps_trB, b_trB)
                pbf = psb.ap().bitcast(BF16)
                tr(pbf[:, 0:128], wstage[:, ic * 128:(ic + 1) * 128], idb[:, :],
                   reads=[b_wstage, b_idb], writes=[bps])
                S.add("scalar",
                      lambda h, w=WT, c=ic * D + oc * 128, p=pbf: h.copy(w[:, c:c + 128], p[:, 0:128]),
                      reads=[bps], writes=[b_WT])

    prep_wT("Wq", WqT, b_WqT)
    prep_wT("Wk", WkT, b_WkT)
    prep_wT("Wv", WvT, b_WvT)

    dma("sync", wgr_st[:, :], wgred_in[:, :], writes=[b_wgr_st])
    for ic in range(8):
        psb, bps = (ps_trA, b_trA) if ic % 2 == 0 else (ps_trB, b_trB)
        tr(psb[0:128, 0:16], wgr_st[0:16, ic * 128:(ic + 1) * 128], idf[0:16, 0:16],
           reads=[b_wgr_st, b_idf], writes=[bps])
        vop("tensor_copy", wgredT[:, ic * 16:(ic + 1) * 16], psb[0:128, 0:16],
            reads=[bps], writes=[b_wgredT])

    dma("sync", wgu[:, :], wg_in[:, :], writes=[b_wgu])
    act(wgtmp[:, 0:16], wgu[:, :], AF.Square, reads=[b_wgu], writes=[b_wgtmp])
    vop("reduce_sum", wgtmp[:, 16:17], wgtmp[:, 0:16], AX.X, reads=[b_wgtmp], writes=[b_wgtmp])
    act(wgtmp[:, 17:18], wgtmp[:, 16:17], AF.Sqrt, reads=[b_wgtmp], writes=[b_wgtmp])
    vop("tensor_scalar_max", wgtmp[:, 17:18], wgtmp[:, 17:18], 1e-4, reads=[b_wgtmp], writes=[b_wgtmp])
    vop("reciprocal", wgtmp[:, 18:19], wgtmp[:, 17:18], reads=[b_wgtmp], writes=[b_wgtmp])
    S.add("vector", lambda h: h.tensor_scalar(wgu[:, :], wgu[:, :], wgtmp[:, 18:19], None, ALU.mult),
          reads=[b_wgtmp], writes=[b_wgu])
    tr(ps_trA[0:16, 0:NE], wgu[:, :], idf[0:NE, 0:NE], reads=[b_wgu, b_idf], writes=[b_trA])
    vop("tensor_copy", wguT[:, :], ps_trA[0:16, 0:NE], reads=[b_trA], writes=[b_wguT])

    for t in range(NTILE):
        if t == 0:
            for k0 in range(2):
                dma("sync", xh[:, (k0 % 2) * D:(k0 % 2 + 1) * D],
                    x_in[k0 * 128:(k0 + 1) * 128, :],
                    reads=[b_xh[k0 % 2]], writes=[b_xh[k0 % 2]])
        for s in range(NSUB):
            kg = t * NSUB + s
            par = kg % 2
            pp = s % 2
            xsub = xh[:, par * D:(par + 1) * D]
            bx = b_xh[par]
            ctr = ctr2[:, pp * D:(pp + 1) * D]
            b_ctr = b_ctr2[pp]
            scr2 = scr22[:, pp * D:(pp + 1) * D]
            b_scr2 = b_scr22[pp]
            lnst = lnst2[:, pp * 8:(pp + 1) * 8]
            b_lnst = b_lnst2[pp]
            vop("reduce_sum", lnst[:, 0:1], xsub, AX.X, reads=[bx], writes=[b_lnst])
            vop("tensor_scalar_mul", lnst[:, 1:2], lnst[:, 0:1], 1.0 / D, reads=[b_lnst], writes=[b_lnst])
            S.add("vector", lambda h, xs=xsub, ctr=ctr, lnst=lnst: h.tensor_scalar(ctr[:, :], xs, lnst[:, 1:2], None, ALU.subtract),
                  reads=[bx, b_lnst], writes=[b_ctr])
            act(scr2[:, :], ctr[:, :], AF.Square, reads=[b_ctr], writes=[b_scr2])
            vop("reduce_sum", lnst[:, 2:3], scr2[:, :], AX.X, reads=[b_scr2], writes=[b_lnst])
            act(lnst[:, 3:4], lnst[:, 2:3], AF.Ln, bias=np.float32(1e-5), scale=1.0 / D, reads=[b_lnst], writes=[b_lnst])
            act(lnst[:, 4:5], lnst[:, 3:4], AF.Exp, scale=-0.5, reads=[b_lnst], writes=[b_lnst])
            S.add("vector", lambda h, ctr=ctr, lnst=lnst, scr2=scr2: h.tensor_scalar(scr2[:, :], ctr[:, :], lnst[:, 4:5], None, ALU.mult),
                  reads=[b_ctr, b_lnst], writes=[b_scr2])
            for ic in range(8):
                psb, bps = (ps_trA, b_trA) if ic % 2 == 0 else (ps_trB, b_trB)
                slot = (ic // 2) % 2
                tr(psb[:, slot * 128:slot * 128 + 128], scr2[:, ic * 128:(ic + 1) * 128], idf[:, :],
                   reads=[b_scr2, b_idf], writes=[bps])
                col = ic * TILE + s * 128
                colf = ic * 256 + pp * 128
                vop("tensor_copy", xnf_f32[:, colf:colf + 128], psb[:, slot * 128:slot * 128 + 128],
                    reads=[bps], writes=[b_xnf32])
                S.add("scalar", lambda h, c=col, cf=colf: h.copy(xnf_bf[:, c:c + 128], xnf_f32[:, cf:cf + 128]),
                      reads=[b_xnf32], writes=[b_xnbf])
            if s % 2 == 1 and "gating" not in ablate:
                hf2 = s // 2
                for ic in range(8):
                    mm(ps_mmA[0:16, hf2 * 256:hf2 * 256 + 256], wgredT[:, ic * 16:(ic + 1) * 16],
                       xnf_f32[:, ic * 256:(ic + 1) * 256],
                       start=(ic == 0), stop=(ic == 7), reads=[b_wgredT, b_xnf32], writes=[b_mmA])
                vop("tensor_copy", xr_sb[:, hf2 * 256:hf2 * 256 + 256], ps_mmA[0:16, hf2 * 256:hf2 * 256 + 256],
                    reads=[b_mmA], writes=[b_xr])
            if kg + 2 < NTILE * NSUB:
                k2 = kg + 2
                dma("sync", xh[:, (k2 % 2) * D:(k2 % 2 + 1) * D],
                    x_in[k2 * 128:(k2 + 1) * 128, :],
                    reads=[b_xh[k2 % 2]], writes=[b_xh[k2 % 2]])

        dma("scalar", xn_spill[t].rearrange("(c p) f -> p c f", p=128),
            xnf_bf.ap().rearrange("p (c f) -> p c f", c=8),
            reads=[b_xnbf], writes=[b_xnsp])
        if "xn" in dbg:
            dma("scalar", dbg["xn"][t].rearrange("(c p) f -> p c f", p=128),
                xnf_bf.ap().rearrange("p (c f) -> p c f", c=8), reads=[b_xnbf])

        for oc in (range(8) if "q" not in ablate else ()):
            psq, bpsq = (ps_mmA, b_mmA) if oc % 2 == 0 else (ps_mmB, b_mmB)
            for ic in range(8):
                mm(psq[:, :], WqT[:, ic * D + oc * 128: ic * D + oc * 128 + 128],
                   xnf_bf[:, ic * TILE:(ic + 1) * TILE],
                   start=(ic == 0), stop=(ic == 7),
                   reads=[b_WqT, b_xnbf], writes=[bpsq])
            act(scr1[:, 0:TILE], psq[:, :], AF.Relu, bias=bqt[:, oc:oc + 1],
                reads=[bpsq, b_bqt], writes=[b_scr1])
            S.add("vector",
                  lambda h, p=psq, oc=oc: h.scalar_tensor_tensor(
                      qsc[:, 0:TILE], p[:, :], bqt[:, oc:oc + 1], scr1[:, 0:TILE], ALU.add, ALU.subtract),
                  reads=[bpsq, b_bqt, b_scr1], writes=[b_qsc])
            act(scr3[:, 0:TILE], qsc[:, 0:TILE], AF.Exp, reads=[b_qsc], writes=[b_scr3])
            vop("tensor_add", q_tile[:, oc * TILE:(oc + 1) * TILE], scr3[:, 0:TILE], scr1[:, 0:TILE],
                reads=[b_scr3, b_scr1], writes=[b_q])
        dma("scalar", q_spill[t].rearrange("(c p) f -> p c f", p=128),
            q_tile.ap().rearrange("p (c f) -> p c f", c=8),
            reads=[b_q], writes=[b_qsp])
        if "q" in dbg:
            dma("scalar", dbg["q"][t].rearrange("(c p) f -> p c f", p=128),
                q_tile.ap().rearrange("p (c f) -> p c f", c=8), reads=[b_q])

        for s in (range(NSUB) if "kv" not in ablate else ()):
            for half in range(2):
                for ic in range(8):
                    mm(ps_kv2[:, half * 512:(half + 1) * 512],
                       xnf_bf[:, ic * TILE + s * 128: ic * TILE + s * 128 + 128],
                       WkT[:, ic * D + half * 512: ic * D + (half + 1) * 512],
                       start=(ic == 0), stop=False,
                       reads=[b_xnbf, b_WkT], writes=[b_pkv2])
                mm(ps_kv2[:, half * 512:(half + 1) * 512],
                   onesb[0:1, 0:128], bkr[0:1, half * 512:(half + 1) * 512],
                   start=False, stop=True,
                   reads=[b_onesb, b_bkr], writes=[b_pkv2])
            act(scr1[:, :], ps_kv2[:, :], AF.Relu, reads=[b_pkv2], writes=[b_scr1])
            vop("tensor_sub", qsc[:, :], ps_kv2[:, :], scr1[:, :], reads=[b_pkv2, b_scr1], writes=[b_qsc])
            act(scr3[:, :], qsc[:, :], AF.Exp, reads=[b_qsc], writes=[b_scr3])
            vop("tensor_add", k_tm[:, :], scr3[:, :], scr1[:, :], reads=[b_scr3, b_scr1], writes=[b_ktm])
            for half in range(2):
                psv, bpsv = (ps_mmA, b_mmA) if half == 0 else (ps_mmB, b_mmB)
                for ic in range(8):
                    mm(psv[:, :],
                       xnf_bf[:, ic * TILE + s * 128: ic * TILE + s * 128 + 128],
                       WvT[:, ic * D + half * 512: ic * D + (half + 1) * 512],
                       start=(ic == 0), stop=False,
                       reads=[b_xnbf, b_WvT], writes=[bpsv])
                mm(psv[:, :], onesb[0:1, 0:128], bvr[0:1, half * 512:(half + 1) * 512],
                   start=False, stop=True, reads=[b_onesb, b_bvr], writes=[bpsv])
                vop("tensor_copy", v_tm[:, half * 512:(half + 1) * 512], psv[:, :],
                    reads=[bpsv], writes=[b_vtm])
            if "ktm" in dbg and t == 0 and s == 0:
                dma("scalar", dbg["ktm"][:, :], k_tm[:, :], reads=[b_ktm])
                dma("scalar", dbg["vtm"][:, :], v_tm[:, :], reads=[b_vtm])
            for h in range(NH):
                pb, fo = (h % 2) * 64, (h // 2) * 64
                mm(ps_kvh[pb:pb + 64, fo:fo + 64],
                   k_tm[:, h * 64:(h + 1) * 64], v_tm[:, h * 64:(h + 1) * 64],
                   start=(s == 0 and h < 2),
                   stop=(s == NSUB - 1),
                   reads=[b_ktm, b_vtm], writes=[b_pkvh], skip_gc=True)
                mm(ps_ksum[pb:pb + 64, (h // 2):(h // 2) + 1],
                   k_tm[:, h * 64:(h + 1) * 64], onescb[:, 0:1],
                   start=(s == 0 and h < 2), stop=(s == NSUB - 1),
                   reads=[b_ktm, b_onescb], writes=[b_pksum], skip_gc=True)

        if "kv" in ablate:
            pass
        elif t == 0:
            vop("tensor_copy", kvacc[:, 0:512], ps_kvh[:, :], reads=[b_pkvh], writes=[b_kvacc])
            vop("tensor_copy", kvacc[:, 512:520], ps_ksum[:, 0:8], reads=[b_pksum], writes=[b_kvacc])
        else:
            vop("tensor_add", kvacc[:, 0:512], kvacc[:, 0:512], ps_kvh[:, :], reads=[b_pkvh, b_kvacc], writes=[b_kvacc])
            vop("tensor_add", kvacc[:, 512:520], kvacc[:, 512:520], ps_ksum[:, 0:8], reads=[b_pksum, b_kvacc], writes=[b_kvacc])

        if "xr" in dbg:
            dma("scalar", dbg["xr"][t], xr_sb[:, :], reads=[b_xr])
        for s in (range(NSUB) if "gating" not in ablate else ()):
            g = t * NSUB + s
            mm(ps_trB[0:128, 0:NE], xr_sb[:, s * 128:(s + 1) * 128], wguT[:, :],
               start=True, stop=True, reads=[b_xr, b_wguT], writes=[b_trB])
            if "lg" in dbg:
                vop("tensor_copy", lgbuf[:, g * NE:(g + 1) * NE], ps_trB[0:128, 0:NE], reads=[b_trB], writes=[b_lg])
            vop("reduce_max", gsc[:, 0:1], ps_trB[0:128, 0:NE], AX.X, reads=[b_trB], writes=[b_gsc])
            S.add("vector", lambda h: h.tensor_scalar(gsc[:, 8:16], ps_trB[0:128, 0:NE], gsc[:, 0:1], None, ALU.subtract),
                  reads=[b_trB, b_gsc], writes=[b_gsc])
            act(gsc[:, 16:24], gsc[:, 8:16], AF.Exp, reads=[b_gsc], writes=[b_gsc])
            vop("reduce_sum", gsc[:, 1:2], gsc[:, 16:24], AX.X, reads=[b_gsc], writes=[b_gsc])
            S.add("vector", lambda h, g=g: h.reciprocal(S_all[:, g:g + 1], gsc[:, 1:2]),
                  reads=[b_gsc], writes=[b_S])
            S.add("vector", lambda h: h.tensor_scalar(gsc[:, 24:32], gsc[:, 8:16], 0.0, None, ALU.is_equal),
                  reads=[b_gsc], writes=[b_gsc])
            vop("tensor_mul", gsc[:, 32:40], gsc[:, 24:32], pw2b[:, :], reads=[b_gsc, b_pw2], writes=[b_gsc])
            vop("reduce_max", gsc[:, 2:3], gsc[:, 32:40], AX.X, reads=[b_gsc], writes=[b_gsc])
            S.add("vector", lambda h: h.tensor_scalar(gsc[:, 40:48], gsc[:, 32:40], gsc[:, 2:3], None, ALU.is_equal),
                  reads=[b_gsc], writes=[b_gsc])
            S.add("vector", lambda h, g=g: h.tensor_copy(C_all[:, g * NE:(g + 1) * NE], gsc[:, 40:48]),
                  reads=[b_gsc], writes=[b_C])

    if True:
        S.add("vector", lambda h: h.tensor_scalar(contrib[:, 0:520], kvacc[:, :], bmb[:, 0:1], None, ALU.mult),
          reads=[b_kvacc, b_bmb], writes=[b_contrib])
    S.add("vector", lambda h: h.tensor_scalar(contrib[:, 520:1040], kvacc[:, :], bmb[:, 1:2], None, ALU.mult),
          reads=[b_kvacc, b_bmb], writes=[b_contrib])
    if "kv" in dbg:
        dma("scalar", dbg["kv"][:, :], kvacc[:, :], reads=[b_kvacc])
    dma("gpsimd", cc_in.ap().rearrange("(b p) f -> p b f", p=128),
        contrib.ap().rearrange("p (b f) -> p b f", b=2),
        reads=[b_contrib], writes=[b_ccin])
    cc_op = S.add("gpsimd",
                  lambda h: h.collective_compute("AllReduce", ALU.add,
                                                 replica_groups=[list(range(NCORE))],
                                                 ins=[cc_in.ap().opt()], outs=[cc_out.ap().opt()]),
                  reads=[b_ccin], writes=[b_ccout], cc=True)

    fence1 = S.snapshot()

    a2 = Arena(ARENA_BASE)
    xnr, b_xnr = a2.take("xnr", [128, 8 * 2048], BF16)
    b_xnr4 = [Buf(f"xnr{i}") for i in range(4)]
    xns, _ = a2.take("xns", [128, 2 * 8 * 512], BF16)
    b_xns = [Buf("xns0"), Buf("xns1")]
    w1q, _ = a2.take("w1q", [128, 2 * 8 * 1024], BF16)
    b_w1q = [Buf("w1q0"), Buf("w1q1")]
    w2q, _ = a2.take("w2q", [128, 2 * 8 * 1024], BF16)
    b_w2q = [Buf("w2q0"), Buf("w2q1")]
    b2e, _ = a2.take("b2e", [1, 2 * D], BF16)
    b_b2e = [Buf("b2e0"), Buf("b2e1")]
    h_sb, b_h = a2.take("h_sb", [128, 8 * 512], BF16)

    ps_hA, b_hA = P512[0], bP512[0]
    ps_hB, b_hB = P512[1], bP512[1]
    ps_gA, b_gA = P1024, bP1024
    ps_gB0, b_gB0 = P512[2], bP512[2]
    ps_gB1, b_gB1 = P512[3], bP512[3]

    xnr4 = xnr.ap().rearrange("p (c tt) -> p c tt", c=8)
    xns4 = xns.ap().rearrange("p (a c f) -> p a c f", a=2, c=8)
    w1q4 = w1q.ap().rearrange("p (a c f) -> p a c f", a=2, c=8)
    w2q4 = w2q.ap().rearrange("p (a c f) -> p a c f", a=2, c=8)

    S.add("vector", lambda h: h.memset(g_pre[:, :], 0.0), reads=[b_gpre],
          writes=[b_gpre] + b_gpre_ch, extra_waits=())

    for t in range(4):
        dma("sync" if t == 0 else "scalar", xnr4[:, :, t * TILE:(t + 1) * TILE],
            xn_spill[t].rearrange("(c p) f -> p c f", p=128),
            reads=[b_xnsp], writes=[b_xnr, b_xnr4[t]], extra=fence1)

    b_w1q0h = [Buf("w1q0a"), Buf("w1q0b")]
    b_w2q0h = [Buf("w2q0a"), Buf("w2q0b")]

    def w_load(j):
        e, qtr = divmod(j, 4)
        par = j % 2
        if j == 0:
            for hh in range(2):
                dma("gpsimd", w1q4[:, 0, :, hh * 512:(hh + 1) * 512],
                    w1_in[e][:, hh * 512:(hh + 1) * 512].rearrange("(c p) f -> p c f", p=128),
                    reads=[b_w1q[0]], writes=[b_w1q0h[hh]], extra=fence1)
            for hh in range(2):
                dma("gpsimd", w2q4[:, 0, hh * 4:(hh + 1) * 4, :],
                    w2_in[e][hh * 512:(hh + 1) * 512, :].rearrange("(c p) f -> p c f", p=128),
                    reads=[b_w2q[0]], writes=[b_w2q0h[hh]], extra=fence1)
        else:
            extra_w = [b_w1q0h[0].w, b_w1q0h[1].w] if j == 2 else []
            extra_w2 = [b_w2q0h[0].w, b_w2q0h[1].w] if j == 2 else []
            dma("gpsimd", w1q4[:, par, :, :],
                w1_in[e][:, qtr * 1024:(qtr + 1) * 1024].rearrange("(c p) f -> p c f", p=128),
                reads=[b_w1q[par]] + ([b_w1q0h[0], b_w1q0h[1]] if j == 2 else []),
                writes=[b_w1q[par]] + ([b_w1q0h[0], b_w1q0h[1]] if j == 2 else []), extra=fence1)
            dma("gpsimd", w2q4[:, par, :, :],
                w2_in[e][qtr * 1024:(qtr + 1) * 1024, :].rearrange("(c p) f -> p c f", p=128),
                reads=[b_w2q[par]] + ([b_w2q0h[0], b_w2q0h[1]] if j == 2 else []),
                writes=[b_w2q[par]] + ([b_w2q0h[0], b_w2q0h[1]] if j == 2 else []), extra=fence1)
        if qtr == 0:
            pe_ = e % 2
            dma("gpsimd", b2e[0:1, pe_ * D:(pe_ + 1) * D], b2row_in[0:1, e * D:(e + 1) * D],
                reads=[b_b2e[pe_]], writes=[b_b2e[pe_]], extra=fence1)

    if "ffn" not in ablate:
        w_load(0)
        w_load(1)

    for j in (range(NE * 4) if "ffn" not in ablate else ()):
        e, qtr = divmod(j, 4)
        par = j % 2
        bw1, bw2 = b_w1q[par], b_w2q[par]
        epar = e % 2
        for t in range(NTILE):
            if t < 4:
                rhs_sel = lambda ic, t=t: xnr4[:, ic, t * TILE:(t + 1) * TILE]
                b_rhs = b_xnr4[t]
            else:
                sp = (t - 4) % 2
                rhs_sel = lambda ic, sp=sp: xns4[:, sp, ic, :]
                b_rhs = b_xns[(t - 4) % 2]
            if t == 2:
                for i in range(2):
                    dma("scalar", xns4[:, i, :, :], xn_spill[4 + i].rearrange("(c p) f -> p c f", p=128),
                        reads=[b_xnsp, b_xns[i]], writes=[b_xns[i]], extra=fence1 if j == 0 else ())
            if t == 5:
                dma("scalar", xns4[:, 0, :, :], xn_spill[6].rearrange("(c p) f -> p c f", p=128),
                    reads=[b_xnsp, b_xns[0]], writes=[b_xns[0]])
            if t == 6:
                dma("scalar", xns4[:, 1, :, :], xn_spill[7].rearrange("(c p) f -> p c f", p=128),
                    reads=[b_xnsp, b_xns[1]], writes=[b_xns[1]])

            for hs in range(8):
                psh, bph = (ps_hA, b_hA) if hs % 2 == 0 else (ps_hB, b_hB)
                bw1u = b_w1q0h[hs // 4] if j == 0 else bw1
                for ic in range(8):
                    mm(psh[:, :], w1q4[:, par, ic, hs * 128:(hs + 1) * 128], rhs_sel(ic),
                       start=(ic == 0), stop=(ic == 7),
                       reads=[bw1u, b_rhs], writes=[bph])
                hsg = qtr * 8 + hs
                act(h_sb[:, hs * 512:(hs + 1) * 512], psh[:, :], AF.Gelu,
                    bias=b1t[:, hsg * NE + e: hsg * NE + e + 1],
                    reads=[bph, b_b1t], writes=[b_h])
            for sub in range(NSUB):
                g = t * NSUB + sub
                if sub % 2 == 0:
                    outs = ((ps_gA[:, 0:512], b_gA), (ps_gA[:, 512:1024], b_gA))
                else:
                    outs = ((ps_gB0[:, :], b_gB0), (ps_gB1[:, :], b_gB1))
                for oh in range(2):
                    po, bpo = outs[oh]
                    for hc in range(8):
                        bw2u = b_w2q0h[hc // 4] if j == 0 else bw2
                        mm(po, h_sb[:, hc * 512 + sub * 128: hc * 512 + sub * 128 + 128],
                           w2q4[:, par, hc, oh * 512:(oh + 1) * 512],
                           start=(hc == 0), stop=(hc == 7 and qtr != 0),
                           reads=[b_h, bw2u], writes=[bpo])
                    if qtr == 0:
                        mm(po, onesb[0:1, 0:128], b2e[0:1, epar * D + oh * 512: epar * D + (oh + 1) * 512],
                           start=False, stop=True, reads=[b_onesb, b_b2e[epar]], writes=[bpo])
                if sub % 2 == 0:
                    S.add("vector",
                          lambda h, g=g, e=e: h.scalar_tensor_tensor(
                              g_pre[:, g * D:(g + 1) * D], ps_gA[:, :], C_all[:, g * NE + e: g * NE + e + 1],
                              g_pre[:, g * D:(g + 1) * D], ALU.mult, ALU.add),
                          reads=[b_gA, b_C, b_gpre_ch[g]], writes=[b_gpre_ch[g]])
                else:
                    S.add("vector",
                          lambda h, g=g, e=e: h.scalar_tensor_tensor(
                              g_pre[:, g * D:g * D + 512], ps_gB0[:, :], C_all[:, g * NE + e: g * NE + e + 1],
                              g_pre[:, g * D:g * D + 512], ALU.mult, ALU.add),
                          reads=[b_gB0, b_C, b_gpre_ch[g]], writes=[b_gpre_ch[g]])
                    S.add("vector",
                          lambda h, g=g, e=e: h.scalar_tensor_tensor(
                              g_pre[:, g * D + 512:(g + 1) * D], ps_gB1[:, :], C_all[:, g * NE + e: g * NE + e + 1],
                              g_pre[:, g * D + 512:(g + 1) * D], ALU.mult, ALU.add),
                          reads=[b_gB1, b_C, b_gpre_ch[g]], writes=[b_gpre_ch[g]])
        if j + 2 <= NE * 4 - 1:
            w_load(j + 2)

    if "gpre" in dbg:
        dma("scalar", dbg["gpre"].ap().rearrange("g p f -> p g f"),
            g_pre.ap().rearrange("p (g f) -> p g f", g=NCH), reads=b_gpre_ch)
    if "C" in dbg:
        dma("scalar", dbg["C"][:, :], C_all[:, :], reads=[b_C])
    if "lg" in dbg:
        dma("scalar", dbg["lg"][:, :], lgbuf[:, :], reads=[b_lg])
    if "Sc" in dbg:
        dma("scalar", dbg["Sc"][:, :], S_all[:, :], reads=[b_S])

    fence2 = S.snapshot()

    a3 = Arena(ARENA_BASE)
    WoT, b_WoT = a3.take("WoT", [128, 8 * D], BF16)
    wstage2, b_wstage2 = a3.take("wstage2", [128, D], BF16)
    kvfull, b_kvfull = a3.take("kvfull", [128, 1040], F32)
    kvm, b_kvm = a3.take("kvm", [128, 520], F32)
    bor, b_bor = a3.take("bor", [1, D], BF16)
    kvaug, b_kvaug = a3.take("kvaug", [128, 8 * 66], BF16)
    q_sb, _ = a3.take("q_sb", [128, 2 * 8 * 512], BF16)
    b_qsb = [Buf("qsb0"), Buf("qsb1")]
    xh2, _ = a3.take("xh2", [128, 2 * 2 * D], F32)
    b_xh2 = [Buf("xh2_0"), Buf("xh2_1")]
    aztm, _ = a3.take("aztm", [128, 2 * 1024], BF16)
    b_aztm = [Buf("aztm0"), Buf("aztm1")]
    azf, _ = a3.take("azf", [128, 2 * 8 * 512], BF16)
    b_azf = [Buf("azf0"), Buf("azf1")]
    den_sb, b_den = a3.take("den_sb", [128, 16], F32)
    zs_sb, b_zs = a3.take("zs_sb", [128, 16], F32)
    out_sb, b_out = a3.take("out_sb", [128, 4 * D], F32)
    gs_sb, b_gs = a3.take("gs_sb", [128, D], F32)
    mres, b_mres = a3.take("mres", [128, 512], F32)

    ps_atA, b_atA = P512[0], bP512[0]
    ps_atB, b_atB = P512[1], bP512[1]
    b_tr2 = [bP1024, Buf("ps1024_b1")]

    dma("gpsimd", bor[:, :], bo_row_in[:, :], writes=[b_bor], extra=fence2)
    for oc in range(8):
        dma("gpsimd", wstage2[:, :], W_in["Wo"][oc * 128:(oc + 1) * 128, :],
            reads=[b_wstage2], writes=[b_wstage2], extra=fence2)
        for ic in range(8):
            psb, bps = (ps_atA, b_atA) if ic % 2 == 0 else (ps_atB, b_atB)
            pbf = psb.ap().bitcast(BF16)
            tr(pbf[:, 0:128], wstage2[:, ic * 128:(ic + 1) * 128], idb[:, :],
               reads=[b_wstage2, b_idb], writes=[bps], extra=fence2)
            S.add("scalar",
                  lambda h, c=ic * D + oc * 128, p=pbf: h.copy(WoT[:, c:c + 128], p[:, 0:128]),
                  reads=[bps], writes=[b_WoT])

    dma("gpsimd", kvfull.ap().rearrange("p (b f) -> p b f", b=2),
        cc_out.ap().rearrange("(b p) f -> p b f", p=128),
        reads=[b_ccout], writes=[b_kvfull], extra=[cc_op] + list(fence2))
    if "kvfull" in dbg:
        dma("scalar", dbg["kvfull"][:, :], kvfull[:, :], reads=[b_kvfull])
    S.add("vector", lambda h: h.tensor_scalar(kvm[:, :], kvfull[:, 0:520], bmb[:, 0:1], None, ALU.mult),
          reads=[b_kvfull, b_bmb], writes=[b_kvm])
    S.add("vector", lambda h: h.scalar_tensor_tensor(kvm[:, :], kvfull[:, 520:1040], bmb[:, 1:2], kvm[:, :], ALU.mult, ALU.add),
          reads=[b_kvfull, b_bmb, b_kvm], writes=[b_kvm])
    for h_ in range(NH):
        pb, fo, sl = (h_ % 2) * 64, (h_ // 2) * 64, h_ // 2
        vop("tensor_copy", kvaug[pb:pb + 64, sl * 66:sl * 66 + 64],
            kvm[pb:pb + 64, fo:fo + 64], reads=[b_kvm], writes=[b_kvaug])
        vop("tensor_copy", kvaug[pb:pb + 64, sl * 66 + 64:sl * 66 + 65],
            kvm[pb:pb + 64, 512 + sl:512 + sl + 1], reads=[b_kvm], writes=[b_kvaug])

    q4 = q_sb.ap().rearrange("p (a c f) -> p a c f", a=2, c=8)
    xh2v = xh2.ap().rearrange("p (a u f) -> p a u f", a=2, u=2)

    for t in (range(NTILE) if "attn" not in ablate else ()):
        qpar = t % 2
        azp = t % 2
        if t == 0:
            dma("sync", q4[:, 0, :, :], q_spill[0].rearrange("(c p) f -> p c f", p=128),
                reads=[b_qsp, b_qsb[0]], writes=[b_qsb[0]], extra=fence2)
            for hf in range(2):
                dma("sync", xh2v[:, hf, :, :],
                    x_in[hf * 256:(hf + 1) * 256, :].rearrange("(u p) f -> p u f", p=128),
                    reads=[b_xh2[hf]], writes=[b_xh2[hf]], extra=fence2)

        for sub in (range(NSUB) if "p3cb" not in ablate else ()):
            pA, bA = (P512[0], bP512[0]) if sub % 2 == 0 else (P512[2], bP512[2])
            pB, bB = (P512[1], bP512[1]) if sub % 2 == 0 else (P512[3], bP512[3])
            pD0, bD0 = P512[4], bP512[4]
            pD1, bD1 = P512[5], bP512[5]
            for h_ in range(NH):
                qrow = (h_ % 2) * 64
                sl = h_ // 2
                qT = q4[qrow:qrow + 64, qpar, sl, sub * 128:(sub + 1) * 128]
                po, bo_ = (pA, bA) if h_ % 2 == 0 else (pB, bB)
                mm(po[:, sl * 64:sl * 64 + 64], qT,
                   kvaug[qrow:qrow + 64, sl * 66:sl * 66 + 64],
                   start=True, stop=True, reads=[b_qsb[qpar], b_kvaug], writes=[bo_])
                pd, bd = (pD0, bD0) if h_ % 2 == 0 else (pD1, bD1)
                mm(pd[:, sl:sl + 1], qT, kvaug[qrow:qrow + 64, sl * 66 + 64:sl * 66 + 65],
                   start=True, stop=True, reads=[b_qsb[qpar], b_kvaug], writes=[bd])
            if "p3z" in ablate:
                continue
            S.add("vector", lambda h, pD0=pD0: h.tensor_scalar_add(den_sb[:, 0:8], pD0[:, 0:8], 1e-6),
                  reads=[bD0], writes=[b_den])
            S.add("vector", lambda h, pD1=pD1: h.tensor_scalar_add(den_sb[:, 8:16], pD1[:, 0:8], 1e-6),
                  reads=[bD1], writes=[b_den])
            S.add("vector", lambda h: h.reciprocal(zs_sb[:, :], den_sb[:, :]),
                  reads=[b_den], writes=[b_zs])
            for h_ in range(NH):
                sl = h_ // 2
                po, bo_ = (pA, bA) if h_ % 2 == 0 else (pB, bB)
                src = po[:, sl * 64:sl * 64 + 64]
                zcol = sl + 8 * (h_ % 2)
                dst = aztm[:, (sub % 2) * 1024 + h_ * 64:(sub % 2) * 1024 + (h_ + 1) * 64]
                if h_ % 2 == 0:
                    S.add("vector", lambda h, d=dst, sr=src, zc=zcol: h.tensor_scalar(d, sr, zs_sb[:, zc:zc + 1], None, ALU.mult),
                          reads=[bo_, b_zs], writes=[b_aztm[sub % 2]])
                else:
                    S.add("scalar", lambda h, d=dst, sr=src, zc=zcol: h.activation(d, sr, AF.Copy, scale=zs_sb[:, zc:zc + 1]),
                          reads=[bo_, b_zs], writes=[b_aztm[sub % 2]])
            for ic in (range(8) if "p3tr" not in ablate else ()):
                trbank = ic % 2
                pt = P1024.ap().bitcast(BF16)[:, trbank * 1024 + 0:trbank * 1024 + 128]
                tr(pt, aztm[:, (sub % 2) * 1024 + ic * 128:(sub % 2) * 1024 + (ic + 1) * 128], idb[:, :],
                   reads=[b_aztm[sub % 2], b_idb], writes=[b_tr2[trbank]])
                col = ic * TILE + sub * 128
                S.add("scalar", lambda h, c=col, p=pt, azp=azp: h.copy(
                      azf[:, azp * 4096 + c: azp * 4096 + c + 128], p),
                      reads=[b_tr2[trbank]], writes=[b_azf[azp]])

        if t + 1 < NTILE:
            dma("sync", q4[:, (t + 1) % 2, :, :], q_spill[t + 1].rearrange("(c p) f -> p c f", p=128),
                reads=[b_qsp, b_qsb[(t + 1) % 2]], writes=[b_qsb[(t + 1) % 2]])
        for sub in (range(NSUB) if "p3wo" not in ablate else ()):
            g = t * NSUB + sub
            par = (t * 2 + sub // 2) % 2
            S.add("scalar",
                  lambda h, g=g: h.activation(gs_sb[:, :], g_pre[:, g * D:(g + 1) * D], AF.Silu,
                                              scale=S_all[:, g:g + 1]),
                  reads=[b_gpre_ch[g], b_S], writes=[b_gs])
            for oh in range(2):
                pw, bpw = P512[sub], bP512[sub]
                for ic in range(8):
                    mm(pw[:, :], azf[:, azp * 4096 + ic * TILE + sub * 128: azp * 4096 + ic * TILE + sub * 128 + 128],
                       WoT[:, ic * D + oh * 512:(ic) * D + (oh + 1) * 512],
                       start=(ic == 0), stop=False, reads=[b_azf[azp], b_WoT], writes=[bpw])
                mm(pw[:, :], onesb[0:1, 0:128], bor[0:1, oh * 512:(oh + 1) * 512],
                   start=False, stop=True, reads=[b_onesb, b_bor], writes=[bpw])
                S.add("vector", lambda h, pw=pw, oh=oh: h.tensor_mul(
                      mres[:, :], pw[:, :], gs_sb[:, oh * 512:(oh + 1) * 512]),
                      reads=[bpw, b_gs], writes=[b_mres])
                S.add("vector",
                      lambda h, sub=sub, par=par, oh=oh: h.tensor_add(
                          out_sb[:, sub * D + oh * 512: sub * D + (oh + 1) * 512],
                          xh2v[:, par, sub % 2, oh * 512:(oh + 1) * 512], mres[:, :]),
                      reads=[b_xh2[par], b_mres], writes=[b_out])
        dma("sync", y_out[t * TILE:(t + 1) * TILE, :].rearrange("(s p) f -> p s f", p=128),
            out_sb.ap().rearrange("p (s f) -> p s f", s=NSUB),
            reads=[b_out], writes=[b_y])
        if t + 1 < NTILE:
            for hf in range(2):
                dma("sync", xh2v[:, hf, :, :],
                    x_in[(t + 1) * TILE + hf * 256: (t + 1) * TILE + (hf + 1) * 256, :].rearrange("(u p) f -> p u f", p=128),
                    reads=[b_xh2[hf]], writes=[b_xh2[hf]])

    S.add("gpsimd", lambda h: h.nop(), extra_waits=S.snapshot())

    guards = S.emit(nc)
    return nc


def kernel(**inputs):
    x = np.ascontiguousarray(np.asarray(inputs["L0_attn_hidden_states"], dtype=np.float32).reshape(2 * 16384, D))
    Wq = np.ascontiguousarray(np.asarray(inputs["L0_attn_Wq"], dtype=np.float32))
    Wk = np.ascontiguousarray(np.asarray(inputs["L0_attn_Wk"], dtype=np.float32))
    Wv = np.ascontiguousarray(np.asarray(inputs["L0_attn_Wv"], dtype=np.float32))
    Wo = np.ascontiguousarray(np.asarray(inputs["L0_attn_Wo"], dtype=np.float32))
    bq = np.asarray(inputs["L0_attn_bq"], dtype=np.float32)
    bk = np.asarray(inputs["L0_attn_bk"], dtype=np.float32)
    bv = np.asarray(inputs["L0_attn_bv"], dtype=np.float32)
    bo = np.asarray(inputs["L0_attn_bo"], dtype=np.float32)
    wg_red = np.ascontiguousarray(np.asarray(inputs["L0_ffn_wg_red"], dtype=np.float32))
    wg = np.ascontiguousarray(np.asarray(inputs["L0_ffn_wg"], dtype=np.float32))
    w1 = np.ascontiguousarray(np.asarray(inputs["L0_ffn_w1"], dtype=np.float32))
    b1 = np.asarray(inputs["L0_ffn_b1"], dtype=np.float32)
    w2 = np.ascontiguousarray(np.asarray(inputs["L0_ffn_w2"], dtype=np.float32))
    b2 = np.asarray(inputs["L0_ffn_b2"], dtype=np.float32)

    bq_t = np.ascontiguousarray(bq.reshape(8, 128).T)
    bo_t = np.ascontiguousarray(bo.reshape(8, 128).T)
    bk_row = np.ascontiguousarray(bk.reshape(1, D))
    bo_row = np.ascontiguousarray(bo.reshape(1, D))
    bv_row = np.ascontiguousarray(bv.reshape(1, D))
    b1_t = np.ascontiguousarray(b1.reshape(NE, 32, 128).transpose(2, 1, 0).reshape(128, 32 * NE))
    b2row = np.ascontiguousarray(b2.reshape(1, NE * D))
    ident = np.eye(128, dtype=np.float32)
    ones_row = np.ones((1, 512), dtype=np.float32)
    ones_col = np.ones((128, 1), dtype=np.float32)
    pw2 = (2.0 ** -np.arange(NE, dtype=np.float32)).reshape(1, NE)

    nc = build_program(debug=tuple(DEBUG), ablate=frozenset(ABLATE))

    in_maps = []
    for c in range(NCORE):
        bmask = np.zeros((1, 2), dtype=np.float32)
        bmask[0, c // 4] = 1.0
        in_maps.append({
            "x": x[c * TOK:(c + 1) * TOK],
            "Wq": Wq, "Wk": Wk, "Wv": Wv, "Wo": Wo,
            "bq_t": bq_t, "bo_t": bo_t, "bk_row": bk_row, "bv_row": bv_row, "bo_row": bo_row,
            "wg_red": wg_red, "wg": wg, "w1": w1, "b1_t": b1_t, "w2": w2, "b2row": b2row,
            "ident": ident, "ones_row": ones_row, "ones_col": ones_col,
            "pw2": pw2, "bmask": bmask,
        })

    res = run_bass_kernel_spmd(nc, in_maps, core_ids=list(range(NCORE)))
    out = np.concatenate([res.results[c]["y"] for c in range(NCORE)], axis=0)
    if DEBUG:
        kernel.debug_results = res.results
    return out.reshape(2, 16384, D)


if __name__ == "__main__":
    build_program()
    print("build ok")


# version 08
# speedup vs baseline: 2.3630x; 2.3630x over previous
import numpy as np
import ml_dtypes
import concourse.bass as bass
import concourse.mybir as mybir
from concourse.bass_utils import run_bass_kernel_spmd

F32 = mybir.dt.float32
BF16 = mybir.dt.bfloat16
F16 = mybir.dt.float16
AF = mybir.ActivationFunctionType
ALU = mybir.AluOpType
AX = mybir.AxisListType

NCORE = 8
TOK = 4096
D = 1024
NH = 16
NE = 8
HID = 4096
TILE = 512
NTILE = TOK // TILE
NSUB = TILE // 128
NCH = TOK // 128

DEBUG = []
ABLATE = set()


class Buf:
    __slots__ = ("name", "w", "readers")

    def __init__(self, name):
        self.name = name
        self.w = None
        self.readers = {}


class Op:
    __slots__ = ("eng", "fn", "waits", "dom", "inc", "flagged", "count_after", "i", "eng_seq", "raw")


RAW_K = 4


class Sched:
    def __init__(self):
        self.plan = []
        self.eng_n = {}

    def add(self, eng, fn, reads=(), writes=(), dma=False, cc=False, extra_waits=()):
        op = Op()
        op.eng = eng
        op.fn = fn
        op.dom = eng + "_dma" if dma else ("cc" if cc else eng)
        op.inc = 16 if dma else 1
        op.flagged = False
        op.count_after = None
        op.i = len(self.plan)
        op.eng_seq = self.eng_n.get(eng, 0)
        self.eng_n[eng] = op.eng_seq + 1
        ws = {}
        rw = {}

        def upd(d, w):
            c = d.get(w.dom)
            if c is None or w.i > c.i:
                d[w.dom] = w

        for b in reads:
            if b.w is not None:
                upd(ws, b.w)
                upd(rw, b.w)
        for b in writes:
            for r in b.readers.values():
                upd(ws, r)
            if b.w is not None:
                upd(ws, b.w)
        for w in extra_waits:
            if w is not None:
                upd(ws, w)
        waits = set(ws.values())
        raw = set(rw.values())
        waits.discard(op)
        raw.discard(op)
        for w in raw:
            waits.add(w)
        op.waits = sorted(waits, key=lambda o: o.i)
        op.raw = raw
        for b in reads:
            b.readers[op.dom] = op
        for b in writes:
            b.w = op
            b.readers = {}
        self.plan.append(op)
        return op

    def snapshot(self):
        last = {}
        for op in self.plan:
            last[op.dom] = op
        return list(last.values())

    def emit(self, nc):
        for op in self.plan:
            if op.dom.endswith("_dma") or op.dom == "cc":
                op.flagged = True
        def need_wait(op, w):
            if w.dom != op.eng:
                return True
            if op.eng == "tensor":
                return False
            return (op.eng_seq - w.eng_seq) <= RAW_K

        for op in self.plan:
            for w in op.waits:
                if need_wait(op, w):
                    w.flagged = True
        counts = {}
        for op in self.plan:
            if op.flagged:
                counts[op.dom] = counts.get(op.dom, 0) + op.inc
                op.count_after = counts[op.dom]
        doms = set(counts.keys())
        sems = {}
        guards = []
        for d in sorted(doms):
            g = nc.semaphore("sem_" + d)
            sems[d] = g.__enter__()
            guards.append(g)

        engs = ("tensor", "vector", "scalar", "gpsimd", "sync")
        per_eng = {e: [op for op in self.plan if op.eng == e] for e in engs}

        def mk_body(eng):
            ops = per_eng[eng]

            def body(h):
                hw = {}
                for op in ops:
                    for w in op.waits:
                        if not need_wait(op, w):
                            continue
                        assert w.flagged, (w.i, op.i)
                        tgt = w.count_after
                        if hw.get(w.dom, 0) >= tgt:
                            continue
                        h.wait_ge(sems[w.dom], tgt)
                        hw[w.dom] = tgt
                    inst = op.fn(h)
                    if op.flagged:
                        inst.then_inc(sems[op.dom], op.inc)
            return body

        with nc.Block() as block:
            for eng in engs:
                if per_eng[eng]:
                    getattr(block, eng)(mk_body(eng))
        return guards


DT_SIZE = {F32: 4, BF16: 2, F16: 2}


def build_program(debug=(), ablate=frozenset()):
    nc = bass.Bass("TRN2", target_bir_lowering=False)
    S = Sched()

    x_in = nc.dram_tensor("x", [TOK, D], F32, kind="ExternalInput")
    W_in = {w: nc.dram_tensor(w, [D, D], F32, kind="ExternalInput") for w in ("Wq", "Wk", "Wv", "Wo")}
    bq_t_in = nc.dram_tensor("bq_t", [128, 8], F32, kind="ExternalInput")
    bo_t_in = nc.dram_tensor("bo_t", [128, 8], F32, kind="ExternalInput")
    bk_row_in = nc.dram_tensor("bk_row", [1, D], F32, kind="ExternalInput")
    bo_row_in = nc.dram_tensor("bo_row", [1, D], F32, kind="ExternalInput")
    bv_row_in = nc.dram_tensor("bv_row", [1, D], F32, kind="ExternalInput")
    wgred_in = nc.dram_tensor("wg_red", [16, D], F32, kind="ExternalInput")
    wg_in = nc.dram_tensor("wg", [NE, 16], F32, kind="ExternalInput")
    w1_in = nc.dram_tensor("w1", [NE, D, HID], F32, kind="ExternalInput")
    b1_t_in = nc.dram_tensor("b1_t", [128, 32 * NE], F32, kind="ExternalInput")
    w2_in = nc.dram_tensor("w2", [NE, HID, D], F32, kind="ExternalInput")
    b2row_in = nc.dram_tensor("b2row", [1, NE * D], F32, kind="ExternalInput")
    ident_in = nc.dram_tensor("ident", [128, 128], F32, kind="ExternalInput")
    ones_in = nc.dram_tensor("ones_row", [1, 512], F32, kind="ExternalInput")
    onesc_in = nc.dram_tensor("ones_col", [128, 1], F32, kind="ExternalInput")
    pw2_in = nc.dram_tensor("pw2", [1, NE], F32, kind="ExternalInput")
    bmask_in = nc.dram_tensor("bmask", [1, 2], F32, kind="ExternalInput")

    y_out = nc.dram_tensor("y", [TOK, D], F32, kind="ExternalOutput")

    xn_spill = nc.dram_tensor("xn_spill", [NTILE, D, TILE], BF16)
    q_spill = nc.dram_tensor("q_spill", [NTILE, D, TILE], BF16)
    cc_in = nc.dram_tensor("cc_in", [2 * 128, 520], F32)
    cc_out = nc.dram_tensor("cc_out", [2 * 128, 520], F32)

    dbg = {}
    if "xn" in debug:
        dbg["xn"] = nc.dram_tensor("dbg_xn", [NTILE, D, TILE], BF16, kind="ExternalOutput")
    if "q" in debug:
        dbg["q"] = nc.dram_tensor("dbg_q", [NTILE, D, TILE], BF16, kind="ExternalOutput")
    if "kv" in debug:
        dbg["kv"] = nc.dram_tensor("dbg_kv", [128, 520], F32, kind="ExternalOutput")
    if "kvfull" in debug:
        dbg["kvfull"] = nc.dram_tensor("dbg_kvfull", [128, 1040], F32, kind="ExternalOutput")
    if "C" in debug:
        dbg["C"] = nc.dram_tensor("dbg_C", [128, NCH * NE], F32, kind="ExternalOutput")
    if "Sc" in debug:
        dbg["Sc"] = nc.dram_tensor("dbg_Sc", [128, NCH], F32, kind="ExternalOutput")
    if "gpre" in debug:
        dbg["gpre"] = nc.dram_tensor("dbg_gpre", [NCH, 128, D], F16, kind="ExternalOutput")
    if "xr" in debug:
        dbg["xr"] = nc.dram_tensor("dbg_xr", [NTILE, 16, TILE], F32, kind="ExternalOutput")
    if "lg" in debug:
        dbg["lg"] = nc.dram_tensor("dbg_lg", [128, NCH * NE], F32, kind="ExternalOutput")
    if "lnraw" in debug:
        dbg["lnraw"] = nc.dram_tensor("dbg_lnraw", [128, D], F32, kind="ExternalOutput")
        dbg["lnst"] = nc.dram_tensor("dbg_lnst", [128, 8], F32, kind="ExternalOutput")
    if "ktm" in debug:
        dbg["ktm"] = nc.dram_tensor("dbg_ktm", [128, D], BF16, kind="ExternalOutput")
        dbg["vtm"] = nc.dram_tensor("dbg_vtm", [128, D], BF16, kind="ExternalOutput")

    SB_BASE = (nc.sbuf_base + 63) & ~63
    SB_TOP = nc.sbuf_top

    class Arena:
        def __init__(self, base):
            self.off = base
            self.hi = base

        def take(self, name, shape, dtype):
            nbytes = int(np.prod(shape[1:])) * DT_SIZE[dtype]
            off = (self.off + 63) & ~63
            t = nc.alloc_sbuf_tensor_at(name, list(shape), dtype, offset=off)
            self.off = off + nbytes
            self.hi = max(self.hi, self.off)
            assert self.off <= SB_TOP, (name, self.off, SB_TOP)
            return t, Buf(name)

    pers = Arena(SB_BASE)
    g_pre, b_gpre = pers.take("g_pre", [128, NCH * D], F16)
    b_gpre_ch = [Buf(f"g_pre{g}") for g in range(NCH)]
    C_all, b_C = pers.take("C_all", [128, NCH * NE], F32)
    S_all, b_S = pers.take("S_all", [128, NCH], F32)
    b1t, b_b1t = pers.take("b1t", [128, 32 * NE], F32)
    bqt, b_bqt = pers.take("bqt", [128, 8], F32)
    bot, b_bot = pers.take("bot", [128, 8], F32)
    idf, b_idf = pers.take("idf", [128, 128], F32)
    idb, b_idb = pers.take("idb", [128, 128], BF16)
    onesf, b_onesf = pers.take("onesf", [1, 512], F32)
    onesb, b_onesb = pers.take("onesb", [1, 512], BF16)
    onescb, b_onescb = pers.take("onescb", [128, 1], BF16)
    pw2b, b_pw2 = pers.take("pw2b", [128, NE], F32)
    bmb, b_bmb = pers.take("bmb", [128, 2], F32)
    ARENA_BASE = (pers.off + 63) & ~63

    b_ccin, b_ccout = Buf("cc_in"), Buf("cc_out")
    b_xnsp, b_qsp = Buf("xn_spill"), Buf("q_spill")
    b_y = Buf("y")

    P512 = []
    bP512 = []
    for i in range(6):
        t = nc.alloc_psum_tensor(f"ps{i}", [128, 512], F32)
        P512.append(t)
        bP512.append(Buf(f"ps{i}"))
    P1024 = nc.alloc_psum_tensor("ps1024", [128, 1024], F32)
    bP1024 = Buf("ps1024")

    def dma(eng, dst, src, reads=(), writes=(), extra=()):
        return S.add(eng, lambda h, d=dst, s=src: h.dma_start(out=d, in_=s),
                     reads=reads, writes=writes, dma=True, extra_waits=extra)

    dma("gpsimd", idb[:, :], ident_in[:, :], writes=[b_idb])
    dma("gpsimd", onesb[:, :], ones_in[:, :], writes=[b_onesb])
    dma("gpsimd", onescb[:, :], onesc_in[:, :], writes=[b_onescb])
    dma("sync", b1t[:, :], b1_t_in[:, :], writes=[b_b1t])
    dma("sync", bqt[:, :], bq_t_in[:, :], writes=[b_bqt])
    dma("sync", bot[:, :], bo_t_in[:, :], writes=[b_bot])
    dma("sync", idf[:, :], ident_in[:, :], writes=[b_idf])
    dma("sync", onesf[:, :], ones_in[:, :], writes=[b_onesf])
    dma("sync", pw2b[:, :], pw2_in.ap().broadcast_to([128, NE]), writes=[b_pw2])
    dma("sync", bmb[:, :], bmask_in.ap().broadcast_to([128, 2]), writes=[b_bmb])

    def mm(out, lhsT, rhs, start, stop, reads, writes, skip_gc=True, extra=()):
        return S.add("tensor",
                     lambda h, o=out, l=lhsT, r=rhs, st=start, sp=stop, sg=skip_gc:
                     h.matmul(o, l, r, start=st, stop=sp, skip_group_check=sg),
                     reads=reads, writes=writes, extra_waits=extra)

    def tr(out, in_, ident, reads, writes, extra=()):
        return S.add("tensor", lambda h, o=out, i=in_, d=ident: h.transpose(o, i, d),
                     reads=reads, writes=writes, extra_waits=extra)

    def act(out, in_, func, bias=0.0, scale=1.0, reads=(), writes=(), extra=()):
        return S.add("scalar",
                     lambda h, o=out, i=in_, f=func, b=bias, sc=scale: h.activation(o, i, f, bias=b, scale=sc),
                     reads=reads, writes=writes, extra_waits=extra)

    def vop(name, *aps, reads=(), writes=(), extra=(), eng="vector"):
        def fn(h, n=name, a=aps):
            return getattr(h, n)(*a)
        return S.add(eng, fn, reads=reads, writes=writes, extra_waits=extra)

    a1 = Arena(ARENA_BASE)
    WqT, b_WqT = a1.take("WqT", [128, 8 * D], BF16)
    WkT, b_WkT = a1.take("WkT", [128, 8 * D], BF16)
    WvT, b_WvT = a1.take("WvT", [128, 8 * D], BF16)
    wstage, b_wstage = a1.take("wstage", [128, D], BF16)
    wgredT, b_wgredT = a1.take("wgredT", [128, 8 * 16], F32)
    wgr_st, b_wgr_st = a1.take("wgr_st", [16, D], F32)
    wgu, b_wgu = a1.take("wgu", [NE, 16], F32)
    wguT, b_wguT = a1.take("wguT", [16, NE], F32)
    wgtmp, b_wgtmp = a1.take("wgtmp", [NE, 20], F32)
    bkr, b_bkr = a1.take("bkr", [1, D], BF16)
    bvr, b_bvr = a1.take("bvr", [1, D], BF16)
    xh, _ = a1.take("xh", [128, 2 * D], F32)
    b_xh = [Buf("xh0"), Buf("xh1")]
    ctr2, _ = a1.take("ctr2", [128, 2 * D], F32)
    b_ctr2 = [Buf("ctr0"), Buf("ctr1")]
    scr1, b_scr1 = a1.take("scr1", [128, D], F32)
    scr22, _ = a1.take("scr22", [128, 2 * D], F32)
    b_scr22 = [Buf("xnl0"), Buf("xnl1")]
    scr3, b_scr3 = a1.take("scr3", [128, D], F32)
    lnst2, _ = a1.take("lnst2", [128, 16], F32)
    b_lnst2 = [Buf("lnst0"), Buf("lnst1")]
    xnf_bf, b_xnbf = a1.take("xnf_bf", [128, 8 * TILE], BF16)
    xnf_f32, b_xnf32 = a1.take("xnf_f32", [128, 8 * 256], F32)
    q_tile, b_q = a1.take("q_tile", [128, 8 * TILE], BF16)
    k_tm, b_ktm = a1.take("k_tm", [128, D], BF16)
    v_tm, b_vtm = a1.take("v_tm", [128, D], BF16)
    xr_sb, b_xr = a1.take("xr_sb", [16, TILE], F32)
    gsc, b_gsc = a1.take("gsc", [128, 8 * 8], F32)
    qsc, b_qsc = a1.take("qsc", [128, D], F32)
    if "lg" in debug:
        lgbuf, b_lg = a1.take("lgbuf", [128, NCH * NE], F32)
    kvacc, b_kvacc = a1.take("kvacc", [128, 520], F32)
    contrib, b_contrib = a1.take("contrib", [128, 2 * 520], F32)

    ps_kvh, b_pkvh = P512[0], bP512[0]
    ps_ksum, b_pksum = P512[1], bP512[1]
    ps_trA, b_trA = P512[2], bP512[2]
    ps_trB, b_trB = P512[3], bP512[3]
    ps_mmA, b_mmA = P512[4], bP512[4]
    ps_mmB, b_mmB = P512[5], bP512[5]
    ps_kv2, b_pkv2 = P1024, bP1024

    dma("gpsimd", bkr[:, :], bk_row_in[:, :], writes=[b_bkr])
    dma("gpsimd", bvr[:, :], bv_row_in[:, :], writes=[b_bvr])

    def prep_wT(nm, WT, b_WT):
        for oc in range(8):
            dma("gpsimd", wstage[:, :], W_in[nm][oc * 128:(oc + 1) * 128, :],
                reads=(), writes=[b_wstage])
            for ic in range(8):
                psb, bps = (ps_trA, b_trA) if ic % 2 == 0 else (ps_trB, b_trB)
                pbf = psb.ap().bitcast(BF16)
                tr(pbf[:, 0:128], wstage[:, ic * 128:(ic + 1) * 128], idb[:, :],
                   reads=[b_wstage, b_idb], writes=[bps])
                S.add("scalar",
                      lambda h, w=WT, c=ic * D + oc * 128, p=pbf: h.copy(w[:, c:c + 128], p[:, 0:128]),
                      reads=[bps], writes=[b_WT])

    prep_wT("Wq", WqT, b_WqT)
    prep_wT("Wk", WkT, b_WkT)
    prep_wT("Wv", WvT, b_WvT)

    dma("sync", wgr_st[:, :], wgred_in[:, :], writes=[b_wgr_st])
    for ic in range(8):
        psb, bps = (ps_trA, b_trA) if ic % 2 == 0 else (ps_trB, b_trB)
        tr(psb[0:128, 0:16], wgr_st[0:16, ic * 128:(ic + 1) * 128], idf[0:16, 0:16],
           reads=[b_wgr_st, b_idf], writes=[bps])
        vop("tensor_copy", wgredT[:, ic * 16:(ic + 1) * 16], psb[0:128, 0:16],
            reads=[bps], writes=[b_wgredT])

    dma("sync", wgu[:, :], wg_in[:, :], writes=[b_wgu])
    act(wgtmp[:, 0:16], wgu[:, :], AF.Square, reads=[b_wgu], writes=[b_wgtmp])
    vop("reduce_sum", wgtmp[:, 16:17], wgtmp[:, 0:16], AX.X, reads=[b_wgtmp], writes=[b_wgtmp])
    act(wgtmp[:, 17:18], wgtmp[:, 16:17], AF.Sqrt, reads=[b_wgtmp], writes=[b_wgtmp])
    vop("tensor_scalar_max", wgtmp[:, 17:18], wgtmp[:, 17:18], 1e-4, reads=[b_wgtmp], writes=[b_wgtmp])
    vop("reciprocal", wgtmp[:, 18:19], wgtmp[:, 17:18], reads=[b_wgtmp], writes=[b_wgtmp])
    S.add("vector", lambda h: h.tensor_scalar(wgu[:, :], wgu[:, :], wgtmp[:, 18:19], None, ALU.mult),
          reads=[b_wgtmp], writes=[b_wgu])
    tr(ps_trA[0:16, 0:NE], wgu[:, :], idf[0:NE, 0:NE], reads=[b_wgu, b_idf], writes=[b_trA])
    vop("tensor_copy", wguT[:, :], ps_trA[0:16, 0:NE], reads=[b_trA], writes=[b_wguT])

    for t in range(NTILE):
        if t == 0:
            for k0 in range(2):
                dma("sync", xh[:, (k0 % 2) * D:(k0 % 2 + 1) * D],
                    x_in[k0 * 128:(k0 + 1) * 128, :],
                    reads=[b_xh[k0 % 2]], writes=[b_xh[k0 % 2]])
        for s in range(NSUB):
            kg = t * NSUB + s
            par = kg % 2
            pp = s % 2
            xsub = xh[:, par * D:(par + 1) * D]
            bx = b_xh[par]
            ctr = ctr2[:, pp * D:(pp + 1) * D]
            b_ctr = b_ctr2[pp]
            scr2 = scr22[:, pp * D:(pp + 1) * D]
            b_scr2 = b_scr22[pp]
            lnst = lnst2[:, pp * 8:(pp + 1) * 8]
            b_lnst = b_lnst2[pp]
            vop("reduce_sum", lnst[:, 0:1], xsub, AX.X, reads=[bx], writes=[b_lnst])
            vop("tensor_scalar_mul", lnst[:, 1:2], lnst[:, 0:1], 1.0 / D, reads=[b_lnst], writes=[b_lnst])
            S.add("vector", lambda h, xs=xsub, ctr=ctr, lnst=lnst: h.tensor_scalar(ctr[:, :], xs, lnst[:, 1:2], None, ALU.subtract),
                  reads=[bx, b_lnst], writes=[b_ctr])
            act(scr2[:, :], ctr[:, :], AF.Square, reads=[b_ctr], writes=[b_scr2])
            vop("reduce_sum", lnst[:, 2:3], scr2[:, :], AX.X, reads=[b_scr2], writes=[b_lnst])
            act(lnst[:, 3:4], lnst[:, 2:3], AF.Ln, bias=np.float32(1e-5), scale=1.0 / D, reads=[b_lnst], writes=[b_lnst])
            act(lnst[:, 4:5], lnst[:, 3:4], AF.Exp, scale=-0.5, reads=[b_lnst], writes=[b_lnst])
            S.add("vector", lambda h, ctr=ctr, lnst=lnst, scr2=scr2: h.tensor_scalar(scr2[:, :], ctr[:, :], lnst[:, 4:5], None, ALU.mult),
                  reads=[b_ctr, b_lnst], writes=[b_scr2])
            for ic in range(8):
                psb, bps = (ps_trA, b_trA) if ic % 2 == 0 else (ps_trB, b_trB)
                slot = (ic // 2) % 2
                tr(psb[:, slot * 128:slot * 128 + 128], scr2[:, ic * 128:(ic + 1) * 128], idf[:, :],
                   reads=[b_scr2, b_idf], writes=[bps])
                col = ic * TILE + s * 128
                colf = ic * 256 + pp * 128
                vop("tensor_copy", xnf_f32[:, colf:colf + 128], psb[:, slot * 128:slot * 128 + 128],
                    reads=[bps], writes=[b_xnf32])
                S.add("scalar", lambda h, c=col, cf=colf: h.copy(xnf_bf[:, c:c + 128], xnf_f32[:, cf:cf + 128]),
                      reads=[b_xnf32], writes=[b_xnbf])
            if s % 2 == 1 and "gating" not in ablate:
                hf2 = s // 2
                for ic in range(8):
                    mm(ps_mmA[0:16, hf2 * 256:hf2 * 256 + 256], wgredT[:, ic * 16:(ic + 1) * 16],
                       xnf_f32[:, ic * 256:(ic + 1) * 256],
                       start=(ic == 0), stop=(ic == 7), reads=[b_wgredT, b_xnf32], writes=[b_mmA])
                vop("tensor_copy", xr_sb[:, hf2 * 256:hf2 * 256 + 256], ps_mmA[0:16, hf2 * 256:hf2 * 256 + 256],
                    reads=[b_mmA], writes=[b_xr])
            if kg + 2 < NTILE * NSUB:
                k2 = kg + 2
                dma("sync", xh[:, (k2 % 2) * D:(k2 % 2 + 1) * D],
                    x_in[k2 * 128:(k2 + 1) * 128, :],
                    reads=[b_xh[k2 % 2]], writes=[b_xh[k2 % 2]])

        dma("scalar", xn_spill[t].rearrange("(c p) f -> p c f", p=128),
            xnf_bf.ap().rearrange("p (c f) -> p c f", c=8),
            reads=[b_xnbf], writes=[b_xnsp])
        if "xn" in dbg:
            dma("scalar", dbg["xn"][t].rearrange("(c p) f -> p c f", p=128),
                xnf_bf.ap().rearrange("p (c f) -> p c f", c=8), reads=[b_xnbf])

        for oc in (range(8) if "q" not in ablate else ()):
            psq, bpsq = (ps_mmA, b_mmA) if oc % 2 == 0 else (ps_mmB, b_mmB)
            for ic in range(8):
                mm(psq[:, :], WqT[:, ic * D + oc * 128: ic * D + oc * 128 + 128],
                   xnf_bf[:, ic * TILE:(ic + 1) * TILE],
                   start=(ic == 0), stop=(ic == 7),
                   reads=[b_WqT, b_xnbf], writes=[bpsq])
            act(scr1[:, 0:TILE], psq[:, :], AF.Relu, bias=bqt[:, oc:oc + 1],
                reads=[bpsq, b_bqt], writes=[b_scr1])
            S.add("vector",
                  lambda h, p=psq, oc=oc: h.scalar_tensor_tensor(
                      qsc[:, 0:TILE], p[:, :], bqt[:, oc:oc + 1], scr1[:, 0:TILE], ALU.add, ALU.subtract),
                  reads=[bpsq, b_bqt, b_scr1], writes=[b_qsc])
            act(scr3[:, 0:TILE], qsc[:, 0:TILE], AF.Exp, reads=[b_qsc], writes=[b_scr3])
            vop("tensor_add", q_tile[:, oc * TILE:(oc + 1) * TILE], scr3[:, 0:TILE], scr1[:, 0:TILE],
                reads=[b_scr3, b_scr1], writes=[b_q])
        dma("scalar", q_spill[t].rearrange("(c p) f -> p c f", p=128),
            q_tile.ap().rearrange("p (c f) -> p c f", c=8),
            reads=[b_q], writes=[b_qsp])
        if "q" in dbg:
            dma("scalar", dbg["q"][t].rearrange("(c p) f -> p c f", p=128),
                q_tile.ap().rearrange("p (c f) -> p c f", c=8), reads=[b_q])

        for s in (range(NSUB) if "kv" not in ablate else ()):
            for half in range(2):
                for ic in range(8):
                    mm(ps_kv2[:, half * 512:(half + 1) * 512],
                       xnf_bf[:, ic * TILE + s * 128: ic * TILE + s * 128 + 128],
                       WkT[:, ic * D + half * 512: ic * D + (half + 1) * 512],
                       start=(ic == 0), stop=False,
                       reads=[b_xnbf, b_WkT], writes=[b_pkv2])
                mm(ps_kv2[:, half * 512:(half + 1) * 512],
                   onesb[0:1, 0:128], bkr[0:1, half * 512:(half + 1) * 512],
                   start=False, stop=True,
                   reads=[b_onesb, b_bkr], writes=[b_pkv2])
            act(scr1[:, :], ps_kv2[:, :], AF.Relu, reads=[b_pkv2], writes=[b_scr1])
            vop("tensor_sub", qsc[:, :], ps_kv2[:, :], scr1[:, :], reads=[b_pkv2, b_scr1], writes=[b_qsc])
            act(scr3[:, :], qsc[:, :], AF.Exp, reads=[b_qsc], writes=[b_scr3])
            vop("tensor_add", k_tm[:, :], scr3[:, :], scr1[:, :], reads=[b_scr3, b_scr1], writes=[b_ktm])
            for half in range(2):
                psv, bpsv = (ps_mmA, b_mmA) if half == 0 else (ps_mmB, b_mmB)
                for ic in range(8):
                    mm(psv[:, :],
                       xnf_bf[:, ic * TILE + s * 128: ic * TILE + s * 128 + 128],
                       WvT[:, ic * D + half * 512: ic * D + (half + 1) * 512],
                       start=(ic == 0), stop=False,
                       reads=[b_xnbf, b_WvT], writes=[bpsv])
                mm(psv[:, :], onesb[0:1, 0:128], bvr[0:1, half * 512:(half + 1) * 512],
                   start=False, stop=True, reads=[b_onesb, b_bvr], writes=[bpsv])
                vop("tensor_copy", v_tm[:, half * 512:(half + 1) * 512], psv[:, :],
                    reads=[bpsv], writes=[b_vtm])
            if "ktm" in dbg and t == 0 and s == 0:
                dma("scalar", dbg["ktm"][:, :], k_tm[:, :], reads=[b_ktm])
                dma("scalar", dbg["vtm"][:, :], v_tm[:, :], reads=[b_vtm])
            for h in range(NH):
                pb, fo = (h % 2) * 64, (h // 2) * 64
                mm(ps_kvh[pb:pb + 64, fo:fo + 64],
                   k_tm[:, h * 64:(h + 1) * 64], v_tm[:, h * 64:(h + 1) * 64],
                   start=(s == 0 and h < 2),
                   stop=(s == NSUB - 1),
                   reads=[b_ktm, b_vtm], writes=[b_pkvh], skip_gc=True)
                mm(ps_ksum[pb:pb + 64, (h // 2):(h // 2) + 1],
                   k_tm[:, h * 64:(h + 1) * 64], onescb[:, 0:1],
                   start=(s == 0 and h < 2), stop=(s == NSUB - 1),
                   reads=[b_ktm, b_onescb], writes=[b_pksum], skip_gc=True)

        if "kv" in ablate:
            pass
        elif t == 0:
            vop("tensor_copy", kvacc[:, 0:512], ps_kvh[:, :], reads=[b_pkvh], writes=[b_kvacc])
            vop("tensor_copy", kvacc[:, 512:520], ps_ksum[:, 0:8], reads=[b_pksum], writes=[b_kvacc])
        else:
            vop("tensor_add", kvacc[:, 0:512], kvacc[:, 0:512], ps_kvh[:, :], reads=[b_pkvh, b_kvacc], writes=[b_kvacc])
            vop("tensor_add", kvacc[:, 512:520], kvacc[:, 512:520], ps_ksum[:, 0:8], reads=[b_pksum, b_kvacc], writes=[b_kvacc])

        if "xr" in dbg:
            dma("scalar", dbg["xr"][t], xr_sb[:, :], reads=[b_xr])
        for s in (range(NSUB) if "gating" not in ablate else ()):
            g = t * NSUB + s
            mm(ps_trB[0:128, 0:NE], xr_sb[:, s * 128:(s + 1) * 128], wguT[:, :],
               start=True, stop=True, reads=[b_xr, b_wguT], writes=[b_trB])
            if "lg" in dbg:
                vop("tensor_copy", lgbuf[:, g * NE:(g + 1) * NE], ps_trB[0:128, 0:NE], reads=[b_trB], writes=[b_lg])
            vop("reduce_max", gsc[:, 0:1], ps_trB[0:128, 0:NE], AX.X, reads=[b_trB], writes=[b_gsc])
            S.add("vector", lambda h: h.tensor_scalar(gsc[:, 8:16], ps_trB[0:128, 0:NE], gsc[:, 0:1], None, ALU.subtract),
                  reads=[b_trB, b_gsc], writes=[b_gsc])
            act(gsc[:, 16:24], gsc[:, 8:16], AF.Exp, reads=[b_gsc], writes=[b_gsc])
            vop("reduce_sum", gsc[:, 1:2], gsc[:, 16:24], AX.X, reads=[b_gsc], writes=[b_gsc])
            S.add("vector", lambda h, g=g: h.reciprocal(S_all[:, g:g + 1], gsc[:, 1:2]),
                  reads=[b_gsc], writes=[b_S])
            S.add("vector", lambda h: h.tensor_scalar(gsc[:, 24:32], gsc[:, 8:16], 0.0, None, ALU.is_equal),
                  reads=[b_gsc], writes=[b_gsc])
            vop("tensor_mul", gsc[:, 32:40], gsc[:, 24:32], pw2b[:, :], reads=[b_gsc, b_pw2], writes=[b_gsc])
            vop("reduce_max", gsc[:, 2:3], gsc[:, 32:40], AX.X, reads=[b_gsc], writes=[b_gsc])
            S.add("vector", lambda h: h.tensor_scalar(gsc[:, 40:48], gsc[:, 32:40], gsc[:, 2:3], None, ALU.is_equal),
                  reads=[b_gsc], writes=[b_gsc])
            S.add("vector", lambda h, g=g: h.tensor_copy(C_all[:, g * NE:(g + 1) * NE], gsc[:, 40:48]),
                  reads=[b_gsc], writes=[b_C])

    if True:
        S.add("vector", lambda h: h.tensor_scalar(contrib[:, 0:520], kvacc[:, :], bmb[:, 0:1], None, ALU.mult),
          reads=[b_kvacc, b_bmb], writes=[b_contrib])
    S.add("vector", lambda h: h.tensor_scalar(contrib[:, 520:1040], kvacc[:, :], bmb[:, 1:2], None, ALU.mult),
          reads=[b_kvacc, b_bmb], writes=[b_contrib])
    if "kv" in dbg:
        dma("scalar", dbg["kv"][:, :], kvacc[:, :], reads=[b_kvacc])
    dma("gpsimd", cc_in.ap().rearrange("(b p) f -> p b f", p=128),
        contrib.ap().rearrange("p (b f) -> p b f", b=2),
        reads=[b_contrib], writes=[b_ccin])
    cc_op = S.add("gpsimd",
                  lambda h: h.collective_compute("AllReduce", ALU.add,
                                                 replica_groups=[list(range(NCORE))],
                                                 ins=[cc_in.ap().opt()], outs=[cc_out.ap().opt()]),
                  reads=[b_ccin], writes=[b_ccout], cc=True)

    fence1 = S.snapshot()

    a2 = Arena(ARENA_BASE)
    xnr, b_xnr = a2.take("xnr", [128, 8 * 2048], BF16)
    b_xnr4 = [Buf(f"xnr{i}") for i in range(4)]
    xns, _ = a2.take("xns", [128, 2 * 8 * 512], BF16)
    b_xns = [Buf("xns0"), Buf("xns1")]
    w1q, _ = a2.take("w1q", [128, 2 * 8 * 1024], BF16)
    b_w1q = [Buf("w1q0"), Buf("w1q1")]
    w2q, _ = a2.take("w2q", [128, 2 * 8 * 1024], BF16)
    b_w2q = [Buf("w2q0"), Buf("w2q1")]
    b2e, _ = a2.take("b2e", [1, 2 * D], BF16)
    b_b2e = [Buf("b2e0"), Buf("b2e1")]
    h_sb, b_h = a2.take("h_sb", [128, 8 * 512], BF16)

    ps_hA, b_hA = P512[0], bP512[0]
    ps_hB, b_hB = P512[1], bP512[1]
    ps_gA, b_gA = P1024, bP1024
    ps_gB0, b_gB0 = P512[2], bP512[2]
    ps_gB1, b_gB1 = P512[3], bP512[3]

    xnr4 = xnr.ap().rearrange("p (c tt) -> p c tt", c=8)
    xns4 = xns.ap().rearrange("p (a c f) -> p a c f", a=2, c=8)
    w1q4 = w1q.ap().rearrange("p (a c f) -> p a c f", a=2, c=8)
    w2q4 = w2q.ap().rearrange("p (a c f) -> p a c f", a=2, c=8)

    S.add("vector", lambda h: h.memset(g_pre[:, :], 0.0), reads=[b_gpre],
          writes=[b_gpre] + b_gpre_ch, extra_waits=())

    for t in range(4):
        dma("sync" if t == 0 else "scalar", xnr4[:, :, t * TILE:(t + 1) * TILE],
            xn_spill[t].rearrange("(c p) f -> p c f", p=128),
            reads=[b_xnsp], writes=[b_xnr, b_xnr4[t]], extra=fence1)

    b_w1q0h = [Buf("w1q0a"), Buf("w1q0b")]
    b_w2q0h = [Buf("w2q0a"), Buf("w2q0b")]

    def w_load(j):
        e, qtr = divmod(j, 4)
        par = j % 2
        if j == 0:
            for hh in range(2):
                dma("gpsimd", w1q4[:, 0, :, hh * 512:(hh + 1) * 512],
                    w1_in[e][:, hh * 512:(hh + 1) * 512].rearrange("(c p) f -> p c f", p=128),
                    reads=[b_w1q[0]], writes=[b_w1q0h[hh]], extra=fence1)
            for hh in range(2):
                dma("gpsimd", w2q4[:, 0, hh * 4:(hh + 1) * 4, :],
                    w2_in[e][hh * 512:(hh + 1) * 512, :].rearrange("(c p) f -> p c f", p=128),
                    reads=[b_w2q[0]], writes=[b_w2q0h[hh]], extra=fence1)
        else:
            extra_w = [b_w1q0h[0].w, b_w1q0h[1].w] if j == 2 else []
            extra_w2 = [b_w2q0h[0].w, b_w2q0h[1].w] if j == 2 else []
            dma("gpsimd", w1q4[:, par, :, :],
                w1_in[e][:, qtr * 1024:(qtr + 1) * 1024].rearrange("(c p) f -> p c f", p=128),
                reads=[b_w1q[par]] + ([b_w1q0h[0], b_w1q0h[1]] if j == 2 else []),
                writes=[b_w1q[par]] + ([b_w1q0h[0], b_w1q0h[1]] if j == 2 else []), extra=fence1)
            dma("gpsimd", w2q4[:, par, :, :],
                w2_in[e][qtr * 1024:(qtr + 1) * 1024, :].rearrange("(c p) f -> p c f", p=128),
                reads=[b_w2q[par]] + ([b_w2q0h[0], b_w2q0h[1]] if j == 2 else []),
                writes=[b_w2q[par]] + ([b_w2q0h[0], b_w2q0h[1]] if j == 2 else []), extra=fence1)
        if qtr == 0:
            pe_ = e % 2
            dma("gpsimd", b2e[0:1, pe_ * D:(pe_ + 1) * D], b2row_in[0:1, e * D:(e + 1) * D],
                reads=[b_b2e[pe_]], writes=[b_b2e[pe_]], extra=fence1)

    if "ffn" not in ablate:
        w_load(0)
        w_load(1)

    for j in (range(NE * 4) if "ffn" not in ablate else ()):
        e, qtr = divmod(j, 4)
        par = j % 2
        bw1, bw2 = b_w1q[par], b_w2q[par]
        epar = e % 2
        for t in range(NTILE):
            if t < 4:
                rhs_sel = lambda ic, t=t: xnr4[:, ic, t * TILE:(t + 1) * TILE]
                b_rhs = b_xnr4[t]
            else:
                sp = (t - 4) % 2
                rhs_sel = lambda ic, sp=sp: xns4[:, sp, ic, :]
                b_rhs = b_xns[(t - 4) % 2]
            if t == 2:
                for i in range(2):
                    dma("scalar", xns4[:, i, :, :], xn_spill[4 + i].rearrange("(c p) f -> p c f", p=128),
                        reads=[b_xnsp, b_xns[i]], writes=[b_xns[i]], extra=fence1 if j == 0 else ())
            if t == 5:
                dma("scalar", xns4[:, 0, :, :], xn_spill[6].rearrange("(c p) f -> p c f", p=128),
                    reads=[b_xnsp, b_xns[0]], writes=[b_xns[0]])
            if t == 6:
                dma("scalar", xns4[:, 1, :, :], xn_spill[7].rearrange("(c p) f -> p c f", p=128),
                    reads=[b_xnsp, b_xns[1]], writes=[b_xns[1]])

            for hs in range(8):
                psh, bph = (ps_hA, b_hA) if hs % 2 == 0 else (ps_hB, b_hB)
                bw1u = b_w1q0h[hs // 4] if j == 0 else bw1
                for ic in range(8):
                    mm(psh[:, :], w1q4[:, par, ic, hs * 128:(hs + 1) * 128], rhs_sel(ic),
                       start=(ic == 0), stop=(ic == 7),
                       reads=[bw1u, b_rhs], writes=[bph])
                hsg = qtr * 8 + hs
                act(h_sb[:, hs * 512:(hs + 1) * 512], psh[:, :], AF.Gelu,
                    bias=b1t[:, hsg * NE + e: hsg * NE + e + 1],
                    reads=[bph, b_b1t], writes=[b_h])
            for sub in range(NSUB):
                g = t * NSUB + sub
                if sub % 2 == 0:
                    outs = ((ps_gA[:, 0:512], b_gA), (ps_gA[:, 512:1024], b_gA))
                else:
                    outs = ((ps_gB0[:, :], b_gB0), (ps_gB1[:, :], b_gB1))
                for oh in range(2):
                    po, bpo = outs[oh]
                    for hc in range(8):
                        bw2u = b_w2q0h[hc // 4] if j == 0 else bw2
                        mm(po, h_sb[:, hc * 512 + sub * 128: hc * 512 + sub * 128 + 128],
                           w2q4[:, par, hc, oh * 512:(oh + 1) * 512],
                           start=(hc == 0), stop=(hc == 7 and qtr != 0),
                           reads=[b_h, bw2u], writes=[bpo])
                    if qtr == 0:
                        mm(po, onesb[0:1, 0:128], b2e[0:1, epar * D + oh * 512: epar * D + (oh + 1) * 512],
                           start=False, stop=True, reads=[b_onesb, b_b2e[epar]], writes=[bpo])
                if sub % 2 == 0:
                    S.add("vector",
                          lambda h, g=g, e=e: h.scalar_tensor_tensor(
                              g_pre[:, g * D:(g + 1) * D], ps_gA[:, :], C_all[:, g * NE + e: g * NE + e + 1],
                              g_pre[:, g * D:(g + 1) * D], ALU.mult, ALU.add),
                          reads=[b_gA, b_C, b_gpre_ch[g]], writes=[b_gpre_ch[g]])
                else:
                    S.add("vector",
                          lambda h, g=g, e=e: h.scalar_tensor_tensor(
                              g_pre[:, g * D:g * D + 512], ps_gB0[:, :], C_all[:, g * NE + e: g * NE + e + 1],
                              g_pre[:, g * D:g * D + 512], ALU.mult, ALU.add),
                          reads=[b_gB0, b_C, b_gpre_ch[g]], writes=[b_gpre_ch[g]])
                    S.add("vector",
                          lambda h, g=g, e=e: h.scalar_tensor_tensor(
                              g_pre[:, g * D + 512:(g + 1) * D], ps_gB1[:, :], C_all[:, g * NE + e: g * NE + e + 1],
                              g_pre[:, g * D + 512:(g + 1) * D], ALU.mult, ALU.add),
                          reads=[b_gB1, b_C, b_gpre_ch[g]], writes=[b_gpre_ch[g]])
        if j + 2 <= NE * 4 - 1:
            w_load(j + 2)

    if "gpre" in dbg:
        dma("scalar", dbg["gpre"].ap().rearrange("g p f -> p g f"),
            g_pre.ap().rearrange("p (g f) -> p g f", g=NCH), reads=b_gpre_ch)
    if "C" in dbg:
        dma("scalar", dbg["C"][:, :], C_all[:, :], reads=[b_C])
    if "lg" in dbg:
        dma("scalar", dbg["lg"][:, :], lgbuf[:, :], reads=[b_lg])
    if "Sc" in dbg:
        dma("scalar", dbg["Sc"][:, :], S_all[:, :], reads=[b_S])

    fence2 = S.snapshot()

    a3 = Arena(ARENA_BASE)
    WoT, b_WoT = a3.take("WoT", [128, 8 * D], BF16)
    wstage2, b_wstage2 = a3.take("wstage2", [128, D], BF16)
    kvfull, b_kvfull = a3.take("kvfull", [128, 1040], F32)
    kvm, b_kvm = a3.take("kvm", [128, 520], F32)
    bor, b_bor = a3.take("bor", [1, D], BF16)
    kvaug, b_kvaug = a3.take("kvaug", [128, 8 * 66], BF16)
    q_sb, _ = a3.take("q_sb", [128, 2 * 8 * 512], BF16)
    b_qsb = [Buf("qsb0"), Buf("qsb1")]
    xh2, _ = a3.take("xh2", [128, 2 * 2 * D], F32)
    b_xh2 = [Buf("xh2_0"), Buf("xh2_1")]
    aztm, _ = a3.take("aztm", [128, 2 * 1024], BF16)
    b_aztm = [Buf("aztm0"), Buf("aztm1")]
    azf, _ = a3.take("azf", [128, 2 * 8 * 512], BF16)
    b_azf = [Buf("azf0"), Buf("azf1")]
    den_sb, b_den = a3.take("den_sb", [128, 16], F32)
    zs_sb, b_zs = a3.take("zs_sb", [128, 16], F32)
    out_sb, b_out = a3.take("out_sb", [128, 4 * D], F32)
    gs_sb, b_gs = a3.take("gs_sb", [128, D], F32)
    mres, b_mres = a3.take("mres", [128, 512], F32)

    ps_atA, b_atA = P512[0], bP512[0]
    ps_atB, b_atB = P512[1], bP512[1]
    b_tr2 = [bP1024, Buf("ps1024_b1")]

    dma("gpsimd", bor[:, :], bo_row_in[:, :], writes=[b_bor], extra=fence2)
    for oc in range(8):
        dma("gpsimd", wstage2[:, :], W_in["Wo"][oc * 128:(oc + 1) * 128, :],
            reads=[b_wstage2], writes=[b_wstage2], extra=fence2)
        for ic in range(8):
            psb, bps = (ps_atA, b_atA) if ic % 2 == 0 else (ps_atB, b_atB)
            pbf = psb.ap().bitcast(BF16)
            tr(pbf[:, 0:128], wstage2[:, ic * 128:(ic + 1) * 128], idb[:, :],
               reads=[b_wstage2, b_idb], writes=[bps], extra=fence2)
            S.add("scalar",
                  lambda h, c=ic * D + oc * 128, p=pbf: h.copy(WoT[:, c:c + 128], p[:, 0:128]),
                  reads=[bps], writes=[b_WoT])

    dma("gpsimd", kvfull.ap().rearrange("p (b f) -> p b f", b=2),
        cc_out.ap().rearrange("(b p) f -> p b f", p=128),
        reads=[b_ccout], writes=[b_kvfull], extra=[cc_op] + list(fence2))
    if "kvfull" in dbg:
        dma("scalar", dbg["kvfull"][:, :], kvfull[:, :], reads=[b_kvfull])
    S.add("vector", lambda h: h.tensor_scalar(kvm[:, :], kvfull[:, 0:520], bmb[:, 0:1], None, ALU.mult),
          reads=[b_kvfull, b_bmb], writes=[b_kvm])
    S.add("vector", lambda h: h.scalar_tensor_tensor(kvm[:, :], kvfull[:, 520:1040], bmb[:, 1:2], kvm[:, :], ALU.mult, ALU.add),
          reads=[b_kvfull, b_bmb, b_kvm], writes=[b_kvm])
    for h_ in range(NH):
        pb, fo, sl = (h_ % 2) * 64, (h_ // 2) * 64, h_ // 2
        vop("tensor_copy", kvaug[pb:pb + 64, sl * 66:sl * 66 + 64],
            kvm[pb:pb + 64, fo:fo + 64], reads=[b_kvm], writes=[b_kvaug])
        vop("tensor_copy", kvaug[pb:pb + 64, sl * 66 + 64:sl * 66 + 65],
            kvm[pb:pb + 64, 512 + sl:512 + sl + 1], reads=[b_kvm], writes=[b_kvaug])

    q4 = q_sb.ap().rearrange("p (a c f) -> p a c f", a=2, c=8)
    xh2v = xh2.ap().rearrange("p (a u f) -> p a u f", a=2, u=2)

    for t in (range(NTILE) if "attn" not in ablate else ()):
        qpar = t % 2
        azp = t % 2
        if t == 0:
            dma("sync", q4[:, 0, :, :], q_spill[0].rearrange("(c p) f -> p c f", p=128),
                reads=[b_qsp, b_qsb[0]], writes=[b_qsb[0]], extra=fence2)
            for hf in range(2):
                dma("sync", xh2v[:, hf, :, :],
                    x_in[hf * 256:(hf + 1) * 256, :].rearrange("(u p) f -> p u f", p=128),
                    reads=[b_xh2[hf]], writes=[b_xh2[hf]], extra=fence2)

        for sub in (range(NSUB) if "p3cb" not in ablate else ()):
            pA, bA = (P512[0], bP512[0]) if sub % 2 == 0 else (P512[2], bP512[2])
            pB, bB = (P512[1], bP512[1]) if sub % 2 == 0 else (P512[3], bP512[3])
            pD0, bD0 = P512[4], bP512[4]
            pD1, bD1 = P512[5], bP512[5]
            for h_ in range(NH):
                qrow = (h_ % 2) * 64
                sl = h_ // 2
                qT = q4[qrow:qrow + 64, qpar, sl, sub * 128:(sub + 1) * 128]
                po, bo_ = (pA, bA) if h_ % 2 == 0 else (pB, bB)
                mm(po[:, sl * 64:sl * 64 + 64], qT,
                   kvaug[qrow:qrow + 64, sl * 66:sl * 66 + 64],
                   start=True, stop=True, reads=[b_qsb[qpar], b_kvaug], writes=[bo_])
                pd, bd = (pD0, bD0) if h_ % 2 == 0 else (pD1, bD1)
                mm(pd[:, sl:sl + 1], qT, kvaug[qrow:qrow + 64, sl * 66 + 64:sl * 66 + 65],
                   start=True, stop=True, reads=[b_qsb[qpar], b_kvaug], writes=[bd])
            if "p3z" in ablate:
                continue
            S.add("vector", lambda h, pD0=pD0: h.tensor_scalar_add(den_sb[:, 0:8], pD0[:, 0:8], 1e-6),
                  reads=[bD0], writes=[b_den])
            S.add("vector", lambda h, pD1=pD1: h.tensor_scalar_add(den_sb[:, 8:16], pD1[:, 0:8], 1e-6),
                  reads=[bD1], writes=[b_den])
            S.add("vector", lambda h: h.reciprocal(zs_sb[:, :], den_sb[:, :]),
                  reads=[b_den], writes=[b_zs])
            for h_ in range(NH):
                sl = h_ // 2
                po, bo_ = (pA, bA) if h_ % 2 == 0 else (pB, bB)
                src = po[:, sl * 64:sl * 64 + 64]
                zcol = sl + 8 * (h_ % 2)
                dst = aztm[:, (sub % 2) * 1024 + h_ * 64:(sub % 2) * 1024 + (h_ + 1) * 64]
                if h_ % 2 == 0:
                    S.add("vector", lambda h, d=dst, sr=src, zc=zcol: h.tensor_scalar(d, sr, zs_sb[:, zc:zc + 1], None, ALU.mult),
                          reads=[bo_, b_zs], writes=[b_aztm[sub % 2]])
                else:
                    S.add("scalar", lambda h, d=dst, sr=src, zc=zcol: h.activation(d, sr, AF.Copy, scale=zs_sb[:, zc:zc + 1]),
                          reads=[bo_, b_zs], writes=[b_aztm[sub % 2]])
            for ic in (range(8) if "p3tr" not in ablate else ()):
                trbank = ic % 2
                pt = P1024.ap().bitcast(BF16)[:, trbank * 1024 + 0:trbank * 1024 + 128]
                tr(pt, aztm[:, (sub % 2) * 1024 + ic * 128:(sub % 2) * 1024 + (ic + 1) * 128], idb[:, :],
                   reads=[b_aztm[sub % 2], b_idb], writes=[b_tr2[trbank]])
                col = ic * TILE + sub * 128
                S.add("scalar", lambda h, c=col, p=pt, azp=azp: h.copy(
                      azf[:, azp * 4096 + c: azp * 4096 + c + 128], p),
                      reads=[b_tr2[trbank]], writes=[b_azf[azp]])

        if t + 1 < NTILE:
            dma("sync", q4[:, (t + 1) % 2, :, :], q_spill[t + 1].rearrange("(c p) f -> p c f", p=128),
                reads=[b_qsp, b_qsb[(t + 1) % 2]], writes=[b_qsb[(t + 1) % 2]])
        for sub in (range(NSUB) if "p3wo" not in ablate else ()):
            g = t * NSUB + sub
            par = (t * 2 + sub // 2) % 2
            S.add("scalar",
                  lambda h, g=g: h.activation(gs_sb[:, :], g_pre[:, g * D:(g + 1) * D], AF.Silu,
                                              scale=S_all[:, g:g + 1]),
                  reads=[b_gpre_ch[g], b_S], writes=[b_gs])
            for oh in range(2):
                pw, bpw = P512[sub], bP512[sub]
                for ic in range(8):
                    mm(pw[:, :], azf[:, azp * 4096 + ic * TILE + sub * 128: azp * 4096 + ic * TILE + sub * 128 + 128],
                       WoT[:, ic * D + oh * 512:(ic) * D + (oh + 1) * 512],
                       start=(ic == 0), stop=False, reads=[b_azf[azp], b_WoT], writes=[bpw])
                mm(pw[:, :], onesb[0:1, 0:128], bor[0:1, oh * 512:(oh + 1) * 512],
                   start=False, stop=True, reads=[b_onesb, b_bor], writes=[bpw])
                S.add("vector", lambda h, pw=pw, oh=oh: h.tensor_mul(
                      mres[:, :], pw[:, :], gs_sb[:, oh * 512:(oh + 1) * 512]),
                      reads=[bpw, b_gs], writes=[b_mres])
                S.add("vector",
                      lambda h, sub=sub, par=par, oh=oh: h.tensor_add(
                          out_sb[:, sub * D + oh * 512: sub * D + (oh + 1) * 512],
                          xh2v[:, par, sub % 2, oh * 512:(oh + 1) * 512], mres[:, :]),
                      reads=[b_xh2[par], b_mres], writes=[b_out])
        dma("sync", y_out[t * TILE:(t + 1) * TILE, :].rearrange("(s p) f -> p s f", p=128),
            out_sb.ap().rearrange("p (s f) -> p s f", s=NSUB),
            reads=[b_out], writes=[b_y])
        if t + 1 < NTILE:
            for hf in range(2):
                dma("sync", xh2v[:, hf, :, :],
                    x_in[(t + 1) * TILE + hf * 256: (t + 1) * TILE + (hf + 1) * 256, :].rearrange("(u p) f -> p u f", p=128),
                    reads=[b_xh2[hf]], writes=[b_xh2[hf]])

    S.add("gpsimd", lambda h: h.nop(), extra_waits=S.snapshot())

    guards = S.emit(nc)
    return nc


def kernel(**inputs):
    x = np.ascontiguousarray(np.asarray(inputs["L0_attn_hidden_states"], dtype=np.float32).reshape(2 * 16384, D))
    Wq = np.ascontiguousarray(np.asarray(inputs["L0_attn_Wq"], dtype=np.float32))
    Wk = np.ascontiguousarray(np.asarray(inputs["L0_attn_Wk"], dtype=np.float32))
    Wv = np.ascontiguousarray(np.asarray(inputs["L0_attn_Wv"], dtype=np.float32))
    Wo = np.ascontiguousarray(np.asarray(inputs["L0_attn_Wo"], dtype=np.float32))
    bq = np.asarray(inputs["L0_attn_bq"], dtype=np.float32)
    bk = np.asarray(inputs["L0_attn_bk"], dtype=np.float32)
    bv = np.asarray(inputs["L0_attn_bv"], dtype=np.float32)
    bo = np.asarray(inputs["L0_attn_bo"], dtype=np.float32)
    wg_red = np.ascontiguousarray(np.asarray(inputs["L0_ffn_wg_red"], dtype=np.float32))
    wg = np.ascontiguousarray(np.asarray(inputs["L0_ffn_wg"], dtype=np.float32))
    w1 = np.ascontiguousarray(np.asarray(inputs["L0_ffn_w1"], dtype=np.float32))
    b1 = np.asarray(inputs["L0_ffn_b1"], dtype=np.float32)
    w2 = np.ascontiguousarray(np.asarray(inputs["L0_ffn_w2"], dtype=np.float32))
    b2 = np.asarray(inputs["L0_ffn_b2"], dtype=np.float32)

    bq_t = np.ascontiguousarray(bq.reshape(8, 128).T)
    bo_t = np.ascontiguousarray(bo.reshape(8, 128).T)
    bk_row = np.ascontiguousarray(bk.reshape(1, D))
    bo_row = np.ascontiguousarray(bo.reshape(1, D))
    bv_row = np.ascontiguousarray(bv.reshape(1, D))
    b1_t = np.ascontiguousarray(b1.reshape(NE, 32, 128).transpose(2, 1, 0).reshape(128, 32 * NE))
    b2row = np.ascontiguousarray(b2.reshape(1, NE * D))
    ident = np.eye(128, dtype=np.float32)
    ones_row = np.ones((1, 512), dtype=np.float32)
    ones_col = np.ones((128, 1), dtype=np.float32)
    pw2 = (2.0 ** -np.arange(NE, dtype=np.float32)).reshape(1, NE)

    nc = build_program(debug=tuple(DEBUG), ablate=frozenset(ABLATE))

    in_maps = []
    for c in range(NCORE):
        bmask = np.zeros((1, 2), dtype=np.float32)
        bmask[0, c // 4] = 1.0
        in_maps.append({
            "x": x[c * TOK:(c + 1) * TOK],
            "Wq": Wq, "Wk": Wk, "Wv": Wv, "Wo": Wo,
            "bq_t": bq_t, "bo_t": bo_t, "bk_row": bk_row, "bv_row": bv_row, "bo_row": bo_row,
            "wg_red": wg_red, "wg": wg, "w1": w1, "b1_t": b1_t, "w2": w2, "b2row": b2row,
            "ident": ident, "ones_row": ones_row, "ones_col": ones_col,
            "pw2": pw2, "bmask": bmask,
        })

    res = run_bass_kernel_spmd(nc, in_maps, core_ids=list(range(NCORE)))
    out = np.concatenate([res.results[c]["y"] for c in range(NCORE)], axis=0)
    if DEBUG:
        kernel.debug_results = res.results
    return out.reshape(2, 16384, D)


if __name__ == "__main__":
    build_program()
    print("build ok")


# version 09
# speedup vs baseline: 2.3691x; 1.0026x over previous
import numpy as np
import ml_dtypes
import concourse.bass as bass
import concourse.mybir as mybir
from concourse.bass_utils import run_bass_kernel_spmd

F32 = mybir.dt.float32
BF16 = mybir.dt.bfloat16
F16 = mybir.dt.float16
AF = mybir.ActivationFunctionType
ALU = mybir.AluOpType
AX = mybir.AxisListType

NCORE = 8
TOK = 4096
D = 1024
NH = 16
NE = 8
HID = 4096
TILE = 512
NTILE = TOK // TILE
NSUB = TILE // 128
NCH = TOK // 128

DEBUG = []
ABLATE = set()


class Buf:
    __slots__ = ("name", "w", "readers")

    def __init__(self, name):
        self.name = name
        self.w = None
        self.readers = {}


class Op:
    __slots__ = ("eng", "fn", "waits", "dom", "inc", "flagged", "count_after", "i", "eng_seq", "raw")


RAW_K = 4


class Sched:
    def __init__(self):
        self.plan = []
        self.eng_n = {}

    def add(self, eng, fn, reads=(), writes=(), dma=False, cc=False, extra_waits=()):
        op = Op()
        op.eng = eng
        op.fn = fn
        op.dom = eng + "_dma" if dma else ("cc" if cc else eng)
        op.inc = 16 if dma else 1
        op.flagged = False
        op.count_after = None
        op.i = len(self.plan)
        op.eng_seq = self.eng_n.get(eng, 0)
        self.eng_n[eng] = op.eng_seq + 1
        ws = {}
        rw = {}

        def upd(d, w):
            c = d.get(w.dom)
            if c is None or w.i > c.i:
                d[w.dom] = w

        for b in reads:
            if b.w is not None:
                upd(ws, b.w)
                upd(rw, b.w)
        for b in writes:
            for r in b.readers.values():
                upd(ws, r)
            if b.w is not None:
                upd(ws, b.w)
        for w in extra_waits:
            if w is not None:
                upd(ws, w)
        waits = set(ws.values())
        raw = set(rw.values())
        waits.discard(op)
        raw.discard(op)
        for w in raw:
            waits.add(w)
        op.waits = sorted(waits, key=lambda o: o.i)
        op.raw = raw
        for b in reads:
            b.readers[op.dom] = op
        for b in writes:
            b.w = op
            b.readers = {}
        self.plan.append(op)
        return op

    def snapshot(self):
        last = {}
        for op in self.plan:
            last[op.dom] = op
        return list(last.values())

    def emit(self, nc):
        for op in self.plan:
            if op.dom.endswith("_dma") or op.dom == "cc":
                op.flagged = True
        def need_wait(op, w):
            if w.dom != op.eng:
                return True
            if op.eng == "tensor":
                return False
            return (op.eng_seq - w.eng_seq) <= RAW_K

        for op in self.plan:
            for w in op.waits:
                if need_wait(op, w):
                    w.flagged = True
        counts = {}
        for op in self.plan:
            if op.flagged:
                counts[op.dom] = counts.get(op.dom, 0) + op.inc
                op.count_after = counts[op.dom]
        doms = set(counts.keys())
        sems = {}
        guards = []
        for d in sorted(doms):
            g = nc.semaphore("sem_" + d)
            sems[d] = g.__enter__()
            guards.append(g)

        engs = ("tensor", "vector", "scalar", "gpsimd", "sync")
        per_eng = {e: [op for op in self.plan if op.eng == e] for e in engs}

        def mk_body(eng):
            ops = per_eng[eng]

            def body(h):
                hw = {}
                for op in ops:
                    for w in op.waits:
                        if not need_wait(op, w):
                            continue
                        assert w.flagged, (w.i, op.i)
                        tgt = w.count_after
                        if hw.get(w.dom, 0) >= tgt:
                            continue
                        h.wait_ge(sems[w.dom], tgt)
                        hw[w.dom] = tgt
                    inst = op.fn(h)
                    if op.flagged:
                        inst.then_inc(sems[op.dom], op.inc)
            return body

        with nc.Block() as block:
            for eng in engs:
                if per_eng[eng]:
                    getattr(block, eng)(mk_body(eng))
        return guards


DT_SIZE = {F32: 4, BF16: 2, F16: 2}


def build_program(debug=(), ablate=frozenset()):
    nc = bass.Bass("TRN2", target_bir_lowering=False)
    S = Sched()

    x_in = nc.dram_tensor("x", [TOK, D], F32, kind="ExternalInput")
    W_in = {w: nc.dram_tensor(w, [D, D], F32, kind="ExternalInput") for w in ("Wq", "Wk", "Wv", "Wo")}
    bq_t_in = nc.dram_tensor("bq_t", [128, 8], F32, kind="ExternalInput")
    bo_t_in = nc.dram_tensor("bo_t", [128, 8], F32, kind="ExternalInput")
    bk_row_in = nc.dram_tensor("bk_row", [1, D], F32, kind="ExternalInput")
    bo_row_in = nc.dram_tensor("bo_row", [1, D], F32, kind="ExternalInput")
    bv_row_in = nc.dram_tensor("bv_row", [1, D], F32, kind="ExternalInput")
    wgred_in = nc.dram_tensor("wg_red", [16, D], F32, kind="ExternalInput")
    wg_in = nc.dram_tensor("wg", [NE, 16], F32, kind="ExternalInput")
    w1_in = nc.dram_tensor("w1", [NE, D, HID], F32, kind="ExternalInput")
    b1_t_in = nc.dram_tensor("b1_t", [128, 32 * NE], F32, kind="ExternalInput")
    w2_in = nc.dram_tensor("w2", [NE, HID, D], F32, kind="ExternalInput")
    b2row_in = nc.dram_tensor("b2row", [1, NE * D], F32, kind="ExternalInput")
    ident_in = nc.dram_tensor("ident", [128, 128], F32, kind="ExternalInput")
    ones_in = nc.dram_tensor("ones_row", [1, 512], F32, kind="ExternalInput")
    onesc_in = nc.dram_tensor("ones_col", [128, 1], F32, kind="ExternalInput")
    pw2_in = nc.dram_tensor("pw2", [1, NE], F32, kind="ExternalInput")
    bmask_in = nc.dram_tensor("bmask", [1, 2], F32, kind="ExternalInput")

    y_out = nc.dram_tensor("y", [TOK, D], F32, kind="ExternalOutput")

    xn_spill = nc.dram_tensor("xn_spill", [NTILE, D, TILE], BF16)
    q_spill = nc.dram_tensor("q_spill", [NTILE, D, TILE], BF16)
    cc_in = nc.dram_tensor("cc_in", [2 * 128, 520], F32)
    cc_out = nc.dram_tensor("cc_out", [2 * 128, 520], F32)

    dbg = {}
    if "xn" in debug:
        dbg["xn"] = nc.dram_tensor("dbg_xn", [NTILE, D, TILE], BF16, kind="ExternalOutput")
    if "q" in debug:
        dbg["q"] = nc.dram_tensor("dbg_q", [NTILE, D, TILE], BF16, kind="ExternalOutput")
    if "kv" in debug:
        dbg["kv"] = nc.dram_tensor("dbg_kv", [128, 520], F32, kind="ExternalOutput")
    if "kvfull" in debug:
        dbg["kvfull"] = nc.dram_tensor("dbg_kvfull", [128, 1040], F32, kind="ExternalOutput")
    if "C" in debug:
        dbg["C"] = nc.dram_tensor("dbg_C", [128, NCH * NE], F32, kind="ExternalOutput")
    if "Sc" in debug:
        dbg["Sc"] = nc.dram_tensor("dbg_Sc", [128, NCH], F32, kind="ExternalOutput")
    if "gpre" in debug:
        dbg["gpre"] = nc.dram_tensor("dbg_gpre", [NCH, 128, D], F16, kind="ExternalOutput")
    if "xr" in debug:
        dbg["xr"] = nc.dram_tensor("dbg_xr", [NTILE, 16, TILE], F32, kind="ExternalOutput")
    if "lg" in debug:
        dbg["lg"] = nc.dram_tensor("dbg_lg", [128, NCH * NE], F32, kind="ExternalOutput")
    if "lnraw" in debug:
        dbg["lnraw"] = nc.dram_tensor("dbg_lnraw", [128, D], F32, kind="ExternalOutput")
        dbg["lnst"] = nc.dram_tensor("dbg_lnst", [128, 8], F32, kind="ExternalOutput")
    if "ktm" in debug:
        dbg["ktm"] = nc.dram_tensor("dbg_ktm", [128, D], BF16, kind="ExternalOutput")
        dbg["vtm"] = nc.dram_tensor("dbg_vtm", [128, D], BF16, kind="ExternalOutput")

    SB_BASE = (nc.sbuf_base + 63) & ~63
    SB_TOP = nc.sbuf_top

    class Arena:
        def __init__(self, base):
            self.off = base
            self.hi = base

        def take(self, name, shape, dtype):
            nbytes = int(np.prod(shape[1:])) * DT_SIZE[dtype]
            off = (self.off + 63) & ~63
            t = nc.alloc_sbuf_tensor_at(name, list(shape), dtype, offset=off)
            self.off = off + nbytes
            self.hi = max(self.hi, self.off)
            assert self.off <= SB_TOP, (name, self.off, SB_TOP)
            return t, Buf(name)

    pers = Arena(SB_BASE)
    g_pre, b_gpre = pers.take("g_pre", [128, NCH * D], F16)
    b_gpre_ch = [Buf(f"g_pre{g}") for g in range(NCH)]
    C_all, b_C = pers.take("C_all", [128, NCH * NE], F32)
    S_all, b_S = pers.take("S_all", [128, NCH], F32)
    b1t, b_b1t = pers.take("b1t", [128, 32 * NE], F32)
    bqt, b_bqt = pers.take("bqt", [128, 8], F32)
    bot, b_bot = pers.take("bot", [128, 8], F32)
    idf, b_idf = pers.take("idf", [128, 128], F32)
    idb, b_idb = pers.take("idb", [128, 128], BF16)
    onesf, b_onesf = pers.take("onesf", [1, 512], F32)
    onesb, b_onesb = pers.take("onesb", [1, 512], BF16)
    onescb, b_onescb = pers.take("onescb", [128, 1], BF16)
    pw2b, b_pw2 = pers.take("pw2b", [128, NE], F32)
    bmb, b_bmb = pers.take("bmb", [128, 2], F32)
    ARENA_BASE = (pers.off + 63) & ~63

    b_ccin, b_ccout = Buf("cc_in"), Buf("cc_out")
    b_xnsp, b_qsp = Buf("xn_spill"), Buf("q_spill")
    b_y = Buf("y")

    P512 = []
    bP512 = []
    for i in range(6):
        t = nc.alloc_psum_tensor(f"ps{i}", [128, 512], F32)
        P512.append(t)
        bP512.append(Buf(f"ps{i}"))
    P1024 = nc.alloc_psum_tensor("ps1024", [128, 1024], F32)
    bP1024 = Buf("ps1024")

    def dma(eng, dst, src, reads=(), writes=(), extra=()):
        return S.add(eng, lambda h, d=dst, s=src: h.dma_start(out=d, in_=s),
                     reads=reads, writes=writes, dma=True, extra_waits=extra)

    dma("gpsimd", idb[:, :], ident_in[:, :], writes=[b_idb])
    dma("gpsimd", onesb[:, :], ones_in[:, :], writes=[b_onesb])
    dma("gpsimd", onescb[:, :], onesc_in[:, :], writes=[b_onescb])
    dma("sync", b1t[:, :], b1_t_in[:, :], writes=[b_b1t])
    dma("sync", bqt[:, :], bq_t_in[:, :], writes=[b_bqt])
    dma("sync", bot[:, :], bo_t_in[:, :], writes=[b_bot])
    dma("sync", idf[:, :], ident_in[:, :], writes=[b_idf])
    dma("sync", onesf[:, :], ones_in[:, :], writes=[b_onesf])
    dma("sync", pw2b[:, :], pw2_in.ap().broadcast_to([128, NE]), writes=[b_pw2])
    dma("sync", bmb[:, :], bmask_in.ap().broadcast_to([128, 2]), writes=[b_bmb])

    def mm(out, lhsT, rhs, start, stop, reads, writes, skip_gc=True, extra=()):
        return S.add("tensor",
                     lambda h, o=out, l=lhsT, r=rhs, st=start, sp=stop, sg=skip_gc:
                     h.matmul(o, l, r, start=st, stop=sp, skip_group_check=sg),
                     reads=reads, writes=writes, extra_waits=extra)

    def tr(out, in_, ident, reads, writes, extra=()):
        return S.add("tensor", lambda h, o=out, i=in_, d=ident: h.transpose(o, i, d),
                     reads=reads, writes=writes, extra_waits=extra)

    def act(out, in_, func, bias=0.0, scale=1.0, reads=(), writes=(), extra=()):
        return S.add("scalar",
                     lambda h, o=out, i=in_, f=func, b=bias, sc=scale: h.activation(o, i, f, bias=b, scale=sc),
                     reads=reads, writes=writes, extra_waits=extra)

    def vop(name, *aps, reads=(), writes=(), extra=(), eng="vector"):
        def fn(h, n=name, a=aps):
            return getattr(h, n)(*a)
        return S.add(eng, fn, reads=reads, writes=writes, extra_waits=extra)

    a1 = Arena(ARENA_BASE)
    WqT, b_WqT = a1.take("WqT", [128, 8 * D], BF16)
    WkT, b_WkT = a1.take("WkT", [128, 8 * D], BF16)
    WvT, b_WvT = a1.take("WvT", [128, 8 * D], BF16)
    wstage, b_wstage = a1.take("wstage", [128, D], BF16)
    wgredT, b_wgredT = a1.take("wgredT", [128, 8 * 16], F32)
    wgr_st, b_wgr_st = a1.take("wgr_st", [16, D], F32)
    wgu, b_wgu = a1.take("wgu", [NE, 16], F32)
    wguT, b_wguT = a1.take("wguT", [16, NE], F32)
    wgtmp, b_wgtmp = a1.take("wgtmp", [NE, 20], F32)
    bkr, b_bkr = a1.take("bkr", [1, D], BF16)
    bvr, b_bvr = a1.take("bvr", [1, D], BF16)
    xh, _ = a1.take("xh", [128, 2 * D], F32)
    b_xh = [Buf("xh0"), Buf("xh1")]
    ctr2, _ = a1.take("ctr2", [128, 2 * D], F32)
    b_ctr2 = [Buf("ctr0"), Buf("ctr1")]
    scr1, b_scr1 = a1.take("scr1", [128, D], F32)
    scr22, _ = a1.take("scr22", [128, 2 * D], F32)
    b_scr22 = [Buf("xnl0"), Buf("xnl1")]
    lnst2, _ = a1.take("lnst2", [128, 16], F32)
    b_lnst2 = [Buf("lnst0"), Buf("lnst1")]
    xnf_bf2, _ = a1.take("xnf_bf2", [128, 2 * 8 * TILE], BF16)
    b_xnbf2 = [Buf("xnbf0"), Buf("xnbf1")]
    xnf_f32, b_xnf32 = a1.take("xnf_f32", [128, 8 * 256], F32)
    q_tile, b_q = a1.take("q_tile", [128, 8 * TILE], BF16)
    k_tm, b_ktm = a1.take("k_tm", [128, D], BF16)
    v_tm, b_vtm = a1.take("v_tm", [128, D], BF16)
    xr_sb, _ = a1.take("xr_sb", [16, 2 * TILE], F32)
    b_xr2 = [Buf("xr0"), Buf("xr1")]
    gsc, b_gsc = a1.take("gsc", [128, 8 * 8], F32)
    qsc, b_qsc = a1.take("qsc", [128, D], F32)
    if "lg" in debug:
        lgbuf, b_lg = a1.take("lgbuf", [128, NCH * NE], F32)
    kvacc, b_kvacc = a1.take("kvacc", [128, 520], F32)
    contrib, b_contrib = a1.take("contrib", [128, 2 * 520], F32)

    ps_kvh, b_pkvh = P512[0], bP512[0]
    ps_ksum, b_pksum = P512[1], bP512[1]
    ps_trA, b_trA = P512[2], bP512[2]
    ps_trB, b_trB = P512[3], bP512[3]
    ps_mmA, b_mmA = P512[4], bP512[4]
    ps_mmB, b_mmB = P512[5], bP512[5]
    ps_kv2, b_pkv2 = P1024, bP1024

    dma("gpsimd", bkr[:, :], bk_row_in[:, :], writes=[b_bkr])
    dma("gpsimd", bvr[:, :], bv_row_in[:, :], writes=[b_bvr])

    def prep_wT(nm, WT, b_WT):
        for oc in range(8):
            dma("gpsimd", wstage[:, :], W_in[nm][oc * 128:(oc + 1) * 128, :],
                reads=(), writes=[b_wstage])
            for ic in range(8):
                psb, bps = (ps_trA, b_trA) if ic % 2 == 0 else (ps_trB, b_trB)
                pbf = psb.ap().bitcast(BF16)
                tr(pbf[:, 0:128], wstage[:, ic * 128:(ic + 1) * 128], idb[:, :],
                   reads=[b_wstage, b_idb], writes=[bps])
                S.add("scalar",
                      lambda h, w=WT, c=ic * D + oc * 128, p=pbf: h.copy(w[:, c:c + 128], p[:, 0:128]),
                      reads=[bps], writes=[b_WT])

    prep_wT("Wq", WqT, b_WqT)
    prep_wT("Wk", WkT, b_WkT)
    prep_wT("Wv", WvT, b_WvT)

    dma("sync", wgr_st[:, :], wgred_in[:, :], writes=[b_wgr_st])
    for ic in range(8):
        psb, bps = (ps_trA, b_trA) if ic % 2 == 0 else (ps_trB, b_trB)
        tr(psb[0:128, 0:16], wgr_st[0:16, ic * 128:(ic + 1) * 128], idf[0:16, 0:16],
           reads=[b_wgr_st, b_idf], writes=[bps])
        vop("tensor_copy", wgredT[:, ic * 16:(ic + 1) * 16], psb[0:128, 0:16],
            reads=[bps], writes=[b_wgredT])

    dma("sync", wgu[:, :], wg_in[:, :], writes=[b_wgu])
    act(wgtmp[:, 0:16], wgu[:, :], AF.Square, reads=[b_wgu], writes=[b_wgtmp])
    vop("reduce_sum", wgtmp[:, 16:17], wgtmp[:, 0:16], AX.X, reads=[b_wgtmp], writes=[b_wgtmp])
    act(wgtmp[:, 17:18], wgtmp[:, 16:17], AF.Sqrt, reads=[b_wgtmp], writes=[b_wgtmp])
    vop("tensor_scalar_max", wgtmp[:, 17:18], wgtmp[:, 17:18], 1e-4, reads=[b_wgtmp], writes=[b_wgtmp])
    vop("reciprocal", wgtmp[:, 18:19], wgtmp[:, 17:18], reads=[b_wgtmp], writes=[b_wgtmp])
    S.add("vector", lambda h: h.tensor_scalar(wgu[:, :], wgu[:, :], wgtmp[:, 18:19], None, ALU.mult),
          reads=[b_wgtmp], writes=[b_wgu])
    tr(ps_trA[0:16, 0:NE], wgu[:, :], idf[0:NE, 0:NE], reads=[b_wgu, b_idf], writes=[b_trA])
    vop("tensor_copy", wguT[:, :], ps_trA[0:16, 0:NE], reads=[b_trA], writes=[b_wguT])


    def xnf(t):
        return xnf_bf2.ap().rearrange("p (a c f) -> p a c f", a=2, c=8)[:, t % 2, :, :]

    def stage_ln(t):
        bxnbf = b_xnbf2[t % 2]
        if t == 0:
            for k0 in range(2):
                dma("sync", xh[:, (k0 % 2) * D:(k0 % 2 + 1) * D],
                    x_in[k0 * 128:(k0 + 1) * 128, :],
                    reads=[b_xh[k0 % 2]], writes=[b_xh[k0 % 2]])
        for s in range(NSUB):
            kg = t * NSUB + s
            par = kg % 2
            pp = s % 2
            xsub = xh[:, par * D:(par + 1) * D]
            bx = b_xh[par]
            ctr = ctr2[:, pp * D:(pp + 1) * D]
            b_ctr = b_ctr2[pp]
            scr2 = scr22[:, pp * D:(pp + 1) * D]
            b_scr2 = b_scr22[pp]
            lnst = lnst2[:, pp * 8:(pp + 1) * 8]
            b_lnst = b_lnst2[pp]
            vop("reduce_sum", lnst[:, 0:1], xsub, AX.X, reads=[bx], writes=[b_lnst])
            vop("tensor_scalar_mul", lnst[:, 1:2], lnst[:, 0:1], 1.0 / D, reads=[b_lnst], writes=[b_lnst])
            S.add("vector", lambda h, xs=xsub, ctr=ctr, lnst=lnst: h.tensor_scalar(ctr[:, :], xs, lnst[:, 1:2], None, ALU.subtract),
                  reads=[bx, b_lnst], writes=[b_ctr])
            act(scr2[:, :], ctr[:, :], AF.Square, reads=[b_ctr], writes=[b_scr2])
            vop("reduce_sum", lnst[:, 2:3], scr2[:, :], AX.X, reads=[b_scr2], writes=[b_lnst])
            act(lnst[:, 3:4], lnst[:, 2:3], AF.Ln, bias=np.float32(1e-5), scale=1.0 / D, reads=[b_lnst], writes=[b_lnst])
            act(lnst[:, 4:5], lnst[:, 3:4], AF.Exp, scale=-0.5, reads=[b_lnst], writes=[b_lnst])
            S.add("vector", lambda h, ctr=ctr, lnst=lnst, scr2=scr2: h.tensor_scalar(scr2[:, :], ctr[:, :], lnst[:, 4:5], None, ALU.mult),
                  reads=[b_ctr, b_lnst], writes=[b_scr2])
            for ic in range(8):
                psb, bps = (ps_trA, b_trA) if ic % 2 == 0 else (ps_trB, b_trB)
                slot = (ic // 2) % 2
                tr(psb[:, slot * 128:slot * 128 + 128], scr2[:, ic * 128:(ic + 1) * 128], idf[:, :],
                   reads=[b_scr2, b_idf], writes=[bps])
                colf = ic * 256 + pp * 128
                vop("tensor_copy", xnf_f32[:, colf:colf + 128], psb[:, slot * 128:slot * 128 + 128],
                    reads=[bps], writes=[b_xnf32])
                S.add("scalar", lambda h, t=t, c=ic * TILE + s * 128, cf=colf: h.copy(
                      xnf(t)[:, c // TILE, c % TILE: c % TILE + 128], xnf_f32[:, cf:cf + 128]),
                      reads=[b_xnf32], writes=[bxnbf])
            if s % 2 == 1 and "gating" not in ablate:
                hf2 = s // 2
                for ic in range(8):
                    mm(ps_mmA[0:16, hf2 * 256:hf2 * 256 + 256], wgredT[:, ic * 16:(ic + 1) * 16],
                       xnf_f32[:, ic * 256:(ic + 1) * 256],
                       start=(ic == 0), stop=(ic == 7), reads=[b_wgredT, b_xnf32], writes=[b_mmA])
                vop("tensor_copy", xr_sb[:, (t % 2) * TILE + hf2 * 256:(t % 2) * TILE + hf2 * 256 + 256],
                    ps_mmA[0:16, hf2 * 256:hf2 * 256 + 256],
                    reads=[b_mmA], writes=[b_xr2[t % 2]])
            if kg + 2 < NTILE * NSUB:
                k2 = kg + 2
                dma("sync", xh[:, (k2 % 2) * D:(k2 % 2 + 1) * D],
                    x_in[k2 * 128:(k2 + 1) * 128, :],
                    reads=[b_xh[k2 % 2]], writes=[b_xh[k2 % 2]])
        dma("scalar", xn_spill[t].rearrange("(c p) f -> p c f", p=128),
            xnf(t), reads=[bxnbf], writes=[b_xnsp])
        if "xn" in dbg:
            dma("scalar", dbg["xn"][t].rearrange("(c p) f -> p c f", p=128), xnf(t), reads=[bxnbf])

    def stage_proj(t):
        bxnbf = b_xnbf2[t % 2]
        xt = xnf(t)
        for oc in (range(8) if "q" not in ablate else ()):
            psq, bpsq = (ps_mmA, b_mmA) if oc % 2 == 0 else (ps_mmB, b_mmB)
            for ic in range(8):
                mm(psq[:, :], WqT[:, ic * D + oc * 128: ic * D + oc * 128 + 128],
                   xt[:, ic, :],
                   start=(ic == 0), stop=(ic == 7),
                   reads=[b_WqT, bxnbf], writes=[bpsq])
            act(scr1[:, 0:TILE], psq[:, :], AF.Relu, bias=bqt[:, oc:oc + 1],
                reads=[bpsq, b_bqt], writes=[b_scr1])
            S.add("vector",
                  lambda h, p=psq, oc=oc: h.scalar_tensor_tensor(
                      qsc[:, 0:TILE], p[:, :], bqt[:, oc:oc + 1], scr1[:, 0:TILE], ALU.add, ALU.subtract),
                  reads=[bpsq, b_bqt, b_scr1], writes=[b_qsc])
            act(psq[:, :], qsc[:, 0:TILE], AF.Exp, reads=[b_qsc], writes=[bpsq])
            S.add("vector", lambda h, p=psq, oc=oc: h.tensor_add(
                  q_tile[:, oc * TILE:(oc + 1) * TILE], p[:, :], scr1[:, 0:TILE]),
                  reads=[bpsq, b_scr1], writes=[b_q])
        dma("scalar", q_spill[t].rearrange("(c p) f -> p c f", p=128),
            q_tile.ap().rearrange("p (c f) -> p c f", c=8),
            reads=[b_q], writes=[b_qsp])
        if "q" in dbg:
            dma("scalar", dbg["q"][t].rearrange("(c p) f -> p c f", p=128),
                q_tile.ap().rearrange("p (c f) -> p c f", c=8), reads=[b_q])

        for s in (range(NSUB) if "kv" not in ablate else ()):
            for half in range(2):
                for ic in range(8):
                    mm(ps_kv2[:, half * 512:(half + 1) * 512],
                       xt[:, ic, s * 128:s * 128 + 128],
                       WkT[:, ic * D + half * 512: ic * D + (half + 1) * 512],
                       start=(ic == 0), stop=False,
                       reads=[bxnbf, b_WkT], writes=[b_pkv2])
                mm(ps_kv2[:, half * 512:(half + 1) * 512],
                   onesb[0:1, 0:128], bkr[0:1, half * 512:(half + 1) * 512],
                   start=False, stop=True,
                   reads=[b_onesb, b_bkr], writes=[b_pkv2])
            act(scr1[:, :], ps_kv2[:, :], AF.Relu, reads=[b_pkv2], writes=[b_scr1])
            vop("tensor_sub", qsc[:, :], ps_kv2[:, :], scr1[:, :], reads=[b_pkv2, b_scr1], writes=[b_qsc])
            act(ps_kv2[:, :], qsc[:, :], AF.Exp, reads=[b_qsc], writes=[b_pkv2])
            vop("tensor_add", k_tm[:, :], ps_kv2[:, :], scr1[:, :], reads=[b_pkv2, b_scr1], writes=[b_ktm])
            for half in range(2):
                psv, bpsv = (ps_mmA, b_mmA) if half == 0 else (ps_mmB, b_mmB)
                for ic in range(8):
                    mm(psv[:, :],
                       xt[:, ic, s * 128:s * 128 + 128],
                       WvT[:, ic * D + half * 512: ic * D + (half + 1) * 512],
                       start=(ic == 0), stop=False,
                       reads=[bxnbf, b_WvT], writes=[bpsv])
                mm(psv[:, :], onesb[0:1, 0:128], bvr[0:1, half * 512:(half + 1) * 512],
                   start=False, stop=True, reads=[b_onesb, b_bvr], writes=[bpsv])
                vop("tensor_copy", v_tm[:, half * 512:(half + 1) * 512], psv[:, :],
                    reads=[bpsv], writes=[b_vtm])
            if "ktm" in dbg and t == 0 and s == 0:
                dma("scalar", dbg["ktm"][:, :], k_tm[:, :], reads=[b_ktm])
                dma("scalar", dbg["vtm"][:, :], v_tm[:, :], reads=[b_vtm])
            for h in range(NH):
                pb, fo = (h % 2) * 64, (h // 2) * 64
                mm(ps_kvh[pb:pb + 64, fo:fo + 64],
                   k_tm[:, h * 64:(h + 1) * 64], v_tm[:, h * 64:(h + 1) * 64],
                   start=(s == 0 and h < 2),
                   stop=(s == NSUB - 1),
                   reads=[b_ktm, b_vtm], writes=[b_pkvh], skip_gc=True)
                mm(ps_ksum[pb:pb + 64, (h // 2):(h // 2) + 1],
                   k_tm[:, h * 64:(h + 1) * 64], onescb[:, 0:1],
                   start=(s == 0 and h < 2), stop=(s == NSUB - 1),
                   reads=[b_ktm, b_onescb], writes=[b_pksum], skip_gc=True)

    def stage_tail(t):
        if "kv" in ablate:
            pass
        elif t == 0:
            vop("tensor_copy", kvacc[:, 0:512], ps_kvh[:, :], reads=[b_pkvh], writes=[b_kvacc])
            vop("tensor_copy", kvacc[:, 512:520], ps_ksum[:, 0:8], reads=[b_pksum], writes=[b_kvacc])
        else:
            vop("tensor_add", kvacc[:, 0:512], kvacc[:, 0:512], ps_kvh[:, :], reads=[b_pkvh, b_kvacc], writes=[b_kvacc])
            vop("tensor_add", kvacc[:, 512:520], kvacc[:, 512:520], ps_ksum[:, 0:8], reads=[b_pksum, b_kvacc], writes=[b_kvacc])
        if "xr" in dbg:
            dma("scalar", dbg["xr"][t], xr_sb[:, (t % 2) * TILE:(t % 2 + 1) * TILE], reads=[b_xr2[t % 2]])
        for s in (range(NSUB) if "gating" not in ablate else ()):
            g = t * NSUB + s
            mm(ps_trB[0:128, 0:NE], xr_sb[:, (t % 2) * TILE + s * 128:(t % 2) * TILE + (s + 1) * 128], wguT[:, :],
               start=True, stop=True, reads=[b_xr2[t % 2], b_wguT], writes=[b_trB])
            if "lg" in dbg:
                vop("tensor_copy", lgbuf[:, g * NE:(g + 1) * NE], ps_trB[0:128, 0:NE], reads=[b_trB], writes=[b_lg])
            vop("reduce_max", gsc[:, 0:1], ps_trB[0:128, 0:NE], AX.X, reads=[b_trB], writes=[b_gsc])
            S.add("vector", lambda h: h.tensor_scalar(gsc[:, 8:16], ps_trB[0:128, 0:NE], gsc[:, 0:1], None, ALU.subtract),
                  reads=[b_trB, b_gsc], writes=[b_gsc])
            act(gsc[:, 16:24], gsc[:, 8:16], AF.Exp, reads=[b_gsc], writes=[b_gsc])
            vop("reduce_sum", gsc[:, 1:2], gsc[:, 16:24], AX.X, reads=[b_gsc], writes=[b_gsc])
            S.add("vector", lambda h, g=g: h.reciprocal(S_all[:, g:g + 1], gsc[:, 1:2]),
                  reads=[b_gsc], writes=[b_S])
            S.add("vector", lambda h: h.tensor_scalar(gsc[:, 24:32], gsc[:, 8:16], 0.0, None, ALU.is_equal),
                  reads=[b_gsc], writes=[b_gsc])
            vop("tensor_mul", gsc[:, 32:40], gsc[:, 24:32], pw2b[:, :], reads=[b_gsc, b_pw2], writes=[b_gsc])
            vop("reduce_max", gsc[:, 2:3], gsc[:, 32:40], AX.X, reads=[b_gsc], writes=[b_gsc])
            S.add("vector", lambda h: h.tensor_scalar(gsc[:, 40:48], gsc[:, 32:40], gsc[:, 2:3], None, ALU.is_equal),
                  reads=[b_gsc], writes=[b_gsc])
            S.add("vector", lambda h, g=g: h.tensor_copy(C_all[:, g * NE:(g + 1) * NE], gsc[:, 40:48]),
                  reads=[b_gsc], writes=[b_C])

    stage_ln(0)
    for t in range(NTILE):
        stage_proj(t)
        if t + 1 < NTILE:
            stage_ln(t + 1)
        stage_tail(t)

    if True:
        S.add("vector", lambda h: h.tensor_scalar(contrib[:, 0:520], kvacc[:, :], bmb[:, 0:1], None, ALU.mult),
          reads=[b_kvacc, b_bmb], writes=[b_contrib])
    S.add("vector", lambda h: h.tensor_scalar(contrib[:, 520:1040], kvacc[:, :], bmb[:, 1:2], None, ALU.mult),
          reads=[b_kvacc, b_bmb], writes=[b_contrib])
    if "kv" in dbg:
        dma("scalar", dbg["kv"][:, :], kvacc[:, :], reads=[b_kvacc])
    dma("gpsimd", cc_in.ap().rearrange("(b p) f -> p b f", p=128),
        contrib.ap().rearrange("p (b f) -> p b f", b=2),
        reads=[b_contrib], writes=[b_ccin])
    cc_op = S.add("gpsimd",
                  lambda h: h.collective_compute("AllReduce", ALU.add,
                                                 replica_groups=[list(range(NCORE))],
                                                 ins=[cc_in.ap().opt()], outs=[cc_out.ap().opt()]),
                  reads=[b_ccin], writes=[b_ccout], cc=True)

    fence1 = S.snapshot()

    a2 = Arena(ARENA_BASE)
    xnr, b_xnr = a2.take("xnr", [128, 8 * 2048], BF16)
    b_xnr4 = [Buf(f"xnr{i}") for i in range(4)]
    xns, _ = a2.take("xns", [128, 2 * 8 * 512], BF16)
    b_xns = [Buf("xns0"), Buf("xns1")]
    w1q, _ = a2.take("w1q", [128, 2 * 8 * 1024], BF16)
    b_w1q = [Buf("w1q0"), Buf("w1q1")]
    w2q, _ = a2.take("w2q", [128, 2 * 8 * 1024], BF16)
    b_w2q = [Buf("w2q0"), Buf("w2q1")]
    b2e, _ = a2.take("b2e", [1, 2 * D], BF16)
    b_b2e = [Buf("b2e0"), Buf("b2e1")]
    h_sb, b_h = a2.take("h_sb", [128, 8 * 512], BF16)

    ps_hA, b_hA = P512[0], bP512[0]
    ps_hB, b_hB = P512[1], bP512[1]
    ps_gA, b_gA = P1024, bP1024
    ps_gB0, b_gB0 = P512[2], bP512[2]
    ps_gB1, b_gB1 = P512[3], bP512[3]

    xnr4 = xnr.ap().rearrange("p (c tt) -> p c tt", c=8)
    xns4 = xns.ap().rearrange("p (a c f) -> p a c f", a=2, c=8)
    w1q4 = w1q.ap().rearrange("p (a c f) -> p a c f", a=2, c=8)
    w2q4 = w2q.ap().rearrange("p (a c f) -> p a c f", a=2, c=8)

    S.add("vector", lambda h: h.memset(g_pre[:, :], 0.0), reads=[b_gpre],
          writes=[b_gpre] + b_gpre_ch, extra_waits=())

    for t in range(4):
        dma("sync" if t == 0 else "scalar", xnr4[:, :, t * TILE:(t + 1) * TILE],
            xn_spill[t].rearrange("(c p) f -> p c f", p=128),
            reads=[b_xnsp], writes=[b_xnr, b_xnr4[t]], extra=fence1)

    b_w1q0h = [Buf("w1q0a"), Buf("w1q0b")]
    b_w2q0h = [Buf("w2q0a"), Buf("w2q0b")]

    def w_load(j):
        e, qtr = divmod(j, 4)
        par = j % 2
        if j == 0:
            for hh in range(2):
                dma("gpsimd", w1q4[:, 0, :, hh * 512:(hh + 1) * 512],
                    w1_in[e][:, hh * 512:(hh + 1) * 512].rearrange("(c p) f -> p c f", p=128),
                    reads=[b_w1q[0]], writes=[b_w1q0h[hh]], extra=fence1)
            for hh in range(2):
                dma("gpsimd", w2q4[:, 0, hh * 4:(hh + 1) * 4, :],
                    w2_in[e][hh * 512:(hh + 1) * 512, :].rearrange("(c p) f -> p c f", p=128),
                    reads=[b_w2q[0]], writes=[b_w2q0h[hh]], extra=fence1)
        else:
            extra_w = [b_w1q0h[0].w, b_w1q0h[1].w] if j == 2 else []
            extra_w2 = [b_w2q0h[0].w, b_w2q0h[1].w] if j == 2 else []
            dma("gpsimd", w1q4[:, par, :, :],
                w1_in[e][:, qtr * 1024:(qtr + 1) * 1024].rearrange("(c p) f -> p c f", p=128),
                reads=[b_w1q[par]] + ([b_w1q0h[0], b_w1q0h[1]] if j == 2 else []),
                writes=[b_w1q[par]] + ([b_w1q0h[0], b_w1q0h[1]] if j == 2 else []), extra=fence1)
            dma("gpsimd", w2q4[:, par, :, :],
                w2_in[e][qtr * 1024:(qtr + 1) * 1024, :].rearrange("(c p) f -> p c f", p=128),
                reads=[b_w2q[par]] + ([b_w2q0h[0], b_w2q0h[1]] if j == 2 else []),
                writes=[b_w2q[par]] + ([b_w2q0h[0], b_w2q0h[1]] if j == 2 else []), extra=fence1)
        if qtr == 0:
            pe_ = e % 2
            dma("gpsimd", b2e[0:1, pe_ * D:(pe_ + 1) * D], b2row_in[0:1, e * D:(e + 1) * D],
                reads=[b_b2e[pe_]], writes=[b_b2e[pe_]], extra=fence1)

    if "ffn" not in ablate:
        w_load(0)
        w_load(1)

    for j in (range(NE * 4) if "ffn" not in ablate else ()):
        e, qtr = divmod(j, 4)
        par = j % 2
        bw1, bw2 = b_w1q[par], b_w2q[par]
        epar = e % 2
        for t in range(NTILE):
            if t < 4:
                rhs_sel = lambda ic, t=t: xnr4[:, ic, t * TILE:(t + 1) * TILE]
                b_rhs = b_xnr4[t]
            else:
                sp = (t - 4) % 2
                rhs_sel = lambda ic, sp=sp: xns4[:, sp, ic, :]
                b_rhs = b_xns[(t - 4) % 2]
            if t == 2:
                for i in range(2):
                    dma("scalar", xns4[:, i, :, :], xn_spill[4 + i].rearrange("(c p) f -> p c f", p=128),
                        reads=[b_xnsp, b_xns[i]], writes=[b_xns[i]], extra=fence1 if j == 0 else ())
            if t == 5:
                dma("scalar", xns4[:, 0, :, :], xn_spill[6].rearrange("(c p) f -> p c f", p=128),
                    reads=[b_xnsp, b_xns[0]], writes=[b_xns[0]])
            if t == 6:
                dma("scalar", xns4[:, 1, :, :], xn_spill[7].rearrange("(c p) f -> p c f", p=128),
                    reads=[b_xnsp, b_xns[1]], writes=[b_xns[1]])

            for hs in range(8):
                psh, bph = (ps_hA, b_hA) if hs % 2 == 0 else (ps_hB, b_hB)
                bw1u = b_w1q0h[hs // 4] if j == 0 else bw1
                for ic in range(8):
                    mm(psh[:, :], w1q4[:, par, ic, hs * 128:(hs + 1) * 128], rhs_sel(ic),
                       start=(ic == 0), stop=(ic == 7),
                       reads=[bw1u, b_rhs], writes=[bph])
                hsg = qtr * 8 + hs
                act(h_sb[:, hs * 512:(hs + 1) * 512], psh[:, :], AF.Gelu,
                    bias=b1t[:, hsg * NE + e: hsg * NE + e + 1],
                    reads=[bph, b_b1t], writes=[b_h])
            for sub in range(NSUB):
                g = t * NSUB + sub
                if sub % 2 == 0:
                    outs = ((ps_gA[:, 0:512], b_gA), (ps_gA[:, 512:1024], b_gA))
                else:
                    outs = ((ps_gB0[:, :], b_gB0), (ps_gB1[:, :], b_gB1))
                for oh in range(2):
                    po, bpo = outs[oh]
                    for hc in range(8):
                        bw2u = b_w2q0h[hc // 4] if j == 0 else bw2
                        mm(po, h_sb[:, hc * 512 + sub * 128: hc * 512 + sub * 128 + 128],
                           w2q4[:, par, hc, oh * 512:(oh + 1) * 512],
                           start=(hc == 0), stop=(hc == 7 and qtr != 0),
                           reads=[b_h, bw2u], writes=[bpo])
                    if qtr == 0:
                        mm(po, onesb[0:1, 0:128], b2e[0:1, epar * D + oh * 512: epar * D + (oh + 1) * 512],
                           start=False, stop=True, reads=[b_onesb, b_b2e[epar]], writes=[bpo])
                if sub % 2 == 0:
                    S.add("vector",
                          lambda h, g=g, e=e: h.scalar_tensor_tensor(
                              g_pre[:, g * D:(g + 1) * D], ps_gA[:, :], C_all[:, g * NE + e: g * NE + e + 1],
                              g_pre[:, g * D:(g + 1) * D], ALU.mult, ALU.add),
                          reads=[b_gA, b_C, b_gpre_ch[g]], writes=[b_gpre_ch[g]])
                else:
                    S.add("vector",
                          lambda h, g=g, e=e: h.scalar_tensor_tensor(
                              g_pre[:, g * D:g * D + 512], ps_gB0[:, :], C_all[:, g * NE + e: g * NE + e + 1],
                              g_pre[:, g * D:g * D + 512], ALU.mult, ALU.add),
                          reads=[b_gB0, b_C, b_gpre_ch[g]], writes=[b_gpre_ch[g]])
                    S.add("vector",
                          lambda h, g=g, e=e: h.scalar_tensor_tensor(
                              g_pre[:, g * D + 512:(g + 1) * D], ps_gB1[:, :], C_all[:, g * NE + e: g * NE + e + 1],
                              g_pre[:, g * D + 512:(g + 1) * D], ALU.mult, ALU.add),
                          reads=[b_gB1, b_C, b_gpre_ch[g]], writes=[b_gpre_ch[g]])
        if j + 2 <= NE * 4 - 1:
            w_load(j + 2)

    if "gpre" in dbg:
        dma("scalar", dbg["gpre"].ap().rearrange("g p f -> p g f"),
            g_pre.ap().rearrange("p (g f) -> p g f", g=NCH), reads=b_gpre_ch)
    if "C" in dbg:
        dma("scalar", dbg["C"][:, :], C_all[:, :], reads=[b_C])
    if "lg" in dbg:
        dma("scalar", dbg["lg"][:, :], lgbuf[:, :], reads=[b_lg])
    if "Sc" in dbg:
        dma("scalar", dbg["Sc"][:, :], S_all[:, :], reads=[b_S])

    fence2 = S.snapshot()

    a3 = Arena(ARENA_BASE)
    WoT, b_WoT = a3.take("WoT", [128, 8 * D], BF16)
    wstage2, b_wstage2 = a3.take("wstage2", [128, D], BF16)
    kvfull, b_kvfull = a3.take("kvfull", [128, 1040], F32)
    kvm, b_kvm = a3.take("kvm", [128, 520], F32)
    bor, b_bor = a3.take("bor", [1, D], BF16)
    kvaug, b_kvaug = a3.take("kvaug", [128, 8 * 66], BF16)
    q_sb, _ = a3.take("q_sb", [128, 2 * 8 * 512], BF16)
    b_qsb = [Buf("qsb0"), Buf("qsb1")]
    xh2, _ = a3.take("xh2", [128, 2 * 2 * D], F32)
    b_xh2 = [Buf("xh2_0"), Buf("xh2_1")]
    aztm, _ = a3.take("aztm", [128, 2 * 1024], BF16)
    b_aztm = [Buf("aztm0"), Buf("aztm1")]
    azf, _ = a3.take("azf", [128, 2 * 8 * 512], BF16)
    b_azf = [Buf("azf0"), Buf("azf1")]
    den_sb, b_den = a3.take("den_sb", [128, 16], F32)
    zs_sb, b_zs = a3.take("zs_sb", [128, 16], F32)
    out_sb, b_out = a3.take("out_sb", [128, 4 * D], F32)
    gs_sb, b_gs = a3.take("gs_sb", [128, D], F32)
    mres, b_mres = a3.take("mres", [128, 512], F32)

    ps_atA, b_atA = P512[0], bP512[0]
    ps_atB, b_atB = P512[1], bP512[1]
    b_tr2 = [bP1024, Buf("ps1024_b1")]

    dma("gpsimd", bor[:, :], bo_row_in[:, :], writes=[b_bor], extra=fence2)
    for oc in range(8):
        dma("gpsimd", wstage2[:, :], W_in["Wo"][oc * 128:(oc + 1) * 128, :],
            reads=[b_wstage2], writes=[b_wstage2], extra=fence2)
        for ic in range(8):
            psb, bps = (ps_atA, b_atA) if ic % 2 == 0 else (ps_atB, b_atB)
            pbf = psb.ap().bitcast(BF16)
            tr(pbf[:, 0:128], wstage2[:, ic * 128:(ic + 1) * 128], idb[:, :],
               reads=[b_wstage2, b_idb], writes=[bps], extra=fence2)
            S.add("scalar",
                  lambda h, c=ic * D + oc * 128, p=pbf: h.copy(WoT[:, c:c + 128], p[:, 0:128]),
                  reads=[bps], writes=[b_WoT])

    dma("gpsimd", kvfull.ap().rearrange("p (b f) -> p b f", b=2),
        cc_out.ap().rearrange("(b p) f -> p b f", p=128),
        reads=[b_ccout], writes=[b_kvfull], extra=[cc_op] + list(fence2))
    if "kvfull" in dbg:
        dma("scalar", dbg["kvfull"][:, :], kvfull[:, :], reads=[b_kvfull])
    S.add("vector", lambda h: h.tensor_scalar(kvm[:, :], kvfull[:, 0:520], bmb[:, 0:1], None, ALU.mult),
          reads=[b_kvfull, b_bmb], writes=[b_kvm])
    S.add("vector", lambda h: h.scalar_tensor_tensor(kvm[:, :], kvfull[:, 520:1040], bmb[:, 1:2], kvm[:, :], ALU.mult, ALU.add),
          reads=[b_kvfull, b_bmb, b_kvm], writes=[b_kvm])
    for h_ in range(NH):
        pb, fo, sl = (h_ % 2) * 64, (h_ // 2) * 64, h_ // 2
        vop("tensor_copy", kvaug[pb:pb + 64, sl * 66:sl * 66 + 64],
            kvm[pb:pb + 64, fo:fo + 64], reads=[b_kvm], writes=[b_kvaug])
        vop("tensor_copy", kvaug[pb:pb + 64, sl * 66 + 64:sl * 66 + 65],
            kvm[pb:pb + 64, 512 + sl:512 + sl + 1], reads=[b_kvm], writes=[b_kvaug])

    q4 = q_sb.ap().rearrange("p (a c f) -> p a c f", a=2, c=8)
    xh2v = xh2.ap().rearrange("p (a u f) -> p a u f", a=2, u=2)

    for t in (range(NTILE) if "attn" not in ablate else ()):
        qpar = t % 2
        azp = t % 2
        if t == 0:
            dma("sync", q4[:, 0, :, :], q_spill[0].rearrange("(c p) f -> p c f", p=128),
                reads=[b_qsp, b_qsb[0]], writes=[b_qsb[0]], extra=fence2)
            for hf in range(2):
                dma("sync", xh2v[:, hf, :, :],
                    x_in[hf * 256:(hf + 1) * 256, :].rearrange("(u p) f -> p u f", p=128),
                    reads=[b_xh2[hf]], writes=[b_xh2[hf]], extra=fence2)

        def p3_att(sub):
            pA, bA = (P512[0], bP512[0]) if sub % 2 == 0 else (P512[2], bP512[2])
            pB, bB = (P512[1], bP512[1]) if sub % 2 == 0 else (P512[3], bP512[3])
            pD0, bD0 = P512[4], bP512[4]
            pD1, bD1 = P512[5], bP512[5]
            for h_ in range(NH):
                qrow = (h_ % 2) * 64
                sl = h_ // 2
                qT = q4[qrow:qrow + 64, qpar, sl, sub * 128:(sub + 1) * 128]
                po, bo_ = (pA, bA) if h_ % 2 == 0 else (pB, bB)
                mm(po[:, sl * 64:sl * 64 + 64], qT,
                   kvaug[qrow:qrow + 64, sl * 66:sl * 66 + 64],
                   start=True, stop=True, reads=[b_qsb[qpar], b_kvaug], writes=[bo_])
                pd, bd = (pD0, bD0) if h_ % 2 == 0 else (pD1, bD1)
                mm(pd[:, sub * 8 + sl // 2 * 0 + sl:sub * 8 + sl + 1] if False else pd[:, sub * 8 + sl:sub * 8 + sl + 1],
                   qT, kvaug[qrow:qrow + 64, sl * 66 + 64:sl * 66 + 65],
                   start=True, stop=True, reads=[b_qsb[qpar], b_kvaug], writes=[bd])

        def p3_ztr(sub):
            pA, bA = (P512[0], bP512[0]) if sub % 2 == 0 else (P512[2], bP512[2])
            pB, bB = (P512[1], bP512[1]) if sub % 2 == 0 else (P512[3], bP512[3])
            pD0, bD0 = P512[4], bP512[4]
            pD1, bD1 = P512[5], bP512[5]
            S.add("vector", lambda h, pD0=pD0, sub=sub: h.tensor_scalar_add(den_sb[:, 0:8], pD0[:, sub * 8:sub * 8 + 8], 1e-6),
                  reads=[bD0], writes=[b_den])
            S.add("vector", lambda h, pD1=pD1, sub=sub: h.tensor_scalar_add(den_sb[:, 8:16], pD1[:, sub * 8:sub * 8 + 8], 1e-6),
                  reads=[bD1], writes=[b_den])
            S.add("vector", lambda h: h.reciprocal(zs_sb[:, :], den_sb[:, :]),
                  reads=[b_den], writes=[b_zs])
            for h_ in range(NH):
                sl = h_ // 2
                po, bo_ = (pA, bA) if h_ % 2 == 0 else (pB, bB)
                src = po[:, sl * 64:sl * 64 + 64]
                zcol = sl + 8 * (h_ % 2)
                dst = aztm[:, (sub % 2) * 1024 + h_ * 64:(sub % 2) * 1024 + (h_ + 1) * 64]
                if h_ % 2 == 0:
                    S.add("vector", lambda h, d=dst, sr=src, zc=zcol: h.tensor_scalar(d, sr, zs_sb[:, zc:zc + 1], None, ALU.mult),
                          reads=[bo_, b_zs], writes=[b_aztm[sub % 2]])
                else:
                    S.add("scalar", lambda h, d=dst, sr=src, zc=zcol: h.activation(d, sr, AF.Copy, scale=zs_sb[:, zc:zc + 1]),
                          reads=[bo_, b_zs], writes=[b_aztm[sub % 2]])
            for ic in range(8):
                trbank = ic % 2
                pt = P1024.ap().bitcast(BF16)[:, trbank * 1024 + 0:trbank * 1024 + 128]
                tr(pt, aztm[:, (sub % 2) * 1024 + ic * 128:(sub % 2) * 1024 + (ic + 1) * 128], idb[:, :],
                   reads=[b_aztm[sub % 2], b_idb], writes=[b_tr2[trbank]])
                col = ic * TILE + sub * 128
                S.add("scalar", lambda h, c=col, p=pt, azp=azp: h.copy(
                      azf[:, azp * 4096 + c: azp * 4096 + c + 128], p),
                      reads=[b_tr2[trbank]], writes=[b_azf[azp]])

        if "p3cb" not in ablate:
            p3_att(0)
            p3_att(1)
            for sub in range(NSUB):
                if "p3z" not in ablate:
                    p3_ztr(sub)
                if sub + 2 < NSUB:
                    p3_att(sub + 2)

        if t + 1 < NTILE:
            dma("sync", q4[:, (t + 1) % 2, :, :], q_spill[t + 1].rearrange("(c p) f -> p c f", p=128),
                reads=[b_qsp, b_qsb[(t + 1) % 2]], writes=[b_qsb[(t + 1) % 2]])
        for sub in (range(NSUB) if "p3wo" not in ablate else ()):
            g = t * NSUB + sub
            par = (t * 2 + sub // 2) % 2
            S.add("scalar",
                  lambda h, g=g: h.activation(gs_sb[:, :], g_pre[:, g * D:(g + 1) * D], AF.Silu,
                                              scale=S_all[:, g:g + 1]),
                  reads=[b_gpre_ch[g], b_S], writes=[b_gs])
            for oh in range(2):
                pw, bpw = P512[sub], bP512[sub]
                for ic in range(8):
                    mm(pw[:, :], azf[:, azp * 4096 + ic * TILE + sub * 128: azp * 4096 + ic * TILE + sub * 128 + 128],
                       WoT[:, ic * D + oh * 512:(ic) * D + (oh + 1) * 512],
                       start=(ic == 0), stop=False, reads=[b_azf[azp], b_WoT], writes=[bpw])
                mm(pw[:, :], onesb[0:1, 0:128], bor[0:1, oh * 512:(oh + 1) * 512],
                   start=False, stop=True, reads=[b_onesb, b_bor], writes=[bpw])
                S.add("vector", lambda h, pw=pw, oh=oh: h.tensor_mul(
                      mres[:, :], pw[:, :], gs_sb[:, oh * 512:(oh + 1) * 512]),
                      reads=[bpw, b_gs], writes=[b_mres])
                S.add("vector",
                      lambda h, sub=sub, par=par, oh=oh: h.tensor_add(
                          out_sb[:, sub * D + oh * 512: sub * D + (oh + 1) * 512],
                          xh2v[:, par, sub % 2, oh * 512:(oh + 1) * 512], mres[:, :]),
                      reads=[b_xh2[par], b_mres], writes=[b_out])
        dma("sync", y_out[t * TILE:(t + 1) * TILE, :].rearrange("(s p) f -> p s f", p=128),
            out_sb.ap().rearrange("p (s f) -> p s f", s=NSUB),
            reads=[b_out], writes=[b_y])
        if t + 1 < NTILE:
            for hf in range(2):
                dma("sync", xh2v[:, hf, :, :],
                    x_in[(t + 1) * TILE + hf * 256: (t + 1) * TILE + (hf + 1) * 256, :].rearrange("(u p) f -> p u f", p=128),
                    reads=[b_xh2[hf]], writes=[b_xh2[hf]])

    S.add("gpsimd", lambda h: h.nop(), extra_waits=S.snapshot())

    guards = S.emit(nc)
    return nc


def kernel(**inputs):
    x = np.ascontiguousarray(np.asarray(inputs["L0_attn_hidden_states"], dtype=np.float32).reshape(2 * 16384, D))
    Wq = np.ascontiguousarray(np.asarray(inputs["L0_attn_Wq"], dtype=np.float32))
    Wk = np.ascontiguousarray(np.asarray(inputs["L0_attn_Wk"], dtype=np.float32))
    Wv = np.ascontiguousarray(np.asarray(inputs["L0_attn_Wv"], dtype=np.float32))
    Wo = np.ascontiguousarray(np.asarray(inputs["L0_attn_Wo"], dtype=np.float32))
    bq = np.asarray(inputs["L0_attn_bq"], dtype=np.float32)
    bk = np.asarray(inputs["L0_attn_bk"], dtype=np.float32)
    bv = np.asarray(inputs["L0_attn_bv"], dtype=np.float32)
    bo = np.asarray(inputs["L0_attn_bo"], dtype=np.float32)
    wg_red = np.ascontiguousarray(np.asarray(inputs["L0_ffn_wg_red"], dtype=np.float32))
    wg = np.ascontiguousarray(np.asarray(inputs["L0_ffn_wg"], dtype=np.float32))
    w1 = np.ascontiguousarray(np.asarray(inputs["L0_ffn_w1"], dtype=np.float32))
    b1 = np.asarray(inputs["L0_ffn_b1"], dtype=np.float32)
    w2 = np.ascontiguousarray(np.asarray(inputs["L0_ffn_w2"], dtype=np.float32))
    b2 = np.asarray(inputs["L0_ffn_b2"], dtype=np.float32)

    bq_t = np.ascontiguousarray(bq.reshape(8, 128).T)
    bo_t = np.ascontiguousarray(bo.reshape(8, 128).T)
    bk_row = np.ascontiguousarray(bk.reshape(1, D))
    bo_row = np.ascontiguousarray(bo.reshape(1, D))
    bv_row = np.ascontiguousarray(bv.reshape(1, D))
    b1_t = np.ascontiguousarray(b1.reshape(NE, 32, 128).transpose(2, 1, 0).reshape(128, 32 * NE))
    b2row = np.ascontiguousarray(b2.reshape(1, NE * D))
    ident = np.eye(128, dtype=np.float32)
    ones_row = np.ones((1, 512), dtype=np.float32)
    ones_col = np.ones((128, 1), dtype=np.float32)
    pw2 = (2.0 ** -np.arange(NE, dtype=np.float32)).reshape(1, NE)

    nc = build_program(debug=tuple(DEBUG), ablate=frozenset(ABLATE))

    in_maps = []
    for c in range(NCORE):
        bmask = np.zeros((1, 2), dtype=np.float32)
        bmask[0, c // 4] = 1.0
        in_maps.append({
            "x": x[c * TOK:(c + 1) * TOK],
            "Wq": Wq, "Wk": Wk, "Wv": Wv, "Wo": Wo,
            "bq_t": bq_t, "bo_t": bo_t, "bk_row": bk_row, "bv_row": bv_row, "bo_row": bo_row,
            "wg_red": wg_red, "wg": wg, "w1": w1, "b1_t": b1_t, "w2": w2, "b2row": b2row,
            "ident": ident, "ones_row": ones_row, "ones_col": ones_col,
            "pw2": pw2, "bmask": bmask,
        })

    res = run_bass_kernel_spmd(nc, in_maps, core_ids=list(range(NCORE)))
    out = np.concatenate([res.results[c]["y"] for c in range(NCORE)], axis=0)
    if DEBUG:
        kernel.debug_results = res.results
    return out.reshape(2, 16384, D)


if __name__ == "__main__":
    build_program()
    print("build ok")


# version 11
# speedup vs baseline: 2.4112x; 1.0178x over previous
import numpy as np
import ml_dtypes
import concourse.bass as bass
import concourse.mybir as mybir
from concourse.bass_utils import run_bass_kernel_spmd

F32 = mybir.dt.float32
BF16 = mybir.dt.bfloat16
F16 = mybir.dt.float16
AF = mybir.ActivationFunctionType
ALU = mybir.AluOpType
AX = mybir.AxisListType

NCORE = 8
TOK = 4096
D = 1024
NH = 16
NE = 8
HID = 4096
TILE = 512
NTILE = TOK // TILE
NSUB = TILE // 128
NCH = TOK // 128

DEBUG = []
ABLATE = set()


class Buf:
    __slots__ = ("name", "w", "readers")

    def __init__(self, name):
        self.name = name
        self.w = None
        self.readers = {}


class Op:
    __slots__ = ("eng", "fn", "waits", "dom", "inc", "flagged", "count_after", "i", "eng_seq", "raw")


RAW_K = 4


class Sched:
    def __init__(self):
        self.plan = []
        self.eng_n = {}

    def add(self, eng, fn, reads=(), writes=(), dma=False, cc=False, extra_waits=()):
        op = Op()
        op.eng = eng
        op.fn = fn
        op.dom = eng + "_dma" if dma else ("cc" if cc else eng)
        op.inc = 16 if dma else 1
        op.flagged = False
        op.count_after = None
        op.i = len(self.plan)
        op.eng_seq = self.eng_n.get(eng, 0)
        self.eng_n[eng] = op.eng_seq + 1
        ws = {}
        rw = {}

        def upd(d, w):
            c = d.get(w.dom)
            if c is None or w.i > c.i:
                d[w.dom] = w

        for b in reads:
            if b.w is not None:
                upd(ws, b.w)
                upd(rw, b.w)
        for b in writes:
            for r in b.readers.values():
                upd(ws, r)
            if b.w is not None:
                upd(ws, b.w)
        for w in extra_waits:
            if w is not None:
                upd(ws, w)
        waits = set(ws.values())
        raw = set(rw.values())
        waits.discard(op)
        raw.discard(op)
        for w in raw:
            waits.add(w)
        op.waits = sorted(waits, key=lambda o: o.i)
        op.raw = raw
        for b in reads:
            b.readers[op.dom] = op
        for b in writes:
            b.w = op
            b.readers = {}
        self.plan.append(op)
        return op

    def snapshot(self):
        last = {}
        for op in self.plan:
            last[op.dom] = op
        return list(last.values())

    def emit(self, nc):
        for op in self.plan:
            if op.dom.endswith("_dma") or op.dom == "cc":
                op.flagged = True
        def need_wait(op, w):
            if w.dom != op.eng:
                return True
            if op.eng == "tensor":
                return False
            return (op.eng_seq - w.eng_seq) <= RAW_K

        for op in self.plan:
            for w in op.waits:
                if need_wait(op, w):
                    w.flagged = True
        counts = {}
        for op in self.plan:
            if op.flagged:
                counts[op.dom] = counts.get(op.dom, 0) + op.inc
                op.count_after = counts[op.dom]
        doms = set(counts.keys())
        sems = {}
        guards = []
        for d in sorted(doms):
            g = nc.semaphore("sem_" + d)
            sems[d] = g.__enter__()
            guards.append(g)

        engs = ("tensor", "vector", "scalar", "gpsimd", "sync")
        per_eng = {e: [op for op in self.plan if op.eng == e] for e in engs}

        def mk_body(eng):
            ops = per_eng[eng]

            def body(h):
                hw = {}
                for op in ops:
                    for w in op.waits:
                        if not need_wait(op, w):
                            continue
                        assert w.flagged, (w.i, op.i)
                        tgt = w.count_after
                        if hw.get(w.dom, 0) >= tgt:
                            continue
                        h.wait_ge(sems[w.dom], tgt)
                        hw[w.dom] = tgt
                    inst = op.fn(h)
                    if op.flagged:
                        inst.then_inc(sems[op.dom], op.inc)
            return body

        with nc.Block() as block:
            for eng in engs:
                if per_eng[eng]:
                    getattr(block, eng)(mk_body(eng))
        return guards


DT_SIZE = {F32: 4, BF16: 2, F16: 2}


def build_program(debug=(), ablate=frozenset()):
    nc = bass.Bass("TRN2", target_bir_lowering=False)
    S = Sched()

    x_in = nc.dram_tensor("x", [TOK, D], F32, kind="ExternalInput")
    W_in = {w: nc.dram_tensor(w, [D, D], F32, kind="ExternalInput") for w in ("Wq", "Wk", "Wv", "Wo")}
    bq_t_in = nc.dram_tensor("bq_t", [128, 8], F32, kind="ExternalInput")
    bo_t_in = nc.dram_tensor("bo_t", [128, 8], F32, kind="ExternalInput")
    bk_row_in = nc.dram_tensor("bk_row", [1, D], F32, kind="ExternalInput")
    bo_row_in = nc.dram_tensor("bo_row", [1, D], F32, kind="ExternalInput")
    bv_row_in = nc.dram_tensor("bv_row", [1, D], F32, kind="ExternalInput")
    wgred_in = nc.dram_tensor("wg_red", [16, D], F32, kind="ExternalInput")
    wg_in = nc.dram_tensor("wg", [NE, 16], F32, kind="ExternalInput")
    w1_in = nc.dram_tensor("w1", [NE, D, HID], F32, kind="ExternalInput")
    b1_t_in = nc.dram_tensor("b1_t", [128, 32 * NE], F32, kind="ExternalInput")
    w2_in = nc.dram_tensor("w2", [NE, HID, D], F32, kind="ExternalInput")
    b2row_in = nc.dram_tensor("b2row", [1, NE * D], F32, kind="ExternalInput")
    ident_in = nc.dram_tensor("ident", [128, 128], F32, kind="ExternalInput")
    ones_in = nc.dram_tensor("ones_row", [1, 512], F32, kind="ExternalInput")
    onesc_in = nc.dram_tensor("ones_col", [128, 1], F32, kind="ExternalInput")
    pw2_in = nc.dram_tensor("pw2", [1, NE], F32, kind="ExternalInput")
    bmask_in = nc.dram_tensor("bmask", [1, 2], F32, kind="ExternalInput")

    y_out = nc.dram_tensor("y", [TOK, D], F32, kind="ExternalOutput")

    xn_spill = nc.dram_tensor("xn_spill", [NTILE, D, TILE], BF16)
    q_spill = nc.dram_tensor("q_spill", [NTILE, D, TILE], BF16)
    cc_in = nc.dram_tensor("cc_in", [2 * 128, 520], F32)
    cc_out = nc.dram_tensor("cc_out", [2 * 128, 520], F32)

    dbg = {}
    if "xn" in debug:
        dbg["xn"] = nc.dram_tensor("dbg_xn", [NTILE, D, TILE], BF16, kind="ExternalOutput")
    if "q" in debug:
        dbg["q"] = nc.dram_tensor("dbg_q", [NTILE, D, TILE], BF16, kind="ExternalOutput")
    if "kv" in debug:
        dbg["kv"] = nc.dram_tensor("dbg_kv", [128, 520], F32, kind="ExternalOutput")
    if "kvfull" in debug:
        dbg["kvfull"] = nc.dram_tensor("dbg_kvfull", [128, 1040], F32, kind="ExternalOutput")
    if "C" in debug:
        dbg["C"] = nc.dram_tensor("dbg_C", [128, NCH * NE], F32, kind="ExternalOutput")
    if "Sc" in debug:
        dbg["Sc"] = nc.dram_tensor("dbg_Sc", [128, NCH], F32, kind="ExternalOutput")
    if "gpre" in debug:
        dbg["gpre"] = nc.dram_tensor("dbg_gpre", [NCH, 128, D], F16, kind="ExternalOutput")
    if "xr" in debug:
        dbg["xr"] = nc.dram_tensor("dbg_xr", [NTILE, 16, TILE], F32, kind="ExternalOutput")
    if "lg" in debug:
        dbg["lg"] = nc.dram_tensor("dbg_lg", [128, NCH * NE], F32, kind="ExternalOutput")
    if "lnraw" in debug:
        dbg["lnraw"] = nc.dram_tensor("dbg_lnraw", [128, D], F32, kind="ExternalOutput")
        dbg["lnst"] = nc.dram_tensor("dbg_lnst", [128, 8], F32, kind="ExternalOutput")
    if "ktm" in debug:
        dbg["ktm"] = nc.dram_tensor("dbg_ktm", [128, D], BF16, kind="ExternalOutput")
        dbg["vtm"] = nc.dram_tensor("dbg_vtm", [128, D], BF16, kind="ExternalOutput")

    SB_BASE = (nc.sbuf_base + 63) & ~63
    SB_TOP = nc.sbuf_top

    class Arena:
        def __init__(self, base):
            self.off = base
            self.hi = base

        def take(self, name, shape, dtype):
            nbytes = int(np.prod(shape[1:])) * DT_SIZE[dtype]
            off = (self.off + 63) & ~63
            t = nc.alloc_sbuf_tensor_at(name, list(shape), dtype, offset=off)
            self.off = off + nbytes
            self.hi = max(self.hi, self.off)
            assert self.off <= SB_TOP, (name, self.off, SB_TOP)
            return t, Buf(name)

    pers = Arena(SB_BASE)
    g_pre, b_gpre = pers.take("g_pre", [128, NCH * D], F16)
    b_gpre_ch = [Buf(f"g_pre{g}") for g in range(NCH)]
    C_all, b_C = pers.take("C_all", [128, NCH * NE], F32)
    S_all, b_S = pers.take("S_all", [128, NCH], F32)
    b1t, b_b1t = pers.take("b1t", [128, 32 * NE], F32)
    bqt, b_bqt = pers.take("bqt", [128, 8], F32)
    bot, b_bot = pers.take("bot", [128, 8], F32)
    idf, b_idf = pers.take("idf", [128, 128], F32)
    idb, b_idb = pers.take("idb", [128, 128], BF16)
    onesf, b_onesf = pers.take("onesf", [1, 512], F32)
    onesb, b_onesb = pers.take("onesb", [1, 512], BF16)
    onescb, b_onescb = pers.take("onescb", [128, 1], BF16)
    pw2b, b_pw2 = pers.take("pw2b", [128, NE], F32)
    bmb, b_bmb = pers.take("bmb", [128, 2], F32)
    ARENA_BASE = (pers.off + 63) & ~63

    b_ccin, b_ccout = Buf("cc_in"), Buf("cc_out")
    b_xnsp, b_qsp = Buf("xn_spill"), Buf("q_spill")
    b_y = Buf("y")

    P512 = []
    bP512 = []
    for i in range(6):
        t = nc.alloc_psum_tensor(f"ps{i}", [128, 512], F32)
        P512.append(t)
        bP512.append(Buf(f"ps{i}"))
    P1024 = nc.alloc_psum_tensor("ps1024", [128, 1024], F32)
    bP1024 = Buf("ps1024")

    def dma(eng, dst, src, reads=(), writes=(), extra=()):
        return S.add(eng, lambda h, d=dst, s=src: h.dma_start(out=d, in_=s),
                     reads=reads, writes=writes, dma=True, extra_waits=extra)

    dma("gpsimd", idb[:, :], ident_in[:, :], writes=[b_idb])
    dma("gpsimd", onesb[:, :], ones_in[:, :], writes=[b_onesb])
    dma("gpsimd", onescb[:, :], onesc_in[:, :], writes=[b_onescb])
    dma("sync", b1t[:, :], b1_t_in[:, :], writes=[b_b1t])
    dma("sync", bqt[:, :], bq_t_in[:, :], writes=[b_bqt])
    dma("sync", bot[:, :], bo_t_in[:, :], writes=[b_bot])
    dma("sync", idf[:, :], ident_in[:, :], writes=[b_idf])
    dma("sync", onesf[:, :], ones_in[:, :], writes=[b_onesf])
    dma("sync", pw2b[:, :], pw2_in.ap().broadcast_to([128, NE]), writes=[b_pw2])
    dma("sync", bmb[:, :], bmask_in.ap().broadcast_to([128, 2]), writes=[b_bmb])

    def mm(out, lhsT, rhs, start, stop, reads, writes, skip_gc=True, extra=()):
        return S.add("tensor",
                     lambda h, o=out, l=lhsT, r=rhs, st=start, sp=stop, sg=skip_gc:
                     h.matmul(o, l, r, start=st, stop=sp, skip_group_check=sg),
                     reads=reads, writes=writes, extra_waits=extra)

    def tr(out, in_, ident, reads, writes, extra=()):
        return S.add("tensor", lambda h, o=out, i=in_, d=ident: h.transpose(o, i, d),
                     reads=reads, writes=writes, extra_waits=extra)

    def act(out, in_, func, bias=0.0, scale=1.0, reads=(), writes=(), extra=()):
        return S.add("scalar",
                     lambda h, o=out, i=in_, f=func, b=bias, sc=scale: h.activation(o, i, f, bias=b, scale=sc),
                     reads=reads, writes=writes, extra_waits=extra)

    def vop(name, *aps, reads=(), writes=(), extra=(), eng="vector"):
        def fn(h, n=name, a=aps):
            return getattr(h, n)(*a)
        return S.add(eng, fn, reads=reads, writes=writes, extra_waits=extra)

    a1 = Arena(ARENA_BASE)
    WqT, b_WqT = a1.take("WqT", [128, 8 * D], BF16)
    WkT, b_WkT = a1.take("WkT", [128, 8 * D], BF16)
    WvT, b_WvT = a1.take("WvT", [128, 8 * D], BF16)
    wstage, b_wstage = a1.take("wstage", [128, D], BF16)
    wgredT, b_wgredT = a1.take("wgredT", [128, 8 * 16], F32)
    wgr_st, b_wgr_st = a1.take("wgr_st", [16, D], F32)
    wgu, b_wgu = a1.take("wgu", [NE, 16], F32)
    wguT, b_wguT = a1.take("wguT", [16, NE], F32)
    wgtmp, b_wgtmp = a1.take("wgtmp", [NE, 20], F32)
    bkr, b_bkr = a1.take("bkr", [1, D], BF16)
    bvr, b_bvr = a1.take("bvr", [1, D], BF16)
    xh, _ = a1.take("xh", [128, 2 * D], F32)
    b_xh = [Buf("xh0"), Buf("xh1")]
    ctr2, _ = a1.take("ctr2", [128, 2 * D], F32)
    b_ctr2 = [Buf("ctr0"), Buf("ctr1")]
    scr1, b_scr1 = a1.take("scr1", [128, D], F32)
    scr22, _ = a1.take("scr22", [128, 2 * D], F32)
    b_scr22 = [Buf("xnl0"), Buf("xnl1")]
    lnst2, _ = a1.take("lnst2", [128, 16], F32)
    b_lnst2 = [Buf("lnst0"), Buf("lnst1")]
    xnf_bf2, _ = a1.take("xnf_bf2", [128, 2 * 8 * TILE], BF16)
    b_xnbf2 = [Buf("xnbf0"), Buf("xnbf1")]
    xnf_f32, b_xnf32 = a1.take("xnf_f32", [128, 8 * 256], F32)
    q_tile, b_q = a1.take("q_tile", [128, 8 * TILE], BF16)
    k_tm, b_ktm = a1.take("k_tm", [128, D], BF16)
    v_tm, b_vtm = a1.take("v_tm", [128, D], BF16)
    xr_sb, _ = a1.take("xr_sb", [16, 2 * TILE], F32)
    b_xr2 = [Buf("xr0"), Buf("xr1")]
    gsc, b_gsc = a1.take("gsc", [128, 8 * 8], F32)
    qsc, b_qsc = a1.take("qsc", [128, D], F32)
    if "lg" in debug:
        lgbuf, b_lg = a1.take("lgbuf", [128, NCH * NE], F32)
    kvacc, b_kvacc = a1.take("kvacc", [128, 520], F32)
    contrib, b_contrib = a1.take("contrib", [128, 2 * 520], F32)

    ps_kvh, b_pkvh = P512[0], bP512[0]
    ps_ksum, b_pksum = P512[1], bP512[1]
    ps_trA, b_trA = P512[2], bP512[2]
    ps_trB, b_trB = P512[3], bP512[3]
    ps_mmA, b_mmA = P512[4], bP512[4]
    ps_mmB, b_mmB = P512[5], bP512[5]
    ps_kv2, b_pkv2 = P1024, bP1024

    dma("gpsimd", bkr[:, :], bk_row_in[:, :], writes=[b_bkr])
    dma("gpsimd", bvr[:, :], bv_row_in[:, :], writes=[b_bvr])

    def prep_wT(nm, WT, b_WT):
        for oc in range(8):
            dma("gpsimd", wstage[:, :], W_in[nm][oc * 128:(oc + 1) * 128, :],
                reads=(), writes=[b_wstage])
            for ic in range(8):
                psb, bps = (ps_trA, b_trA) if ic % 2 == 0 else (ps_trB, b_trB)
                pbf = psb.ap().bitcast(BF16)
                tr(pbf[:, 0:128], wstage[:, ic * 128:(ic + 1) * 128], idb[:, :],
                   reads=[b_wstage, b_idb], writes=[bps])
                S.add("scalar",
                      lambda h, w=WT, c=ic * D + oc * 128, p=pbf: h.copy(w[:, c:c + 128], p[:, 0:128]),
                      reads=[bps], writes=[b_WT])

    prep_wT("Wq", WqT, b_WqT)
    prep_wT("Wk", WkT, b_WkT)
    prep_wT("Wv", WvT, b_WvT)

    dma("sync", wgr_st[:, :], wgred_in[:, :], writes=[b_wgr_st])
    for ic in range(8):
        psb, bps = (ps_trA, b_trA) if ic % 2 == 0 else (ps_trB, b_trB)
        tr(psb[0:128, 0:16], wgr_st[0:16, ic * 128:(ic + 1) * 128], idf[0:16, 0:16],
           reads=[b_wgr_st, b_idf], writes=[bps])
        vop("tensor_copy", wgredT[:, ic * 16:(ic + 1) * 16], psb[0:128, 0:16],
            reads=[bps], writes=[b_wgredT])

    dma("sync", wgu[:, :], wg_in[:, :], writes=[b_wgu])
    act(wgtmp[:, 0:16], wgu[:, :], AF.Square, reads=[b_wgu], writes=[b_wgtmp])
    vop("reduce_sum", wgtmp[:, 16:17], wgtmp[:, 0:16], AX.X, reads=[b_wgtmp], writes=[b_wgtmp])
    act(wgtmp[:, 17:18], wgtmp[:, 16:17], AF.Sqrt, reads=[b_wgtmp], writes=[b_wgtmp])
    vop("tensor_scalar_max", wgtmp[:, 17:18], wgtmp[:, 17:18], 1e-4, reads=[b_wgtmp], writes=[b_wgtmp])
    vop("reciprocal", wgtmp[:, 18:19], wgtmp[:, 17:18], reads=[b_wgtmp], writes=[b_wgtmp])
    S.add("vector", lambda h: h.tensor_scalar(wgu[:, :], wgu[:, :], wgtmp[:, 18:19], None, ALU.mult),
          reads=[b_wgtmp], writes=[b_wgu])
    tr(ps_trA[0:16, 0:NE], wgu[:, :], idf[0:NE, 0:NE], reads=[b_wgu, b_idf], writes=[b_trA])
    vop("tensor_copy", wguT[:, :], ps_trA[0:16, 0:NE], reads=[b_trA], writes=[b_wguT])


    def xnf(t):
        return xnf_bf2.ap().rearrange("p (a c f) -> p a c f", a=2, c=8)[:, t % 2, :, :]

    def stage_ln(t):
        bxnbf = b_xnbf2[t % 2]
        if t == 0:
            for k0 in range(2):
                dma("sync", xh[:, (k0 % 2) * D:(k0 % 2 + 1) * D],
                    x_in[k0 * 128:(k0 + 1) * 128, :],
                    reads=[b_xh[k0 % 2]], writes=[b_xh[k0 % 2]])
        for s in range(NSUB):
            kg = t * NSUB + s
            par = kg % 2
            pp = s % 2
            xsub = xh[:, par * D:(par + 1) * D]
            bx = b_xh[par]
            ctr = ctr2[:, pp * D:(pp + 1) * D]
            b_ctr = b_ctr2[pp]
            scr2 = scr22[:, pp * D:(pp + 1) * D]
            b_scr2 = b_scr22[pp]
            lnst = lnst2[:, pp * 8:(pp + 1) * 8]
            b_lnst = b_lnst2[pp]
            vop("reduce_sum", lnst[:, 0:1], xsub, AX.X, reads=[bx], writes=[b_lnst])
            vop("tensor_scalar_mul", lnst[:, 1:2], lnst[:, 0:1], 1.0 / D, reads=[b_lnst], writes=[b_lnst])
            S.add("vector", lambda h, xs=xsub, ctr=ctr, lnst=lnst: h.tensor_scalar(ctr[:, :], xs, lnst[:, 1:2], None, ALU.subtract),
                  reads=[bx, b_lnst], writes=[b_ctr])
            act(scr2[:, :], ctr[:, :], AF.Square, reads=[b_ctr], writes=[b_scr2])
            vop("reduce_sum", lnst[:, 2:3], scr2[:, :], AX.X, reads=[b_scr2], writes=[b_lnst])
            act(lnst[:, 3:4], lnst[:, 2:3], AF.Ln, bias=np.float32(1e-5), scale=1.0 / D, reads=[b_lnst], writes=[b_lnst])
            act(lnst[:, 4:5], lnst[:, 3:4], AF.Exp, scale=-0.5, reads=[b_lnst], writes=[b_lnst])
            S.add("vector", lambda h, ctr=ctr, lnst=lnst, scr2=scr2: h.tensor_scalar(scr2[:, :], ctr[:, :], lnst[:, 4:5], None, ALU.mult),
                  reads=[b_ctr, b_lnst], writes=[b_scr2])
            for ic in range(8):
                psb, bps = (ps_trA, b_trA) if ic % 2 == 0 else (ps_trB, b_trB)
                slot = (ic // 2) % 2
                tr(psb[:, slot * 128:slot * 128 + 128], scr2[:, ic * 128:(ic + 1) * 128], idf[:, :],
                   reads=[b_scr2, b_idf], writes=[bps])
                colf = ic * 256 + pp * 128
                vop("tensor_copy", xnf_f32[:, colf:colf + 128], psb[:, slot * 128:slot * 128 + 128],
                    reads=[bps], writes=[b_xnf32])
                S.add("scalar", lambda h, t=t, c=ic * TILE + s * 128, cf=colf: h.copy(
                      xnf(t)[:, c // TILE, c % TILE: c % TILE + 128], xnf_f32[:, cf:cf + 128]),
                      reads=[b_xnf32], writes=[bxnbf])
            if s % 2 == 1 and "gating" not in ablate:
                hf2 = s // 2
                for ic in range(8):
                    mm(ps_mmA[0:16, hf2 * 256:hf2 * 256 + 256], wgredT[:, ic * 16:(ic + 1) * 16],
                       xnf_f32[:, ic * 256:(ic + 1) * 256],
                       start=(ic == 0), stop=(ic == 7), reads=[b_wgredT, b_xnf32], writes=[b_mmA])
                vop("tensor_copy", xr_sb[:, (t % 2) * TILE + hf2 * 256:(t % 2) * TILE + hf2 * 256 + 256],
                    ps_mmA[0:16, hf2 * 256:hf2 * 256 + 256],
                    reads=[b_mmA], writes=[b_xr2[t % 2]])
            if kg + 2 < NTILE * NSUB:
                k2 = kg + 2
                dma("sync", xh[:, (k2 % 2) * D:(k2 % 2 + 1) * D],
                    x_in[k2 * 128:(k2 + 1) * 128, :],
                    reads=[b_xh[k2 % 2]], writes=[b_xh[k2 % 2]])
        dma("scalar", xn_spill[t].rearrange("(c p) f -> p c f", p=128),
            xnf(t), reads=[bxnbf], writes=[b_xnsp])
        if "xn" in dbg:
            dma("scalar", dbg["xn"][t].rearrange("(c p) f -> p c f", p=128), xnf(t), reads=[bxnbf])

    def stage_proj(t):
        bxnbf = b_xnbf2[t % 2]
        xt = xnf(t)
        for oc in (range(8) if "q" not in ablate else ()):
            psq, bpsq = (ps_mmA, b_mmA) if oc % 2 == 0 else (ps_mmB, b_mmB)
            for ic in range(8):
                mm(psq[:, :], WqT[:, ic * D + oc * 128: ic * D + oc * 128 + 128],
                   xt[:, ic, :],
                   start=(ic == 0), stop=(ic == 7),
                   reads=[b_WqT, bxnbf], writes=[bpsq])
            act(scr1[:, 0:TILE], psq[:, :], AF.Relu, bias=bqt[:, oc:oc + 1],
                reads=[bpsq, b_bqt], writes=[b_scr1])
            act(qsc[:, 0:TILE], psq[:, :], AF.Exp, bias=bqt[:, oc:oc + 1],
                reads=[bpsq, b_bqt], writes=[b_qsc])
            S.add("vector",
                  lambda h, oc=oc: h.scalar_tensor_tensor(
                      q_tile[:, oc * TILE:(oc + 1) * TILE], qsc[:, 0:TILE], 1.0, scr1[:, 0:TILE],
                      ALU.min, ALU.add),
                  reads=[b_qsc, b_scr1], writes=[b_q])
        dma("scalar", q_spill[t].rearrange("(c p) f -> p c f", p=128),
            q_tile.ap().rearrange("p (c f) -> p c f", c=8),
            reads=[b_q], writes=[b_qsp])
        if "q" in dbg:
            dma("scalar", dbg["q"][t].rearrange("(c p) f -> p c f", p=128),
                q_tile.ap().rearrange("p (c f) -> p c f", c=8), reads=[b_q])

        for s in (range(NSUB) if "kv" not in ablate else ()):
            for half in range(2):
                for ic in range(8):
                    mm(ps_kv2[:, half * 512:(half + 1) * 512],
                       xt[:, ic, s * 128:s * 128 + 128],
                       WkT[:, ic * D + half * 512: ic * D + (half + 1) * 512],
                       start=(ic == 0), stop=False,
                       reads=[bxnbf, b_WkT], writes=[b_pkv2])
                mm(ps_kv2[:, half * 512:(half + 1) * 512],
                   onesb[0:1, 0:128], bkr[0:1, half * 512:(half + 1) * 512],
                   start=False, stop=True,
                   reads=[b_onesb, b_bkr], writes=[b_pkv2])
            act(scr1[:, :], ps_kv2[:, :], AF.Relu, reads=[b_pkv2], writes=[b_scr1])
            act(qsc[:, :], ps_kv2[:, :], AF.Exp, reads=[b_pkv2], writes=[b_qsc])
            S.add("vector", lambda h: h.scalar_tensor_tensor(
                  k_tm[:, :], qsc[:, :], 1.0, scr1[:, :], ALU.min, ALU.add),
                  reads=[b_qsc, b_scr1], writes=[b_ktm])
            for half in range(2):
                psv, bpsv = (ps_mmA, b_mmA) if half == 0 else (ps_mmB, b_mmB)
                for ic in range(8):
                    mm(psv[:, :],
                       xt[:, ic, s * 128:s * 128 + 128],
                       WvT[:, ic * D + half * 512: ic * D + (half + 1) * 512],
                       start=(ic == 0), stop=False,
                       reads=[bxnbf, b_WvT], writes=[bpsv])
                mm(psv[:, :], onesb[0:1, 0:128], bvr[0:1, half * 512:(half + 1) * 512],
                   start=False, stop=True, reads=[b_onesb, b_bvr], writes=[bpsv])
                vop("tensor_copy", v_tm[:, half * 512:(half + 1) * 512], psv[:, :],
                    reads=[bpsv], writes=[b_vtm])
            if "ktm" in dbg and t == 0 and s == 0:
                dma("scalar", dbg["ktm"][:, :], k_tm[:, :], reads=[b_ktm])
                dma("scalar", dbg["vtm"][:, :], v_tm[:, :], reads=[b_vtm])
            for h in range(NH):
                pb, fo = (h % 2) * 64, (h // 2) * 64
                mm(ps_kvh[pb:pb + 64, fo:fo + 64],
                   k_tm[:, h * 64:(h + 1) * 64], v_tm[:, h * 64:(h + 1) * 64],
                   start=(s == 0 and h < 2),
                   stop=(s == NSUB - 1),
                   reads=[b_ktm, b_vtm], writes=[b_pkvh], skip_gc=True)
                mm(ps_ksum[pb:pb + 64, (h // 2):(h // 2) + 1],
                   k_tm[:, h * 64:(h + 1) * 64], onescb[:, 0:1],
                   start=(s == 0 and h < 2), stop=(s == NSUB - 1),
                   reads=[b_ktm, b_onescb], writes=[b_pksum], skip_gc=True)

    def stage_tail(t):
        if "kv" in ablate:
            pass
        elif t == 0:
            vop("tensor_copy", kvacc[:, 0:512], ps_kvh[:, :], reads=[b_pkvh], writes=[b_kvacc])
            vop("tensor_copy", kvacc[:, 512:520], ps_ksum[:, 0:8], reads=[b_pksum], writes=[b_kvacc])
        else:
            vop("tensor_add", kvacc[:, 0:512], kvacc[:, 0:512], ps_kvh[:, :], reads=[b_pkvh, b_kvacc], writes=[b_kvacc])
            vop("tensor_add", kvacc[:, 512:520], kvacc[:, 512:520], ps_ksum[:, 0:8], reads=[b_pksum, b_kvacc], writes=[b_kvacc])
        if "xr" in dbg:
            dma("scalar", dbg["xr"][t], xr_sb[:, (t % 2) * TILE:(t % 2 + 1) * TILE], reads=[b_xr2[t % 2]])
        for s in (range(NSUB) if "gating" not in ablate else ()):
            g = t * NSUB + s
            mm(ps_trB[0:128, 0:NE], xr_sb[:, (t % 2) * TILE + s * 128:(t % 2) * TILE + (s + 1) * 128], wguT[:, :],
               start=True, stop=True, reads=[b_xr2[t % 2], b_wguT], writes=[b_trB])
            if "lg" in dbg:
                vop("tensor_copy", lgbuf[:, g * NE:(g + 1) * NE], ps_trB[0:128, 0:NE], reads=[b_trB], writes=[b_lg])
            vop("reduce_max", gsc[:, 0:1], ps_trB[0:128, 0:NE], AX.X, reads=[b_trB], writes=[b_gsc])
            S.add("vector", lambda h: h.tensor_scalar(gsc[:, 8:16], ps_trB[0:128, 0:NE], gsc[:, 0:1], None, ALU.subtract),
                  reads=[b_trB, b_gsc], writes=[b_gsc])
            act(gsc[:, 16:24], gsc[:, 8:16], AF.Exp, reads=[b_gsc], writes=[b_gsc])
            vop("reduce_sum", gsc[:, 1:2], gsc[:, 16:24], AX.X, reads=[b_gsc], writes=[b_gsc])
            S.add("vector", lambda h, g=g: h.reciprocal(S_all[:, g:g + 1], gsc[:, 1:2]),
                  reads=[b_gsc], writes=[b_S])
            S.add("vector", lambda h: h.tensor_scalar(gsc[:, 24:32], gsc[:, 8:16], 0.0, None, ALU.is_equal),
                  reads=[b_gsc], writes=[b_gsc])
            vop("tensor_mul", gsc[:, 32:40], gsc[:, 24:32], pw2b[:, :], reads=[b_gsc, b_pw2], writes=[b_gsc])
            vop("reduce_max", gsc[:, 2:3], gsc[:, 32:40], AX.X, reads=[b_gsc], writes=[b_gsc])
            S.add("vector", lambda h: h.tensor_scalar(gsc[:, 40:48], gsc[:, 32:40], gsc[:, 2:3], None, ALU.is_equal),
                  reads=[b_gsc], writes=[b_gsc])
            S.add("vector", lambda h, g=g: h.tensor_copy(C_all[:, g * NE:(g + 1) * NE], gsc[:, 40:48]),
                  reads=[b_gsc], writes=[b_C])

    stage_ln(0)
    for t in range(NTILE):
        stage_proj(t)
        if t + 1 < NTILE:
            stage_ln(t + 1)
        stage_tail(t)

    if True:
        S.add("vector", lambda h: h.tensor_scalar(contrib[:, 0:520], kvacc[:, :], bmb[:, 0:1], None, ALU.mult),
          reads=[b_kvacc, b_bmb], writes=[b_contrib])
    S.add("vector", lambda h: h.tensor_scalar(contrib[:, 520:1040], kvacc[:, :], bmb[:, 1:2], None, ALU.mult),
          reads=[b_kvacc, b_bmb], writes=[b_contrib])
    if "kv" in dbg:
        dma("scalar", dbg["kv"][:, :], kvacc[:, :], reads=[b_kvacc])
    dma("gpsimd", cc_in.ap().rearrange("(b p) f -> p b f", p=128),
        contrib.ap().rearrange("p (b f) -> p b f", b=2),
        reads=[b_contrib], writes=[b_ccin])
    fence1 = S.snapshot()
    cc_op = S.add("gpsimd",
                  lambda h: h.collective_compute("AllReduce", ALU.add,
                                                 replica_groups=[list(range(NCORE))],
                                                 ins=[cc_in.ap().opt()], outs=[cc_out.ap().opt()]),
                  reads=[b_ccin], writes=[b_ccout], cc=True)

    a2 = Arena(ARENA_BASE)
    xnr, b_xnr = a2.take("xnr", [128, 8 * 2048], BF16)
    b_xnr4 = [Buf(f"xnr{i}") for i in range(4)]
    xns, _ = a2.take("xns", [128, 2 * 8 * 512], BF16)
    b_xns = [Buf("xns0"), Buf("xns1")]
    w1q, _ = a2.take("w1q", [128, 2 * 8 * 1024], BF16)
    b_w1q = [Buf("w1q0"), Buf("w1q1")]
    w2q, _ = a2.take("w2q", [128, 2 * 8 * 1024], BF16)
    b_w2q = [Buf("w2q0"), Buf("w2q1")]
    b2e, _ = a2.take("b2e", [1, 2 * D], BF16)
    b_b2e = [Buf("b2e0"), Buf("b2e1")]
    h_sb, b_h = a2.take("h_sb", [128, 8 * 512], BF16)

    ps_hA, b_hA = P512[0], bP512[0]
    ps_hB, b_hB = P512[1], bP512[1]
    ps_gA, b_gA = P1024, bP1024
    ps_gB0, b_gB0 = P512[2], bP512[2]
    ps_gB1, b_gB1 = P512[3], bP512[3]

    xnr4 = xnr.ap().rearrange("p (c tt) -> p c tt", c=8)
    xns4 = xns.ap().rearrange("p (a c f) -> p a c f", a=2, c=8)
    w1q4 = w1q.ap().rearrange("p (a c f) -> p a c f", a=2, c=8)
    w2q4 = w2q.ap().rearrange("p (a c f) -> p a c f", a=2, c=8)

    S.add("vector", lambda h: h.memset(g_pre[:, :], 0.0), reads=[b_gpre],
          writes=[b_gpre] + b_gpre_ch, extra_waits=())

    for t in range(4):
        dma("sync" if t == 0 else "scalar", xnr4[:, :, t * TILE:(t + 1) * TILE],
            xn_spill[t].rearrange("(c p) f -> p c f", p=128),
            reads=[b_xnsp], writes=[b_xnr, b_xnr4[t]], extra=fence1)

    b_w1q0h = [Buf("w1q0a"), Buf("w1q0b")]
    b_w2q0h = [Buf("w2q0a"), Buf("w2q0b")]

    def w_load(j):
        e, qtr = divmod(j, 4)
        par = j % 2
        if j == 0:
            for hh in range(2):
                dma("gpsimd", w1q4[:, 0, :, hh * 512:(hh + 1) * 512],
                    w1_in[e][:, hh * 512:(hh + 1) * 512].rearrange("(c p) f -> p c f", p=128),
                    reads=[b_w1q[0]], writes=[b_w1q0h[hh]], extra=fence1)
            for hh in range(2):
                dma("gpsimd", w2q4[:, 0, hh * 4:(hh + 1) * 4, :],
                    w2_in[e][hh * 512:(hh + 1) * 512, :].rearrange("(c p) f -> p c f", p=128),
                    reads=[b_w2q[0]], writes=[b_w2q0h[hh]], extra=fence1)
        else:
            extra_w = [b_w1q0h[0].w, b_w1q0h[1].w] if j == 2 else []
            extra_w2 = [b_w2q0h[0].w, b_w2q0h[1].w] if j == 2 else []
            dma("gpsimd", w1q4[:, par, :, :],
                w1_in[e][:, qtr * 1024:(qtr + 1) * 1024].rearrange("(c p) f -> p c f", p=128),
                reads=[b_w1q[par]] + ([b_w1q0h[0], b_w1q0h[1]] if j == 2 else []),
                writes=[b_w1q[par]] + ([b_w1q0h[0], b_w1q0h[1]] if j == 2 else []), extra=fence1)
            dma("gpsimd", w2q4[:, par, :, :],
                w2_in[e][qtr * 1024:(qtr + 1) * 1024, :].rearrange("(c p) f -> p c f", p=128),
                reads=[b_w2q[par]] + ([b_w2q0h[0], b_w2q0h[1]] if j == 2 else []),
                writes=[b_w2q[par]] + ([b_w2q0h[0], b_w2q0h[1]] if j == 2 else []), extra=fence1)
        if qtr == 0:
            pe_ = e % 2
            dma("gpsimd", b2e[0:1, pe_ * D:(pe_ + 1) * D], b2row_in[0:1, e * D:(e + 1) * D],
                reads=[b_b2e[pe_]], writes=[b_b2e[pe_]], extra=fence1)

    if "ffn" not in ablate:
        w_load(0)
        w_load(1)

    for j in (range(NE * 4) if "ffn" not in ablate else ()):
        e, qtr = divmod(j, 4)
        par = j % 2
        bw1, bw2 = b_w1q[par], b_w2q[par]
        epar = e % 2
        for t in range(NTILE):
            if t < 4:
                rhs_sel = lambda ic, t=t: xnr4[:, ic, t * TILE:(t + 1) * TILE]
                b_rhs = b_xnr4[t]
            else:
                sp = (t - 4) % 2
                rhs_sel = lambda ic, sp=sp: xns4[:, sp, ic, :]
                b_rhs = b_xns[(t - 4) % 2]
            if t == 2:
                for i in range(2):
                    dma("scalar", xns4[:, i, :, :], xn_spill[4 + i].rearrange("(c p) f -> p c f", p=128),
                        reads=[b_xnsp, b_xns[i]], writes=[b_xns[i]], extra=fence1 if j == 0 else ())
            if t == 5:
                dma("scalar", xns4[:, 0, :, :], xn_spill[6].rearrange("(c p) f -> p c f", p=128),
                    reads=[b_xnsp, b_xns[0]], writes=[b_xns[0]])
            if t == 6:
                dma("scalar", xns4[:, 1, :, :], xn_spill[7].rearrange("(c p) f -> p c f", p=128),
                    reads=[b_xnsp, b_xns[1]], writes=[b_xns[1]])

            for hs in range(8):
                psh, bph = (ps_hA, b_hA) if hs % 2 == 0 else (ps_hB, b_hB)
                bw1u = b_w1q0h[hs // 4] if j == 0 else bw1
                for ic in range(8):
                    mm(psh[:, :], w1q4[:, par, ic, hs * 128:(hs + 1) * 128], rhs_sel(ic),
                       start=(ic == 0), stop=(ic == 7),
                       reads=[bw1u, b_rhs], writes=[bph])
                hsg = qtr * 8 + hs
                act(h_sb[:, hs * 512:(hs + 1) * 512], psh[:, :], AF.Gelu,
                    bias=b1t[:, hsg * NE + e: hsg * NE + e + 1],
                    reads=[bph, b_b1t], writes=[b_h])
            for sub in range(NSUB):
                g = t * NSUB + sub
                if sub % 2 == 0:
                    outs = ((ps_gA[:, 0:512], b_gA), (ps_gA[:, 512:1024], b_gA))
                else:
                    outs = ((ps_gB0[:, :], b_gB0), (ps_gB1[:, :], b_gB1))
                for oh in range(2):
                    po, bpo = outs[oh]
                    for hc in range(8):
                        bw2u = b_w2q0h[hc // 4] if j == 0 else bw2
                        mm(po, h_sb[:, hc * 512 + sub * 128: hc * 512 + sub * 128 + 128],
                           w2q4[:, par, hc, oh * 512:(oh + 1) * 512],
                           start=(hc == 0), stop=(hc == 7 and qtr != 0),
                           reads=[b_h, bw2u], writes=[bpo])
                    if qtr == 0:
                        mm(po, onesb[0:1, 0:128], b2e[0:1, epar * D + oh * 512: epar * D + (oh + 1) * 512],
                           start=False, stop=True, reads=[b_onesb, b_b2e[epar]], writes=[bpo])
                if sub % 2 == 0:
                    S.add("vector",
                          lambda h, g=g, e=e: h.scalar_tensor_tensor(
                              g_pre[:, g * D:(g + 1) * D], ps_gA[:, :], C_all[:, g * NE + e: g * NE + e + 1],
                              g_pre[:, g * D:(g + 1) * D], ALU.mult, ALU.add),
                          reads=[b_gA, b_C, b_gpre_ch[g]], writes=[b_gpre_ch[g]])
                else:
                    S.add("vector",
                          lambda h, g=g, e=e: h.scalar_tensor_tensor(
                              g_pre[:, g * D:g * D + 512], ps_gB0[:, :], C_all[:, g * NE + e: g * NE + e + 1],
                              g_pre[:, g * D:g * D + 512], ALU.mult, ALU.add),
                          reads=[b_gB0, b_C, b_gpre_ch[g]], writes=[b_gpre_ch[g]])
                    S.add("vector",
                          lambda h, g=g, e=e: h.scalar_tensor_tensor(
                              g_pre[:, g * D + 512:(g + 1) * D], ps_gB1[:, :], C_all[:, g * NE + e: g * NE + e + 1],
                              g_pre[:, g * D + 512:(g + 1) * D], ALU.mult, ALU.add),
                          reads=[b_gB1, b_C, b_gpre_ch[g]], writes=[b_gpre_ch[g]])
        if j + 2 <= NE * 4 - 1:
            w_load(j + 2)

    if "gpre" in dbg:
        dma("scalar", dbg["gpre"].ap().rearrange("g p f -> p g f"),
            g_pre.ap().rearrange("p (g f) -> p g f", g=NCH), reads=b_gpre_ch)
    if "C" in dbg:
        dma("scalar", dbg["C"][:, :], C_all[:, :], reads=[b_C])
    if "lg" in dbg:
        dma("scalar", dbg["lg"][:, :], lgbuf[:, :], reads=[b_lg])
    if "Sc" in dbg:
        dma("scalar", dbg["Sc"][:, :], S_all[:, :], reads=[b_S])

    fence2 = S.snapshot()

    a3 = Arena(ARENA_BASE)
    WoT, b_WoT = a3.take("WoT", [128, 8 * D], BF16)
    wstage2, b_wstage2 = a3.take("wstage2", [128, D], BF16)
    kvfull, b_kvfull = a3.take("kvfull", [128, 1040], F32)
    kvm, b_kvm = a3.take("kvm", [128, 520], F32)
    bor, b_bor = a3.take("bor", [1, D], BF16)
    kvaug, b_kvaug = a3.take("kvaug", [128, 8 * 66], BF16)
    q_sb, _ = a3.take("q_sb", [128, 2 * 8 * 512], BF16)
    b_qsb = [Buf("qsb0"), Buf("qsb1")]
    xh2, _ = a3.take("xh2", [128, 2 * 2 * D], F32)
    b_xh2 = [Buf("xh2_0"), Buf("xh2_1")]
    aztm, _ = a3.take("aztm", [128, 2 * 1024], BF16)
    b_aztm = [Buf("aztm0"), Buf("aztm1")]
    azf, _ = a3.take("azf", [128, 2 * 8 * 512], BF16)
    b_azf = [Buf("azf0"), Buf("azf1")]
    den_sb, b_den = a3.take("den_sb", [128, 16], F32)
    zs_sb, b_zs = a3.take("zs_sb", [128, 16], F32)
    out_sb, b_out = a3.take("out_sb", [128, 4 * D], F32)
    gs_sb, b_gs = a3.take("gs_sb", [128, D], F32)
    mres, b_mres = a3.take("mres", [128, 512], F32)

    ps_atA, b_atA = P512[0], bP512[0]
    ps_atB, b_atB = P512[1], bP512[1]
    b_tr2 = [bP1024, Buf("ps1024_b1")]

    dma("gpsimd", bor[:, :], bo_row_in[:, :], writes=[b_bor], extra=fence2)
    for oc in range(8):
        dma("gpsimd", wstage2[:, :], W_in["Wo"][oc * 128:(oc + 1) * 128, :],
            reads=[b_wstage2], writes=[b_wstage2], extra=fence2)
        for ic in range(8):
            psb, bps = (ps_atA, b_atA) if ic % 2 == 0 else (ps_atB, b_atB)
            pbf = psb.ap().bitcast(BF16)
            tr(pbf[:, 0:128], wstage2[:, ic * 128:(ic + 1) * 128], idb[:, :],
               reads=[b_wstage2, b_idb], writes=[bps], extra=fence2)
            S.add("scalar",
                  lambda h, c=ic * D + oc * 128, p=pbf: h.copy(WoT[:, c:c + 128], p[:, 0:128]),
                  reads=[bps], writes=[b_WoT])

    dma("gpsimd", kvfull.ap().rearrange("p (b f) -> p b f", b=2),
        cc_out.ap().rearrange("(b p) f -> p b f", p=128),
        reads=[b_ccout], writes=[b_kvfull], extra=[cc_op] + list(fence2))
    if "kvfull" in dbg:
        dma("scalar", dbg["kvfull"][:, :], kvfull[:, :], reads=[b_kvfull])
    S.add("vector", lambda h: h.tensor_scalar(kvm[:, :], kvfull[:, 0:520], bmb[:, 0:1], None, ALU.mult),
          reads=[b_kvfull, b_bmb], writes=[b_kvm])
    S.add("vector", lambda h: h.scalar_tensor_tensor(kvm[:, :], kvfull[:, 520:1040], bmb[:, 1:2], kvm[:, :], ALU.mult, ALU.add),
          reads=[b_kvfull, b_bmb, b_kvm], writes=[b_kvm])
    for h_ in range(NH):
        pb, fo, sl = (h_ % 2) * 64, (h_ // 2) * 64, h_ // 2
        vop("tensor_copy", kvaug[pb:pb + 64, sl * 66:sl * 66 + 64],
            kvm[pb:pb + 64, fo:fo + 64], reads=[b_kvm], writes=[b_kvaug])
        vop("tensor_copy", kvaug[pb:pb + 64, sl * 66 + 64:sl * 66 + 65],
            kvm[pb:pb + 64, 512 + sl:512 + sl + 1], reads=[b_kvm], writes=[b_kvaug])

    q4 = q_sb.ap().rearrange("p (a c f) -> p a c f", a=2, c=8)
    xh2v = xh2.ap().rearrange("p (a u f) -> p a u f", a=2, u=2)

    for t in (range(NTILE) if "attn" not in ablate else ()):
        qpar = t % 2
        azp = t % 2
        if t == 0:
            dma("sync", q4[:, 0, :, :], q_spill[0].rearrange("(c p) f -> p c f", p=128),
                reads=[b_qsp, b_qsb[0]], writes=[b_qsb[0]], extra=fence2)
            for hf in range(2):
                dma("sync", xh2v[:, hf, :, :],
                    x_in[hf * 256:(hf + 1) * 256, :].rearrange("(u p) f -> p u f", p=128),
                    reads=[b_xh2[hf]], writes=[b_xh2[hf]], extra=fence2)

        def p3_att(sub):
            pA, bA = (P512[0], bP512[0]) if sub % 2 == 0 else (P512[2], bP512[2])
            pB, bB = (P512[1], bP512[1]) if sub % 2 == 0 else (P512[3], bP512[3])
            for h_ in range(NH):
                qrow = (h_ % 2) * 64
                sl = h_ // 2
                qT = q4[qrow:qrow + 64, qpar, sl, sub * 128:(sub + 1) * 128]
                po, bo_ = (pA, bA) if h_ % 2 == 0 else (pB, bB)
                mm(po[:, sl * 64:sl * 64 + 64], qT,
                   kvaug[qrow:qrow + 64, sl * 66:sl * 66 + 64],
                   start=True, stop=True, reads=[b_qsb[qpar], b_kvaug], writes=[bo_])

        def p3_den(sub):
            pD0, bD0 = P512[4], bP512[4]
            pD1, bD1 = P512[5], bP512[5]
            for h_ in range(NH):
                qrow = (h_ % 2) * 64
                sl = h_ // 2
                qT = q4[qrow:qrow + 64, qpar, sl, sub * 128:(sub + 1) * 128]
                pd, bd = (pD0, bD0) if h_ % 2 == 0 else (pD1, bD1)
                mm(pd[:, sub * 8 + sl:sub * 8 + sl + 1],
                   qT, kvaug[qrow:qrow + 64, sl * 66 + 64:sl * 66 + 65],
                   start=True, stop=True, reads=[b_qsb[qpar], b_kvaug], writes=[bd])

        def p3_ztr(sub):
            pA, bA = (P512[0], bP512[0]) if sub % 2 == 0 else (P512[2], bP512[2])
            pB, bB = (P512[1], bP512[1]) if sub % 2 == 0 else (P512[3], bP512[3])
            pD0, bD0 = P512[4], bP512[4]
            pD1, bD1 = P512[5], bP512[5]
            S.add("vector", lambda h, pD0=pD0, sub=sub: h.tensor_scalar_add(den_sb[:, 0:8], pD0[:, sub * 8:sub * 8 + 8], 1e-6),
                  reads=[bD0], writes=[b_den])
            S.add("vector", lambda h, pD1=pD1, sub=sub: h.tensor_scalar_add(den_sb[:, 8:16], pD1[:, sub * 8:sub * 8 + 8], 1e-6),
                  reads=[bD1], writes=[b_den])
            S.add("vector", lambda h: h.reciprocal(zs_sb[:, :], den_sb[:, :]),
                  reads=[b_den], writes=[b_zs])
            for h_ in range(NH):
                sl = h_ // 2
                po, bo_ = (pA, bA) if h_ % 2 == 0 else (pB, bB)
                src = po[:, sl * 64:sl * 64 + 64]
                zcol = sl + 8 * (h_ % 2)
                dst = aztm[:, (sub % 2) * 1024 + h_ * 64:(sub % 2) * 1024 + (h_ + 1) * 64]
                if h_ % 2 == 0:
                    S.add("vector", lambda h, d=dst, sr=src, zc=zcol: h.tensor_scalar(d, sr, zs_sb[:, zc:zc + 1], None, ALU.mult),
                          reads=[bo_, b_zs], writes=[b_aztm[sub % 2]])
                else:
                    S.add("scalar", lambda h, d=dst, sr=src, zc=zcol: h.activation(d, sr, AF.Copy, scale=zs_sb[:, zc:zc + 1]),
                          reads=[bo_, b_zs], writes=[b_aztm[sub % 2]])
            for ic in range(8):
                trbank = ic % 2
                pt = P1024.ap().bitcast(BF16)[:, trbank * 1024 + 0:trbank * 1024 + 128]
                tr(pt, aztm[:, (sub % 2) * 1024 + ic * 128:(sub % 2) * 1024 + (ic + 1) * 128], idb[:, :],
                   reads=[b_aztm[sub % 2], b_idb], writes=[b_tr2[trbank]])
                col = ic * TILE + sub * 128
                S.add("scalar", lambda h, c=col, p=pt, azp=azp: h.copy(
                      azf[:, azp * 4096 + c: azp * 4096 + c + 128], p),
                      reads=[b_tr2[trbank]], writes=[b_azf[azp]])

        if "p3cb" not in ablate:
            p3_att(0)
            p3_den(0)
            p3_att(1)
            for sub in range(NSUB):
                if "p3z" not in ablate:
                    p3_ztr(sub)
                if sub + 1 < NSUB:
                    p3_den(sub + 1)
                if sub + 2 < NSUB:
                    p3_att(sub + 2)

        if t + 1 < NTILE:
            dma("sync", q4[:, (t + 1) % 2, :, :], q_spill[t + 1].rearrange("(c p) f -> p c f", p=128),
                reads=[b_qsp, b_qsb[(t + 1) % 2]], writes=[b_qsb[(t + 1) % 2]])
        for sub in (range(NSUB) if "p3wo" not in ablate else ()):
            g = t * NSUB + sub
            par = (t * 2 + sub // 2) % 2
            S.add("scalar",
                  lambda h, g=g: h.activation(gs_sb[:, :], g_pre[:, g * D:(g + 1) * D], AF.Silu,
                                              scale=S_all[:, g:g + 1]),
                  reads=[b_gpre_ch[g], b_S], writes=[b_gs])
            for oh in range(2):
                pw, bpw = P512[sub], bP512[sub]
                for ic in range(8):
                    mm(pw[:, :], azf[:, azp * 4096 + ic * TILE + sub * 128: azp * 4096 + ic * TILE + sub * 128 + 128],
                       WoT[:, ic * D + oh * 512:(ic) * D + (oh + 1) * 512],
                       start=(ic == 0), stop=False, reads=[b_azf[azp], b_WoT], writes=[bpw])
                mm(pw[:, :], onesb[0:1, 0:128], bor[0:1, oh * 512:(oh + 1) * 512],
                   start=False, stop=True, reads=[b_onesb, b_bor], writes=[bpw])
                S.add("vector", lambda h, pw=pw, oh=oh: h.tensor_mul(
                      mres[:, :], pw[:, :], gs_sb[:, oh * 512:(oh + 1) * 512]),
                      reads=[bpw, b_gs], writes=[b_mres])
                S.add("vector",
                      lambda h, sub=sub, par=par, oh=oh: h.tensor_add(
                          out_sb[:, sub * D + oh * 512: sub * D + (oh + 1) * 512],
                          xh2v[:, par, sub % 2, oh * 512:(oh + 1) * 512], mres[:, :]),
                      reads=[b_xh2[par], b_mres], writes=[b_out])
        dma("sync", y_out[t * TILE:(t + 1) * TILE, :].rearrange("(s p) f -> p s f", p=128),
            out_sb.ap().rearrange("p (s f) -> p s f", s=NSUB),
            reads=[b_out], writes=[b_y])
        if t + 1 < NTILE:
            for hf in range(2):
                dma("sync", xh2v[:, hf, :, :],
                    x_in[(t + 1) * TILE + hf * 256: (t + 1) * TILE + (hf + 1) * 256, :].rearrange("(u p) f -> p u f", p=128),
                    reads=[b_xh2[hf]], writes=[b_xh2[hf]])

    S.add("gpsimd", lambda h: h.nop(), extra_waits=S.snapshot())

    guards = S.emit(nc)
    return nc


def kernel(**inputs):
    x = np.ascontiguousarray(np.asarray(inputs["L0_attn_hidden_states"], dtype=np.float32).reshape(2 * 16384, D))
    Wq = np.ascontiguousarray(np.asarray(inputs["L0_attn_Wq"], dtype=np.float32))
    Wk = np.ascontiguousarray(np.asarray(inputs["L0_attn_Wk"], dtype=np.float32))
    Wv = np.ascontiguousarray(np.asarray(inputs["L0_attn_Wv"], dtype=np.float32))
    Wo = np.ascontiguousarray(np.asarray(inputs["L0_attn_Wo"], dtype=np.float32))
    bq = np.asarray(inputs["L0_attn_bq"], dtype=np.float32)
    bk = np.asarray(inputs["L0_attn_bk"], dtype=np.float32)
    bv = np.asarray(inputs["L0_attn_bv"], dtype=np.float32)
    bo = np.asarray(inputs["L0_attn_bo"], dtype=np.float32)
    wg_red = np.ascontiguousarray(np.asarray(inputs["L0_ffn_wg_red"], dtype=np.float32))
    wg = np.ascontiguousarray(np.asarray(inputs["L0_ffn_wg"], dtype=np.float32))
    w1 = np.ascontiguousarray(np.asarray(inputs["L0_ffn_w1"], dtype=np.float32))
    b1 = np.asarray(inputs["L0_ffn_b1"], dtype=np.float32)
    w2 = np.ascontiguousarray(np.asarray(inputs["L0_ffn_w2"], dtype=np.float32))
    b2 = np.asarray(inputs["L0_ffn_b2"], dtype=np.float32)

    bq_t = np.ascontiguousarray(bq.reshape(8, 128).T)
    bo_t = np.ascontiguousarray(bo.reshape(8, 128).T)
    bk_row = np.ascontiguousarray(bk.reshape(1, D))
    bo_row = np.ascontiguousarray(bo.reshape(1, D))
    bv_row = np.ascontiguousarray(bv.reshape(1, D))
    b1_t = np.ascontiguousarray(b1.reshape(NE, 32, 128).transpose(2, 1, 0).reshape(128, 32 * NE))
    b2row = np.ascontiguousarray(b2.reshape(1, NE * D))
    ident = np.eye(128, dtype=np.float32)
    ones_row = np.ones((1, 512), dtype=np.float32)
    ones_col = np.ones((128, 1), dtype=np.float32)
    pw2 = (2.0 ** -np.arange(NE, dtype=np.float32)).reshape(1, NE)

    nc = build_program(debug=tuple(DEBUG), ablate=frozenset(ABLATE))

    in_maps = []
    for c in range(NCORE):
        bmask = np.zeros((1, 2), dtype=np.float32)
        bmask[0, c // 4] = 1.0
        in_maps.append({
            "x": x[c * TOK:(c + 1) * TOK],
            "Wq": Wq, "Wk": Wk, "Wv": Wv, "Wo": Wo,
            "bq_t": bq_t, "bo_t": bo_t, "bk_row": bk_row, "bv_row": bv_row, "bo_row": bo_row,
            "wg_red": wg_red, "wg": wg, "w1": w1, "b1_t": b1_t, "w2": w2, "b2row": b2row,
            "ident": ident, "ones_row": ones_row, "ones_col": ones_col,
            "pw2": pw2, "bmask": bmask,
        })

    res = run_bass_kernel_spmd(nc, in_maps, core_ids=list(range(NCORE)))
    out = np.concatenate([res.results[c]["y"] for c in range(NCORE)], axis=0)
    if DEBUG:
        kernel.debug_results = res.results
    return out.reshape(2, 16384, D)


if __name__ == "__main__":
    build_program()
    print("build ok")


# version 12
# speedup vs baseline: 2.4285x; 1.0072x over previous
import numpy as np
import ml_dtypes
import concourse.bass as bass
import concourse.mybir as mybir
from concourse.bass_utils import run_bass_kernel_spmd

F32 = mybir.dt.float32
BF16 = mybir.dt.bfloat16
F16 = mybir.dt.float16
AF = mybir.ActivationFunctionType
ALU = mybir.AluOpType
AX = mybir.AxisListType

NCORE = 8
TOK = 4096
D = 1024
NH = 16
NE = 8
HID = 4096
TILE = 512
NTILE = TOK // TILE
NSUB = TILE // 128
NCH = TOK // 128

DEBUG = []
ABLATE = set()


class Buf:
    __slots__ = ("name", "w", "readers")

    def __init__(self, name):
        self.name = name
        self.w = None
        self.readers = {}


class Op:
    __slots__ = ("eng", "fn", "waits", "dom", "inc", "flagged", "count_after", "i", "eng_seq", "raw")


RAW_K = 4


class Sched:
    def __init__(self):
        self.plan = []
        self.eng_n = {}

    def add(self, eng, fn, reads=(), writes=(), dma=False, cc=False, extra_waits=()):
        op = Op()
        op.eng = eng
        op.fn = fn
        op.dom = eng + "_dma" if dma else ("cc" if cc else eng)
        op.inc = 16 if dma else 1
        op.flagged = False
        op.count_after = None
        op.i = len(self.plan)
        op.eng_seq = self.eng_n.get(eng, 0)
        self.eng_n[eng] = op.eng_seq + 1
        ws = {}
        rw = {}

        def upd(d, w):
            c = d.get(w.dom)
            if c is None or w.i > c.i:
                d[w.dom] = w

        for b in reads:
            if b.w is not None:
                upd(ws, b.w)
                upd(rw, b.w)
        for b in writes:
            for r in b.readers.values():
                upd(ws, r)
            if b.w is not None:
                upd(ws, b.w)
        for w in extra_waits:
            if w is not None:
                upd(ws, w)
        waits = set(ws.values())
        raw = set(rw.values())
        waits.discard(op)
        raw.discard(op)
        for w in raw:
            waits.add(w)
        op.waits = sorted(waits, key=lambda o: o.i)
        op.raw = raw
        for b in reads:
            b.readers[op.dom] = op
        for b in writes:
            b.w = op
            b.readers = {}
        self.plan.append(op)
        return op

    def snapshot(self):
        last = {}
        for op in self.plan:
            last[op.dom] = op
        return list(last.values())

    def emit(self, nc):
        for op in self.plan:
            if op.dom.endswith("_dma") or op.dom == "cc":
                op.flagged = True
        def need_wait(op, w):
            if w.dom != op.eng:
                return True
            if op.eng == "tensor":
                return False
            return (op.eng_seq - w.eng_seq) <= RAW_K

        for op in self.plan:
            for w in op.waits:
                if need_wait(op, w):
                    w.flagged = True
        counts = {}
        for op in self.plan:
            if op.flagged:
                counts[op.dom] = counts.get(op.dom, 0) + op.inc
                op.count_after = counts[op.dom]
        doms = set(counts.keys())
        sems = {}
        guards = []
        for d in sorted(doms):
            g = nc.semaphore("sem_" + d)
            sems[d] = g.__enter__()
            guards.append(g)

        engs = ("tensor", "vector", "scalar", "gpsimd", "sync")
        per_eng = {e: [op for op in self.plan if op.eng == e] for e in engs}

        def mk_body(eng):
            ops = per_eng[eng]

            def body(h):
                hw = {}
                for op in ops:
                    for w in op.waits:
                        if not need_wait(op, w):
                            continue
                        assert w.flagged, (w.i, op.i)
                        tgt = w.count_after
                        if hw.get(w.dom, 0) >= tgt:
                            continue
                        h.wait_ge(sems[w.dom], tgt)
                        hw[w.dom] = tgt
                    inst = op.fn(h)
                    if op.flagged:
                        inst.then_inc(sems[op.dom], op.inc)
            return body

        with nc.Block() as block:
            for eng in engs:
                if per_eng[eng]:
                    getattr(block, eng)(mk_body(eng))
        return guards


DT_SIZE = {F32: 4, BF16: 2, F16: 2}


def build_program(debug=(), ablate=frozenset()):
    nc = bass.Bass("TRN2", target_bir_lowering=False)
    S = Sched()

    x_in = nc.dram_tensor("x", [TOK, D], F32, kind="ExternalInput")
    W_in = {w: nc.dram_tensor(w, [D, D], F32, kind="ExternalInput") for w in ("Wq", "Wk", "Wv", "Wo")}
    bq_t_in = nc.dram_tensor("bq_t", [128, 8], F32, kind="ExternalInput")
    bo_t_in = nc.dram_tensor("bo_t", [128, 8], F32, kind="ExternalInput")
    bk_row_in = nc.dram_tensor("bk_row", [1, D], F32, kind="ExternalInput")
    bo_row_in = nc.dram_tensor("bo_row", [1, D], F32, kind="ExternalInput")
    bv_row_in = nc.dram_tensor("bv_row", [1, D], F32, kind="ExternalInput")
    wgred_in = nc.dram_tensor("wg_red", [16, D], F32, kind="ExternalInput")
    wg_in = nc.dram_tensor("wg", [NE, 16], F32, kind="ExternalInput")
    w1_in = nc.dram_tensor("w1", [NE, D, HID], F32, kind="ExternalInput")
    b1_t_in = nc.dram_tensor("b1_t", [128, 32 * NE], F32, kind="ExternalInput")
    w2_in = nc.dram_tensor("w2", [NE, HID, D], F32, kind="ExternalInput")
    b2row_in = nc.dram_tensor("b2row", [1, NE * D], F32, kind="ExternalInput")
    ident_in = nc.dram_tensor("ident", [128, 128], F32, kind="ExternalInput")
    ones_in = nc.dram_tensor("ones_row", [1, 512], F32, kind="ExternalInput")
    onesc_in = nc.dram_tensor("ones_col", [128, 1], F32, kind="ExternalInput")
    pw2_in = nc.dram_tensor("pw2", [1, NE], F32, kind="ExternalInput")
    bmask_in = nc.dram_tensor("bmask", [1, 2], F32, kind="ExternalInput")

    y_out = nc.dram_tensor("y", [TOK, D], F32, kind="ExternalOutput")

    xn_spill = nc.dram_tensor("xn_spill", [NTILE, D, TILE], BF16)
    q_spill = nc.dram_tensor("q_spill", [NTILE, D, TILE], BF16)
    cc_in = nc.dram_tensor("cc_in", [2 * 128, 520], F32)
    cc_out = nc.dram_tensor("cc_out", [2 * 128, 520], F32)

    dbg = {}
    if "xn" in debug:
        dbg["xn"] = nc.dram_tensor("dbg_xn", [NTILE, D, TILE], BF16, kind="ExternalOutput")
    if "q" in debug:
        dbg["q"] = nc.dram_tensor("dbg_q", [NTILE, D, TILE], BF16, kind="ExternalOutput")
    if "kv" in debug:
        dbg["kv"] = nc.dram_tensor("dbg_kv", [128, 520], F32, kind="ExternalOutput")
    if "kvfull" in debug:
        dbg["kvfull"] = nc.dram_tensor("dbg_kvfull", [128, 1040], F32, kind="ExternalOutput")
    if "C" in debug:
        dbg["C"] = nc.dram_tensor("dbg_C", [128, NCH * NE], F32, kind="ExternalOutput")
    if "Sc" in debug:
        dbg["Sc"] = nc.dram_tensor("dbg_Sc", [128, NCH], F32, kind="ExternalOutput")
    if "gpre" in debug:
        dbg["gpre"] = nc.dram_tensor("dbg_gpre", [NCH, 128, D], F16, kind="ExternalOutput")
    if "xr" in debug:
        dbg["xr"] = nc.dram_tensor("dbg_xr", [NTILE, 16, TILE], F32, kind="ExternalOutput")
    if "lg" in debug:
        dbg["lg"] = nc.dram_tensor("dbg_lg", [128, NCH * NE], F32, kind="ExternalOutput")
    if "lnraw" in debug:
        dbg["lnraw"] = nc.dram_tensor("dbg_lnraw", [128, D], F32, kind="ExternalOutput")
        dbg["lnst"] = nc.dram_tensor("dbg_lnst", [128, 8], F32, kind="ExternalOutput")
    if "ktm" in debug:
        dbg["ktm"] = nc.dram_tensor("dbg_ktm", [128, D], BF16, kind="ExternalOutput")
        dbg["vtm"] = nc.dram_tensor("dbg_vtm", [128, D], BF16, kind="ExternalOutput")

    SB_BASE = (nc.sbuf_base + 63) & ~63
    SB_TOP = nc.sbuf_top

    class Arena:
        def __init__(self, base):
            self.off = base
            self.hi = base

        def take(self, name, shape, dtype):
            nbytes = int(np.prod(shape[1:])) * DT_SIZE[dtype]
            off = (self.off + 63) & ~63
            t = nc.alloc_sbuf_tensor_at(name, list(shape), dtype, offset=off)
            self.off = off + nbytes
            self.hi = max(self.hi, self.off)
            assert self.off <= SB_TOP, (name, self.off, SB_TOP)
            return t, Buf(name)

    pers = Arena(SB_BASE)
    g_pre, b_gpre = pers.take("g_pre", [128, NCH * D], F16)
    b_gpre_ch = [Buf(f"g_pre{g}") for g in range(NCH)]
    C_all, b_C = pers.take("C_all", [128, NCH * NE], F32)
    S_all, b_S = pers.take("S_all", [128, NCH], F32)
    b1t, b_b1t = pers.take("b1t", [128, 32 * NE], F32)
    bqt, b_bqt = pers.take("bqt", [128, 8], F32)
    bot, b_bot = pers.take("bot", [128, 8], F32)
    idf, b_idf = pers.take("idf", [128, 128], F32)
    idb, b_idb = pers.take("idb", [128, 128], BF16)
    onesf, b_onesf = pers.take("onesf", [1, 512], F32)
    onesb, b_onesb = pers.take("onesb", [1, 512], BF16)
    onescb, b_onescb = pers.take("onescb", [128, 1], BF16)
    pw2b, b_pw2 = pers.take("pw2b", [128, NE], F32)
    bmb, b_bmb = pers.take("bmb", [128, 2], F32)
    ARENA_BASE = (pers.off + 63) & ~63

    b_ccin, b_ccout = Buf("cc_in"), Buf("cc_out")
    b_xnsp, b_qsp = Buf("xn_spill"), Buf("q_spill")
    b_y = Buf("y")

    P512 = []
    bP512 = []
    for i in range(6):
        t = nc.alloc_psum_tensor(f"ps{i}", [128, 512], F32)
        P512.append(t)
        bP512.append(Buf(f"ps{i}"))
    P1024 = nc.alloc_psum_tensor("ps1024", [128, 1024], F32)
    bP1024 = Buf("ps1024")

    def dma(eng, dst, src, reads=(), writes=(), extra=()):
        return S.add(eng, lambda h, d=dst, s=src: h.dma_start(out=d, in_=s),
                     reads=reads, writes=writes, dma=True, extra_waits=extra)

    dma("gpsimd", idb[:, :], ident_in[:, :], writes=[b_idb])
    dma("gpsimd", onesb[:, :], ones_in[:, :], writes=[b_onesb])
    dma("gpsimd", onescb[:, :], onesc_in[:, :], writes=[b_onescb])
    dma("sync", b1t[:, :], b1_t_in[:, :], writes=[b_b1t])
    dma("sync", bqt[:, :], bq_t_in[:, :], writes=[b_bqt])
    dma("sync", bot[:, :], bo_t_in[:, :], writes=[b_bot])
    dma("sync", idf[:, :], ident_in[:, :], writes=[b_idf])
    dma("sync", onesf[:, :], ones_in[:, :], writes=[b_onesf])
    dma("sync", pw2b[:, :], pw2_in.ap().broadcast_to([128, NE]), writes=[b_pw2])
    dma("sync", bmb[:, :], bmask_in.ap().broadcast_to([128, 2]), writes=[b_bmb])

    def mm(out, lhsT, rhs, start, stop, reads, writes, skip_gc=True, extra=()):
        return S.add("tensor",
                     lambda h, o=out, l=lhsT, r=rhs, st=start, sp=stop, sg=skip_gc:
                     h.matmul(o, l, r, start=st, stop=sp, skip_group_check=sg),
                     reads=reads, writes=writes, extra_waits=extra)

    def tr(out, in_, ident, reads, writes, extra=()):
        return S.add("tensor", lambda h, o=out, i=in_, d=ident: h.transpose(o, i, d),
                     reads=reads, writes=writes, extra_waits=extra)

    def act(out, in_, func, bias=0.0, scale=1.0, reads=(), writes=(), extra=()):
        return S.add("scalar",
                     lambda h, o=out, i=in_, f=func, b=bias, sc=scale: h.activation(o, i, f, bias=b, scale=sc),
                     reads=reads, writes=writes, extra_waits=extra)

    def vop(name, *aps, reads=(), writes=(), extra=(), eng="vector"):
        def fn(h, n=name, a=aps):
            return getattr(h, n)(*a)
        return S.add(eng, fn, reads=reads, writes=writes, extra_waits=extra)

    a1 = Arena(ARENA_BASE)
    WqT, b_WqT = a1.take("WqT", [128, 8 * D], BF16)
    WkT, b_WkT = a1.take("WkT", [128, 8 * D], BF16)
    WvT, b_WvT = a1.take("WvT", [128, 8 * D], BF16)
    wstage, b_wstage = a1.take("wstage", [128, D], BF16)
    wgredT, b_wgredT = a1.take("wgredT", [128, 8 * 16], F32)
    wgr_st, b_wgr_st = a1.take("wgr_st", [16, D], F32)
    wgu, b_wgu = a1.take("wgu", [NE, 16], F32)
    wguT, b_wguT = a1.take("wguT", [16, NE], F32)
    wgtmp, b_wgtmp = a1.take("wgtmp", [NE, 20], F32)
    bkr, b_bkr = a1.take("bkr", [1, D], BF16)
    bvr, b_bvr = a1.take("bvr", [1, D], BF16)
    xh, _ = a1.take("xh", [128, 2 * D], F32)
    b_xh = [Buf("xh0"), Buf("xh1")]
    ctr2, _ = a1.take("ctr2", [128, 2 * D], F32)
    b_ctr2 = [Buf("ctr0"), Buf("ctr1")]
    scr1, b_scr1 = a1.take("scr1", [128, D], F32)
    scr22, _ = a1.take("scr22", [128, 2 * D], F32)
    b_scr22 = [Buf("xnl0"), Buf("xnl1")]
    lnst2, _ = a1.take("lnst2", [128, 16], F32)
    b_lnst2 = [Buf("lnst0"), Buf("lnst1")]
    xnf_bf2, _ = a1.take("xnf_bf2", [128, 2 * 8 * TILE], BF16)
    b_xnbf2 = [Buf("xnbf0"), Buf("xnbf1")]
    xnf_f32, b_xnf32 = a1.take("xnf_f32", [128, 8 * 256], F32)
    q_tile, b_q = a1.take("q_tile", [128, 8 * TILE], BF16)
    k_tm, b_ktm = a1.take("k_tm", [128, D], BF16)
    v_tm, b_vtm = a1.take("v_tm", [128, D], BF16)
    xr_sb, _ = a1.take("xr_sb", [16, 2 * TILE], F32)
    b_xr2 = [Buf("xr0"), Buf("xr1")]
    gsc, b_gsc = a1.take("gsc", [128, 8 * 8], F32)
    qsc, b_qsc = a1.take("qsc", [128, D], F32)
    if "lg" in debug:
        lgbuf, b_lg = a1.take("lgbuf", [128, NCH * NE], F32)
    kvacc, b_kvacc = a1.take("kvacc", [128, 520], F32)
    contrib, b_contrib = a1.take("contrib", [128, 2 * 520], F32)

    ps_kvh, b_pkvh = P512[0], bP512[0]
    ps_ksum, b_pksum = P512[1], bP512[1]
    ps_trA, b_trA = P512[2], bP512[2]
    ps_trB, b_trB = P512[3], bP512[3]
    ps_mmA, b_mmA = P512[4], bP512[4]
    ps_mmB, b_mmB = P512[5], bP512[5]
    ps_kv2, b_pkv2 = P1024, bP1024

    dma("gpsimd", bkr[:, :], bk_row_in[:, :], writes=[b_bkr])
    dma("gpsimd", bvr[:, :], bv_row_in[:, :], writes=[b_bvr])

    def prep_wT(nm, WT, b_WT):
        for oc in range(8):
            dma("gpsimd", wstage[:, :], W_in[nm][oc * 128:(oc + 1) * 128, :],
                reads=(), writes=[b_wstage])
            for ic in range(8):
                psb, bps = (ps_trA, b_trA) if ic % 2 == 0 else (ps_trB, b_trB)
                pbf = psb.ap().bitcast(BF16)
                tr(pbf[:, 0:128], wstage[:, ic * 128:(ic + 1) * 128], idb[:, :],
                   reads=[b_wstage, b_idb], writes=[bps])
                S.add("scalar",
                      lambda h, w=WT, c=ic * D + oc * 128, p=pbf: h.copy(w[:, c:c + 128], p[:, 0:128]),
                      reads=[bps], writes=[b_WT])

    prep_wT("Wq", WqT, b_WqT)
    prep_wT("Wk", WkT, b_WkT)
    prep_wT("Wv", WvT, b_WvT)

    dma("sync", wgr_st[:, :], wgred_in[:, :], writes=[b_wgr_st])
    for ic in range(8):
        psb, bps = (ps_trA, b_trA) if ic % 2 == 0 else (ps_trB, b_trB)
        tr(psb[0:128, 0:16], wgr_st[0:16, ic * 128:(ic + 1) * 128], idf[0:16, 0:16],
           reads=[b_wgr_st, b_idf], writes=[bps])
        vop("tensor_copy", wgredT[:, ic * 16:(ic + 1) * 16], psb[0:128, 0:16],
            reads=[bps], writes=[b_wgredT])

    dma("sync", wgu[:, :], wg_in[:, :], writes=[b_wgu])
    act(wgtmp[:, 0:16], wgu[:, :], AF.Square, reads=[b_wgu], writes=[b_wgtmp])
    vop("reduce_sum", wgtmp[:, 16:17], wgtmp[:, 0:16], AX.X, reads=[b_wgtmp], writes=[b_wgtmp])
    act(wgtmp[:, 17:18], wgtmp[:, 16:17], AF.Sqrt, reads=[b_wgtmp], writes=[b_wgtmp])
    vop("tensor_scalar_max", wgtmp[:, 17:18], wgtmp[:, 17:18], 1e-4, reads=[b_wgtmp], writes=[b_wgtmp])
    vop("reciprocal", wgtmp[:, 18:19], wgtmp[:, 17:18], reads=[b_wgtmp], writes=[b_wgtmp])
    S.add("vector", lambda h: h.tensor_scalar(wgu[:, :], wgu[:, :], wgtmp[:, 18:19], None, ALU.mult),
          reads=[b_wgtmp], writes=[b_wgu])
    tr(ps_trA[0:16, 0:NE], wgu[:, :], idf[0:NE, 0:NE], reads=[b_wgu, b_idf], writes=[b_trA])
    vop("tensor_copy", wguT[:, :], ps_trA[0:16, 0:NE], reads=[b_trA], writes=[b_wguT])


    def xnf(t):
        return xnf_bf2.ap().rearrange("p (a c f) -> p a c f", a=2, c=8)[:, t % 2, :, :]

    def stage_ln(t):
        bxnbf = b_xnbf2[t % 2]
        if t == 0:
            for k0 in range(2):
                dma("sync", xh[:, (k0 % 2) * D:(k0 % 2 + 1) * D],
                    x_in[k0 * 128:(k0 + 1) * 128, :],
                    reads=[b_xh[k0 % 2]], writes=[b_xh[k0 % 2]])
        for s in range(NSUB):
            kg = t * NSUB + s
            par = kg % 2
            pp = s % 2
            xsub = xh[:, par * D:(par + 1) * D]
            bx = b_xh[par]
            ctr = ctr2[:, pp * D:(pp + 1) * D]
            b_ctr = b_ctr2[pp]
            scr2 = scr22[:, pp * D:(pp + 1) * D]
            b_scr2 = b_scr22[pp]
            lnst = lnst2[:, pp * 8:(pp + 1) * 8]
            b_lnst = b_lnst2[pp]
            vop("reduce_sum", lnst[:, 0:1], xsub, AX.X, reads=[bx], writes=[b_lnst])
            vop("tensor_scalar_mul", lnst[:, 1:2], lnst[:, 0:1], 1.0 / D, reads=[b_lnst], writes=[b_lnst])
            S.add("vector", lambda h, xs=xsub, ctr=ctr, lnst=lnst: h.tensor_scalar(ctr[:, :], xs, lnst[:, 1:2], None, ALU.subtract),
                  reads=[bx, b_lnst], writes=[b_ctr])
            act(scr2[:, :], ctr[:, :], AF.Square, reads=[b_ctr], writes=[b_scr2])
            vop("reduce_sum", lnst[:, 2:3], scr2[:, :], AX.X, reads=[b_scr2], writes=[b_lnst])
            act(lnst[:, 3:4], lnst[:, 2:3], AF.Ln, bias=np.float32(1e-5), scale=1.0 / D, reads=[b_lnst], writes=[b_lnst])
            act(lnst[:, 4:5], lnst[:, 3:4], AF.Exp, scale=-0.5, reads=[b_lnst], writes=[b_lnst])
            S.add("vector", lambda h, ctr=ctr, lnst=lnst, scr2=scr2: h.tensor_scalar(scr2[:, :], ctr[:, :], lnst[:, 4:5], None, ALU.mult),
                  reads=[b_ctr, b_lnst], writes=[b_scr2])
            for ic in range(8):
                psb, bps = (ps_trA, b_trA) if ic % 2 == 0 else (ps_trB, b_trB)
                slot = (ic // 2) % 2
                tr(psb[:, slot * 128:slot * 128 + 128], scr2[:, ic * 128:(ic + 1) * 128], idf[:, :],
                   reads=[b_scr2, b_idf], writes=[bps])
                colf = ic * 256 + pp * 128
                vop("tensor_copy", xnf_f32[:, colf:colf + 128], psb[:, slot * 128:slot * 128 + 128],
                    reads=[bps], writes=[b_xnf32])
                S.add("scalar", lambda h, t=t, c=ic * TILE + s * 128, cf=colf: h.copy(
                      xnf(t)[:, c // TILE, c % TILE: c % TILE + 128], xnf_f32[:, cf:cf + 128]),
                      reads=[b_xnf32], writes=[bxnbf])
            if s % 2 == 1 and "gating" not in ablate:
                hf2 = s // 2
                for ic in range(8):
                    mm(ps_mmA[0:16, hf2 * 256:hf2 * 256 + 256], wgredT[:, ic * 16:(ic + 1) * 16],
                       xnf_f32[:, ic * 256:(ic + 1) * 256],
                       start=(ic == 0), stop=(ic == 7), reads=[b_wgredT, b_xnf32], writes=[b_mmA])
                vop("tensor_copy", xr_sb[:, (t % 2) * TILE + hf2 * 256:(t % 2) * TILE + hf2 * 256 + 256],
                    ps_mmA[0:16, hf2 * 256:hf2 * 256 + 256],
                    reads=[b_mmA], writes=[b_xr2[t % 2]])
            if kg + 2 < NTILE * NSUB:
                k2 = kg + 2
                dma("sync", xh[:, (k2 % 2) * D:(k2 % 2 + 1) * D],
                    x_in[k2 * 128:(k2 + 1) * 128, :],
                    reads=[b_xh[k2 % 2]], writes=[b_xh[k2 % 2]])
        dma("scalar", xn_spill[t].rearrange("(c p) f -> p c f", p=128),
            xnf(t), reads=[bxnbf], writes=[b_xnsp])
        if "xn" in dbg:
            dma("scalar", dbg["xn"][t].rearrange("(c p) f -> p c f", p=128), xnf(t), reads=[bxnbf])

    def stage_proj(t):
        bxnbf = b_xnbf2[t % 2]
        xt = xnf(t)
        for oc in (range(8) if "q" not in ablate else ()):
            psq, bpsq = (ps_mmA, b_mmA) if oc % 2 == 0 else (ps_mmB, b_mmB)
            for ic in range(8):
                mm(psq[:, :], WqT[:, ic * D + oc * 128: ic * D + oc * 128 + 128],
                   xt[:, ic, :],
                   start=(ic == 0), stop=(ic == 7),
                   reads=[b_WqT, bxnbf], writes=[bpsq])
            act(scr1[:, 0:TILE], psq[:, :], AF.Relu, bias=bqt[:, oc:oc + 1],
                reads=[bpsq, b_bqt], writes=[b_scr1])
            act(qsc[:, 0:TILE], psq[:, :], AF.Exp, bias=bqt[:, oc:oc + 1],
                reads=[bpsq, b_bqt], writes=[b_qsc])
            S.add("vector",
                  lambda h, oc=oc: h.scalar_tensor_tensor(
                      q_tile[:, oc * TILE:(oc + 1) * TILE], qsc[:, 0:TILE], 1.0, scr1[:, 0:TILE],
                      ALU.min, ALU.add),
                  reads=[b_qsc, b_scr1], writes=[b_q])
        dma("scalar", q_spill[t].rearrange("(c p) f -> p c f", p=128),
            q_tile.ap().rearrange("p (c f) -> p c f", c=8),
            reads=[b_q], writes=[b_qsp])
        if "q" in dbg:
            dma("scalar", dbg["q"][t].rearrange("(c p) f -> p c f", p=128),
                q_tile.ap().rearrange("p (c f) -> p c f", c=8), reads=[b_q])

        for s in (range(NSUB) if "kv" not in ablate else ()):
            for half in range(2):
                for ic in range(8):
                    mm(ps_kv2[:, half * 512:(half + 1) * 512],
                       xt[:, ic, s * 128:s * 128 + 128],
                       WkT[:, ic * D + half * 512: ic * D + (half + 1) * 512],
                       start=(ic == 0), stop=False,
                       reads=[bxnbf, b_WkT], writes=[b_pkv2])
                mm(ps_kv2[:, half * 512:(half + 1) * 512],
                   onesb[0:1, 0:128], bkr[0:1, half * 512:(half + 1) * 512],
                   start=False, stop=True,
                   reads=[b_onesb, b_bkr], writes=[b_pkv2])
            act(scr1[:, :], ps_kv2[:, :], AF.Relu, reads=[b_pkv2], writes=[b_scr1])
            act(qsc[:, :], ps_kv2[:, :], AF.Exp, reads=[b_pkv2], writes=[b_qsc])
            S.add("vector", lambda h: h.scalar_tensor_tensor(
                  k_tm[:, :], qsc[:, :], 1.0, scr1[:, :], ALU.min, ALU.add),
                  reads=[b_qsc, b_scr1], writes=[b_ktm])
            for half in range(2):
                psv, bpsv = (ps_mmA, b_mmA) if half == 0 else (ps_mmB, b_mmB)
                for ic in range(8):
                    mm(psv[:, :],
                       xt[:, ic, s * 128:s * 128 + 128],
                       WvT[:, ic * D + half * 512: ic * D + (half + 1) * 512],
                       start=(ic == 0), stop=False,
                       reads=[bxnbf, b_WvT], writes=[bpsv])
                mm(psv[:, :], onesb[0:1, 0:128], bvr[0:1, half * 512:(half + 1) * 512],
                   start=False, stop=True, reads=[b_onesb, b_bvr], writes=[bpsv])
                vop("tensor_copy", v_tm[:, half * 512:(half + 1) * 512], psv[:, :],
                    reads=[bpsv], writes=[b_vtm])
            if "ktm" in dbg and t == 0 and s == 0:
                dma("scalar", dbg["ktm"][:, :], k_tm[:, :], reads=[b_ktm])
                dma("scalar", dbg["vtm"][:, :], v_tm[:, :], reads=[b_vtm])
            for h in range(NH):
                pb, fo = (h % 2) * 64, (h // 2) * 64
                mm(ps_kvh[pb:pb + 64, fo:fo + 64],
                   k_tm[:, h * 64:(h + 1) * 64], v_tm[:, h * 64:(h + 1) * 64],
                   start=(s == 0 and h < 2),
                   stop=(s == NSUB - 1),
                   reads=[b_ktm, b_vtm], writes=[b_pkvh], skip_gc=True)
                mm(ps_ksum[pb:pb + 64, (h // 2):(h // 2) + 1],
                   k_tm[:, h * 64:(h + 1) * 64], onescb[:, 0:1],
                   start=(s == 0 and h < 2), stop=(s == NSUB - 1),
                   reads=[b_ktm, b_onescb], writes=[b_pksum], skip_gc=True)

    def stage_tail(t):
        if "kv" in ablate:
            pass
        elif t == 0:
            vop("tensor_copy", kvacc[:, 0:512], ps_kvh[:, :], reads=[b_pkvh], writes=[b_kvacc])
            vop("tensor_copy", kvacc[:, 512:520], ps_ksum[:, 0:8], reads=[b_pksum], writes=[b_kvacc])
        else:
            vop("tensor_add", kvacc[:, 0:512], kvacc[:, 0:512], ps_kvh[:, :], reads=[b_pkvh, b_kvacc], writes=[b_kvacc])
            vop("tensor_add", kvacc[:, 512:520], kvacc[:, 512:520], ps_ksum[:, 0:8], reads=[b_pksum, b_kvacc], writes=[b_kvacc])
        if "xr" in dbg:
            dma("scalar", dbg["xr"][t], xr_sb[:, (t % 2) * TILE:(t % 2 + 1) * TILE], reads=[b_xr2[t % 2]])
        for s in (range(NSUB) if "gating" not in ablate else ()):
            g = t * NSUB + s
            mm(ps_trB[0:128, 0:NE], xr_sb[:, (t % 2) * TILE + s * 128:(t % 2) * TILE + (s + 1) * 128], wguT[:, :],
               start=True, stop=True, reads=[b_xr2[t % 2], b_wguT], writes=[b_trB])
            if "lg" in dbg:
                vop("tensor_copy", lgbuf[:, g * NE:(g + 1) * NE], ps_trB[0:128, 0:NE], reads=[b_trB], writes=[b_lg])
            vop("reduce_max", gsc[:, 0:1], ps_trB[0:128, 0:NE], AX.X, reads=[b_trB], writes=[b_gsc])
            S.add("vector", lambda h: h.tensor_scalar(gsc[:, 8:16], ps_trB[0:128, 0:NE], gsc[:, 0:1], None, ALU.subtract),
                  reads=[b_trB, b_gsc], writes=[b_gsc])
            act(gsc[:, 16:24], gsc[:, 8:16], AF.Exp, reads=[b_gsc], writes=[b_gsc])
            vop("reduce_sum", gsc[:, 1:2], gsc[:, 16:24], AX.X, reads=[b_gsc], writes=[b_gsc])
            S.add("vector", lambda h, g=g: h.reciprocal(S_all[:, g:g + 1], gsc[:, 1:2]),
                  reads=[b_gsc], writes=[b_S])
            S.add("vector", lambda h: h.tensor_scalar(gsc[:, 24:32], gsc[:, 8:16], 0.0, None, ALU.is_equal),
                  reads=[b_gsc], writes=[b_gsc])
            vop("tensor_mul", gsc[:, 32:40], gsc[:, 24:32], pw2b[:, :], reads=[b_gsc, b_pw2], writes=[b_gsc])
            vop("reduce_max", gsc[:, 2:3], gsc[:, 32:40], AX.X, reads=[b_gsc], writes=[b_gsc])
            S.add("vector", lambda h: h.tensor_scalar(gsc[:, 40:48], gsc[:, 32:40], gsc[:, 2:3], None, ALU.is_equal),
                  reads=[b_gsc], writes=[b_gsc])
            S.add("vector", lambda h, g=g: h.tensor_copy(C_all[:, g * NE:(g + 1) * NE], gsc[:, 40:48]),
                  reads=[b_gsc], writes=[b_C])

    stage_ln(0)
    for t in range(NTILE):
        stage_proj(t)
        if t + 1 < NTILE:
            stage_ln(t + 1)
        stage_tail(t)

    if True:
        S.add("vector", lambda h: h.tensor_scalar(contrib[:, 0:520], kvacc[:, :], bmb[:, 0:1], None, ALU.mult),
          reads=[b_kvacc, b_bmb], writes=[b_contrib])
    S.add("vector", lambda h: h.tensor_scalar(contrib[:, 520:1040], kvacc[:, :], bmb[:, 1:2], None, ALU.mult),
          reads=[b_kvacc, b_bmb], writes=[b_contrib])
    if "kv" in dbg:
        dma("scalar", dbg["kv"][:, :], kvacc[:, :], reads=[b_kvacc])
    dma("gpsimd", cc_in.ap().rearrange("(b p) f -> p b f", p=128),
        contrib.ap().rearrange("p (b f) -> p b f", b=2),
        reads=[b_contrib], writes=[b_ccin])
    fence1 = S.snapshot()
    cc_op = S.add("gpsimd",
                  lambda h: h.collective_compute("AllReduce", ALU.add,
                                                 replica_groups=[list(range(NCORE))],
                                                 ins=[cc_in.ap().opt()], outs=[cc_out.ap().opt()]),
                  reads=[b_ccin], writes=[b_ccout], cc=True)

    a2 = Arena(ARENA_BASE)
    xnr, b_xnr = a2.take("xnr", [128, 8 * 2048], BF16)
    b_xnr4 = [Buf(f"xnr{i}") for i in range(4)]
    xns, _ = a2.take("xns", [128, 2 * 8 * 512], BF16)
    b_xns = [Buf("xns0"), Buf("xns1")]
    w1q, _ = a2.take("w1q", [128, 2 * 8 * 1024], BF16)
    b_w1q = [Buf("w1q0"), Buf("w1q1")]
    w2q, _ = a2.take("w2q", [128, 2 * 8 * 1024], BF16)
    b_w2q = [Buf("w2q0"), Buf("w2q1")]
    b2e, _ = a2.take("b2e", [1, 2 * D], BF16)
    b_b2e = [Buf("b2e0"), Buf("b2e1")]
    h_sb, b_h = a2.take("h_sb", [128, 8 * 512], BF16)

    ps_hA, b_hA = P512[0], bP512[0]
    ps_hB, b_hB = P512[1], bP512[1]
    ps_gA, b_gA = P1024, bP1024
    ps_gB0, b_gB0 = P512[2], bP512[2]
    ps_gB1, b_gB1 = P512[3], bP512[3]

    xnr4 = xnr.ap().rearrange("p (c tt) -> p c tt", c=8)
    xns4 = xns.ap().rearrange("p (a c f) -> p a c f", a=2, c=8)
    w1q4 = w1q.ap().rearrange("p (a c f) -> p a c f", a=2, c=8)
    w2q4 = w2q.ap().rearrange("p (a c f) -> p a c f", a=2, c=8)

    S.add("vector", lambda h: h.memset(g_pre[:, :], 0.0), reads=[b_gpre],
          writes=[b_gpre] + b_gpre_ch, extra_waits=())

    for t in range(4):
        dma("sync" if t == 0 else "scalar", xnr4[:, :, t * TILE:(t + 1) * TILE],
            xn_spill[t].rearrange("(c p) f -> p c f", p=128),
            reads=[b_xnsp], writes=[b_xnr, b_xnr4[t]], extra=fence1)

    b_w1q0h = [Buf("w1q0a"), Buf("w1q0b")]
    b_w2q0h = [Buf("w2q0a"), Buf("w2q0b")]

    def w_load(j):
        e, qtr = divmod(j, 4)
        par = j % 2
        if j == 0:
            for hh in range(2):
                dma("gpsimd", w1q4[:, 0, :, hh * 512:(hh + 1) * 512],
                    w1_in[e][:, hh * 512:(hh + 1) * 512].rearrange("(c p) f -> p c f", p=128),
                    reads=[b_w1q[0]], writes=[b_w1q0h[hh]], extra=fence1)
            for hh in range(2):
                dma("gpsimd", w2q4[:, 0, hh * 4:(hh + 1) * 4, :],
                    w2_in[e][hh * 512:(hh + 1) * 512, :].rearrange("(c p) f -> p c f", p=128),
                    reads=[b_w2q[0]], writes=[b_w2q0h[hh]], extra=fence1)
        else:
            extra_w = [b_w1q0h[0].w, b_w1q0h[1].w] if j == 2 else []
            extra_w2 = [b_w2q0h[0].w, b_w2q0h[1].w] if j == 2 else []
            dma("gpsimd", w1q4[:, par, :, :],
                w1_in[e][:, qtr * 1024:(qtr + 1) * 1024].rearrange("(c p) f -> p c f", p=128),
                reads=[b_w1q[par]] + ([b_w1q0h[0], b_w1q0h[1]] if j == 2 else []),
                writes=[b_w1q[par]] + ([b_w1q0h[0], b_w1q0h[1]] if j == 2 else []), extra=fence1)
            dma("gpsimd", w2q4[:, par, :, :],
                w2_in[e][qtr * 1024:(qtr + 1) * 1024, :].rearrange("(c p) f -> p c f", p=128),
                reads=[b_w2q[par]] + ([b_w2q0h[0], b_w2q0h[1]] if j == 2 else []),
                writes=[b_w2q[par]] + ([b_w2q0h[0], b_w2q0h[1]] if j == 2 else []), extra=fence1)
        if qtr == 0:
            pe_ = e % 2
            dma("gpsimd", b2e[0:1, pe_ * D:(pe_ + 1) * D], b2row_in[0:1, e * D:(e + 1) * D],
                reads=[b_b2e[pe_]], writes=[b_b2e[pe_]], extra=fence1)

    if "ffn" not in ablate:
        w_load(0)
        w_load(1)

    for j in (range(NE * 4) if "ffn" not in ablate else ()):
        e, qtr = divmod(j, 4)
        par = j % 2
        bw1, bw2 = b_w1q[par], b_w2q[par]
        epar = e % 2
        for t in range(NTILE):
            if t < 4:
                rhs_sel = lambda ic, t=t: xnr4[:, ic, t * TILE:(t + 1) * TILE]
                b_rhs = b_xnr4[t]
            else:
                sp = (t - 4) % 2
                rhs_sel = lambda ic, sp=sp: xns4[:, sp, ic, :]
                b_rhs = b_xns[(t - 4) % 2]
            if t == 2:
                for i in range(2):
                    dma("scalar", xns4[:, i, :, :], xn_spill[4 + i].rearrange("(c p) f -> p c f", p=128),
                        reads=[b_xnsp, b_xns[i]], writes=[b_xns[i]], extra=fence1 if j == 0 else ())
            if t == 5:
                dma("scalar", xns4[:, 0, :, :], xn_spill[6].rearrange("(c p) f -> p c f", p=128),
                    reads=[b_xnsp, b_xns[0]], writes=[b_xns[0]])
            if t == 6:
                dma("scalar", xns4[:, 1, :, :], xn_spill[7].rearrange("(c p) f -> p c f", p=128),
                    reads=[b_xnsp, b_xns[1]], writes=[b_xns[1]])

            for hs in range(8):
                psh, bph = (ps_hA, b_hA) if hs % 2 == 0 else (ps_hB, b_hB)
                bw1u = b_w1q0h[hs // 4] if j == 0 else bw1
                for ic in range(8):
                    mm(psh[:, :], w1q4[:, par, ic, hs * 128:(hs + 1) * 128], rhs_sel(ic),
                       start=(ic == 0), stop=(ic == 7),
                       reads=[bw1u, b_rhs], writes=[bph])
                hsg = qtr * 8 + hs
                act(h_sb[:, hs * 512:(hs + 1) * 512], psh[:, :], AF.Gelu,
                    bias=b1t[:, hsg * NE + e: hsg * NE + e + 1],
                    reads=[bph, b_b1t], writes=[b_h])
            for sub in range(NSUB):
                g = t * NSUB + sub
                if sub % 2 == 0:
                    outs = ((ps_gA[:, 0:512], b_gA), (ps_gA[:, 512:1024], b_gA))
                else:
                    outs = ((ps_gB0[:, :], b_gB0), (ps_gB1[:, :], b_gB1))
                for oh in range(2):
                    po, bpo = outs[oh]
                    for hc in range(8):
                        bw2u = b_w2q0h[hc // 4] if j == 0 else bw2
                        mm(po, h_sb[:, hc * 512 + sub * 128: hc * 512 + sub * 128 + 128],
                           w2q4[:, par, hc, oh * 512:(oh + 1) * 512],
                           start=(hc == 0), stop=(hc == 7 and qtr != 0),
                           reads=[b_h, bw2u], writes=[bpo])
                    if qtr == 0:
                        mm(po, onesb[0:1, 0:128], b2e[0:1, epar * D + oh * 512: epar * D + (oh + 1) * 512],
                           start=False, stop=True, reads=[b_onesb, b_b2e[epar]], writes=[bpo])
                if sub % 2 == 0:
                    S.add("vector",
                          lambda h, g=g, e=e: h.scalar_tensor_tensor(
                              g_pre[:, g * D:(g + 1) * D], ps_gA[:, :], C_all[:, g * NE + e: g * NE + e + 1],
                              g_pre[:, g * D:(g + 1) * D], ALU.mult, ALU.add),
                          reads=[b_gA, b_C, b_gpre_ch[g]], writes=[b_gpre_ch[g]])
                else:
                    S.add("vector",
                          lambda h, g=g, e=e: h.scalar_tensor_tensor(
                              g_pre[:, g * D:g * D + 512], ps_gB0[:, :], C_all[:, g * NE + e: g * NE + e + 1],
                              g_pre[:, g * D:g * D + 512], ALU.mult, ALU.add),
                          reads=[b_gB0, b_C, b_gpre_ch[g]], writes=[b_gpre_ch[g]])
                    S.add("vector",
                          lambda h, g=g, e=e: h.scalar_tensor_tensor(
                              g_pre[:, g * D + 512:(g + 1) * D], ps_gB1[:, :], C_all[:, g * NE + e: g * NE + e + 1],
                              g_pre[:, g * D + 512:(g + 1) * D], ALU.mult, ALU.add),
                          reads=[b_gB1, b_C, b_gpre_ch[g]], writes=[b_gpre_ch[g]])
        if j + 2 <= NE * 4 - 1:
            w_load(j + 2)

    if "gpre" in dbg:
        dma("scalar", dbg["gpre"].ap().rearrange("g p f -> p g f"),
            g_pre.ap().rearrange("p (g f) -> p g f", g=NCH), reads=b_gpre_ch)
    if "C" in dbg:
        dma("scalar", dbg["C"][:, :], C_all[:, :], reads=[b_C])
    if "lg" in dbg:
        dma("scalar", dbg["lg"][:, :], lgbuf[:, :], reads=[b_lg])
    if "Sc" in dbg:
        dma("scalar", dbg["Sc"][:, :], S_all[:, :], reads=[b_S])

    fence2 = S.snapshot()

    a3 = Arena(ARENA_BASE)
    WoT, b_WoT = a3.take("WoT", [128, 8 * D], BF16)
    wstage2, b_wstage2 = a3.take("wstage2", [128, D], BF16)
    kvfull, b_kvfull = a3.take("kvfull", [128, 1040], F32)
    kvm, b_kvm = a3.take("kvm", [128, 520], F32)
    bor, b_bor = a3.take("bor", [1, D], BF16)
    kvaug, b_kvaug = a3.take("kvaug", [128, 8 * 66], BF16)
    q_sb, _ = a3.take("q_sb", [128, 2 * 8 * 512], BF16)
    b_qsb = [Buf("qsb0"), Buf("qsb1")]
    xh2, _ = a3.take("xh2", [128, 2 * 2 * D], F32)
    b_xh2 = [Buf("xh2_0"), Buf("xh2_1")]
    aztm, _ = a3.take("aztm", [128, 2 * 1024], BF16)
    b_aztm = [Buf("aztm0"), Buf("aztm1")]
    azf, _ = a3.take("azf", [128, 2 * 8 * 512], BF16)
    b_azf = [Buf("azf0"), Buf("azf1")]
    den_sb, b_den = a3.take("den_sb", [128, 16], F32)
    zs_sb, b_zs = a3.take("zs_sb", [128, 16], F32)
    out_sb, b_out = a3.take("out_sb", [128, 4 * D], F32)
    gs_sb, b_gs = a3.take("gs_sb", [128, D], F32)
    mres, b_mres = a3.take("mres", [128, 512], F32)

    ps_atA, b_atA = P512[0], bP512[0]
    ps_atB, b_atB = P512[1], bP512[1]
    b_tr2 = [bP1024, Buf("ps1024_b1")]

    dma("gpsimd", bor[:, :], bo_row_in[:, :], writes=[b_bor], extra=fence2)
    for oc in range(8):
        dma("gpsimd", wstage2[:, :], W_in["Wo"][oc * 128:(oc + 1) * 128, :],
            reads=[b_wstage2], writes=[b_wstage2], extra=fence2)
        for ic in range(8):
            psb, bps = (ps_atA, b_atA) if ic % 2 == 0 else (ps_atB, b_atB)
            pbf = psb.ap().bitcast(BF16)
            tr(pbf[:, 0:128], wstage2[:, ic * 128:(ic + 1) * 128], idb[:, :],
               reads=[b_wstage2, b_idb], writes=[bps], extra=fence2)
            S.add("scalar",
                  lambda h, c=ic * D + oc * 128, p=pbf: h.copy(WoT[:, c:c + 128], p[:, 0:128]),
                  reads=[bps], writes=[b_WoT])

    dma("gpsimd", kvfull.ap().rearrange("p (b f) -> p b f", b=2),
        cc_out.ap().rearrange("(b p) f -> p b f", p=128),
        reads=[b_ccout], writes=[b_kvfull], extra=[cc_op] + list(fence2))
    if "kvfull" in dbg:
        dma("scalar", dbg["kvfull"][:, :], kvfull[:, :], reads=[b_kvfull])
    S.add("vector", lambda h: h.tensor_scalar(kvm[:, :], kvfull[:, 0:520], bmb[:, 0:1], None, ALU.mult),
          reads=[b_kvfull, b_bmb], writes=[b_kvm])
    S.add("vector", lambda h: h.scalar_tensor_tensor(kvm[:, :], kvfull[:, 520:1040], bmb[:, 1:2], kvm[:, :], ALU.mult, ALU.add),
          reads=[b_kvfull, b_bmb, b_kvm], writes=[b_kvm])
    for h_ in range(NH):
        pb, fo, sl = (h_ % 2) * 64, (h_ // 2) * 64, h_ // 2
        vop("tensor_copy", kvaug[pb:pb + 64, sl * 66:sl * 66 + 64],
            kvm[pb:pb + 64, fo:fo + 64], reads=[b_kvm], writes=[b_kvaug])
        vop("tensor_copy", kvaug[pb:pb + 64, sl * 66 + 64:sl * 66 + 65],
            kvm[pb:pb + 64, 512 + sl:512 + sl + 1], reads=[b_kvm], writes=[b_kvaug])

    q4 = q_sb.ap().rearrange("p (a c f) -> p a c f", a=2, c=8)
    xh2v = xh2.ap().rearrange("p (a u f) -> p a u f", a=2, u=2)

    for t in (range(NTILE) if "attn" not in ablate else ()):
        qpar = t % 2
        azp = t % 2
        if t == 0:
            dma("sync", q4[:, 0, :, :], q_spill[0].rearrange("(c p) f -> p c f", p=128),
                reads=[b_qsp, b_qsb[0]], writes=[b_qsb[0]], extra=fence2)
            for hf in range(2):
                dma("sync", xh2v[:, hf, :, :],
                    x_in[hf * 256:(hf + 1) * 256, :].rearrange("(u p) f -> p u f", p=128),
                    reads=[b_xh2[hf]], writes=[b_xh2[hf]], extra=fence2)

        def p3_att(sub):
            pA, bA = (P512[0], bP512[0]) if sub % 2 == 0 else (P512[2], bP512[2])
            pB, bB = (P512[1], bP512[1]) if sub % 2 == 0 else (P512[3], bP512[3])
            for h_ in range(NH):
                qrow = (h_ % 2) * 64
                sl = h_ // 2
                qT = q4[qrow:qrow + 64, qpar, sl, sub * 128:(sub + 1) * 128]
                po, bo_ = (pA, bA) if h_ % 2 == 0 else (pB, bB)
                mm(po[:, sl * 64:sl * 64 + 64], qT,
                   kvaug[qrow:qrow + 64, sl * 66:sl * 66 + 64],
                   start=True, stop=True, reads=[b_qsb[qpar], b_kvaug], writes=[bo_])

        def p3_den(sub):
            pD0, bD0 = P512[4], bP512[4]
            pD1, bD1 = P512[5], bP512[5]
            for h_ in range(NH):
                qrow = (h_ % 2) * 64
                sl = h_ // 2
                qT = q4[qrow:qrow + 64, qpar, sl, sub * 128:(sub + 1) * 128]
                pd, bd = (pD0, bD0) if h_ % 2 == 0 else (pD1, bD1)
                mm(pd[:, sub * 8 + sl:sub * 8 + sl + 1],
                   qT, kvaug[qrow:qrow + 64, sl * 66 + 64:sl * 66 + 65],
                   start=True, stop=True, reads=[b_qsb[qpar], b_kvaug], writes=[bd])

        def p3_z(sub):
            pA, bA = (P512[0], bP512[0]) if sub % 2 == 0 else (P512[2], bP512[2])
            pB, bB = (P512[1], bP512[1]) if sub % 2 == 0 else (P512[3], bP512[3])
            pD0, bD0 = P512[4], bP512[4]
            pD1, bD1 = P512[5], bP512[5]
            S.add("vector", lambda h, pD0=pD0, sub=sub: h.tensor_scalar_add(den_sb[:, 0:8], pD0[:, sub * 8:sub * 8 + 8], 1e-6),
                  reads=[bD0], writes=[b_den])
            S.add("vector", lambda h, pD1=pD1, sub=sub: h.tensor_scalar_add(den_sb[:, 8:16], pD1[:, sub * 8:sub * 8 + 8], 1e-6),
                  reads=[bD1], writes=[b_den])
            S.add("vector", lambda h: h.reciprocal(zs_sb[:, :], den_sb[:, :]),
                  reads=[b_den], writes=[b_zs])
            for h_ in range(NH):
                sl = h_ // 2
                po, bo_ = (pA, bA) if h_ % 2 == 0 else (pB, bB)
                src = po[:, sl * 64:sl * 64 + 64]
                zcol = sl + 8 * (h_ % 2)
                dst = aztm[:, (sub % 2) * 1024 + h_ * 64:(sub % 2) * 1024 + (h_ + 1) * 64]
                S.add("vector", lambda h, d=dst, sr=src, zc=zcol: h.tensor_scalar(d, sr, zs_sb[:, zc:zc + 1], None, ALU.mult),
                      reads=[bo_, b_zs], writes=[b_aztm[sub % 2]])

        def p3_tr(sub):
            for ic in range(8):
                trbank = ic % 2
                pt = P1024.ap().bitcast(BF16)[:, trbank * 1024 + 0:trbank * 1024 + 128]
                tr(pt, aztm[:, (sub % 2) * 1024 + ic * 128:(sub % 2) * 1024 + (ic + 1) * 128], idb[:, :],
                   reads=[b_aztm[sub % 2], b_idb], writes=[b_tr2[trbank]])
                col = ic * TILE + sub * 128
                S.add("scalar", lambda h, c=col, p=pt, azp=azp: h.copy(
                      azf[:, azp * 4096 + c: azp * 4096 + c + 128], p),
                      reads=[b_tr2[trbank]], writes=[b_azf[azp]])

        if "p3cb" not in ablate:
            p3_att(0)
            p3_den(0)
            p3_att(1)
            for sub in range(NSUB):
                if "p3z" not in ablate:
                    p3_z(sub)
                if sub + 1 < NSUB:
                    p3_den(sub + 1)
                if sub + 2 < NSUB:
                    p3_att(sub + 2)
                if "p3z" not in ablate:
                    p3_tr(sub)

        if t + 1 < NTILE:
            dma("sync", q4[:, (t + 1) % 2, :, :], q_spill[t + 1].rearrange("(c p) f -> p c f", p=128),
                reads=[b_qsp, b_qsb[(t + 1) % 2]], writes=[b_qsb[(t + 1) % 2]])
        for sub in (range(NSUB) if "p3wo" not in ablate else ()):
            g = t * NSUB + sub
            par = (t * 2 + sub // 2) % 2
            S.add("scalar",
                  lambda h, g=g: h.activation(gs_sb[:, :], g_pre[:, g * D:(g + 1) * D], AF.Silu,
                                              scale=S_all[:, g:g + 1]),
                  reads=[b_gpre_ch[g], b_S], writes=[b_gs])
            for oh in range(2):
                pw, bpw = P512[sub], bP512[sub]
                for ic in range(8):
                    mm(pw[:, :], azf[:, azp * 4096 + ic * TILE + sub * 128: azp * 4096 + ic * TILE + sub * 128 + 128],
                       WoT[:, ic * D + oh * 512:(ic) * D + (oh + 1) * 512],
                       start=(ic == 0), stop=False, reads=[b_azf[azp], b_WoT], writes=[bpw])
                mm(pw[:, :], onesb[0:1, 0:128], bor[0:1, oh * 512:(oh + 1) * 512],
                   start=False, stop=True, reads=[b_onesb, b_bor], writes=[bpw])
                S.add("vector", lambda h, pw=pw, oh=oh: h.tensor_mul(
                      mres[:, :], pw[:, :], gs_sb[:, oh * 512:(oh + 1) * 512]),
                      reads=[bpw, b_gs], writes=[b_mres])
                S.add("vector",
                      lambda h, sub=sub, par=par, oh=oh: h.tensor_add(
                          out_sb[:, sub * D + oh * 512: sub * D + (oh + 1) * 512],
                          xh2v[:, par, sub % 2, oh * 512:(oh + 1) * 512], mres[:, :]),
                      reads=[b_xh2[par], b_mres], writes=[b_out])
        dma("sync", y_out[t * TILE:(t + 1) * TILE, :].rearrange("(s p) f -> p s f", p=128),
            out_sb.ap().rearrange("p (s f) -> p s f", s=NSUB),
            reads=[b_out], writes=[b_y])
        if t + 1 < NTILE:
            for hf in range(2):
                dma("sync", xh2v[:, hf, :, :],
                    x_in[(t + 1) * TILE + hf * 256: (t + 1) * TILE + (hf + 1) * 256, :].rearrange("(u p) f -> p u f", p=128),
                    reads=[b_xh2[hf]], writes=[b_xh2[hf]])

    S.add("gpsimd", lambda h: h.nop(), extra_waits=S.snapshot())

    guards = S.emit(nc)
    return nc


def kernel(**inputs):
    x = np.ascontiguousarray(np.asarray(inputs["L0_attn_hidden_states"], dtype=np.float32).reshape(2 * 16384, D))
    Wq = np.ascontiguousarray(np.asarray(inputs["L0_attn_Wq"], dtype=np.float32))
    Wk = np.ascontiguousarray(np.asarray(inputs["L0_attn_Wk"], dtype=np.float32))
    Wv = np.ascontiguousarray(np.asarray(inputs["L0_attn_Wv"], dtype=np.float32))
    Wo = np.ascontiguousarray(np.asarray(inputs["L0_attn_Wo"], dtype=np.float32))
    bq = np.asarray(inputs["L0_attn_bq"], dtype=np.float32)
    bk = np.asarray(inputs["L0_attn_bk"], dtype=np.float32)
    bv = np.asarray(inputs["L0_attn_bv"], dtype=np.float32)
    bo = np.asarray(inputs["L0_attn_bo"], dtype=np.float32)
    wg_red = np.ascontiguousarray(np.asarray(inputs["L0_ffn_wg_red"], dtype=np.float32))
    wg = np.ascontiguousarray(np.asarray(inputs["L0_ffn_wg"], dtype=np.float32))
    w1 = np.ascontiguousarray(np.asarray(inputs["L0_ffn_w1"], dtype=np.float32))
    b1 = np.asarray(inputs["L0_ffn_b1"], dtype=np.float32)
    w2 = np.ascontiguousarray(np.asarray(inputs["L0_ffn_w2"], dtype=np.float32))
    b2 = np.asarray(inputs["L0_ffn_b2"], dtype=np.float32)

    bq_t = np.ascontiguousarray(bq.reshape(8, 128).T)
    bo_t = np.ascontiguousarray(bo.reshape(8, 128).T)
    bk_row = np.ascontiguousarray(bk.reshape(1, D))
    bo_row = np.ascontiguousarray(bo.reshape(1, D))
    bv_row = np.ascontiguousarray(bv.reshape(1, D))
    b1_t = np.ascontiguousarray(b1.reshape(NE, 32, 128).transpose(2, 1, 0).reshape(128, 32 * NE))
    b2row = np.ascontiguousarray(b2.reshape(1, NE * D))
    ident = np.eye(128, dtype=np.float32)
    ones_row = np.ones((1, 512), dtype=np.float32)
    ones_col = np.ones((128, 1), dtype=np.float32)
    pw2 = (2.0 ** -np.arange(NE, dtype=np.float32)).reshape(1, NE)

    nc = build_program(debug=tuple(DEBUG), ablate=frozenset(ABLATE))

    in_maps = []
    for c in range(NCORE):
        bmask = np.zeros((1, 2), dtype=np.float32)
        bmask[0, c // 4] = 1.0
        in_maps.append({
            "x": x[c * TOK:(c + 1) * TOK],
            "Wq": Wq, "Wk": Wk, "Wv": Wv, "Wo": Wo,
            "bq_t": bq_t, "bo_t": bo_t, "bk_row": bk_row, "bv_row": bv_row, "bo_row": bo_row,
            "wg_red": wg_red, "wg": wg, "w1": w1, "b1_t": b1_t, "w2": w2, "b2row": b2row,
            "ident": ident, "ones_row": ones_row, "ones_col": ones_col,
            "pw2": pw2, "bmask": bmask,
        })

    res = run_bass_kernel_spmd(nc, in_maps, core_ids=list(range(NCORE)))
    out = np.concatenate([res.results[c]["y"] for c in range(NCORE)], axis=0)
    if DEBUG:
        kernel.debug_results = res.results
    return out.reshape(2, 16384, D)


if __name__ == "__main__":
    build_program()
    print("build ok")


# version 13
# speedup vs baseline: 2.4323x; 1.0016x over previous
import numpy as np
import ml_dtypes
import concourse.bass as bass
import concourse.mybir as mybir
from concourse.bass_utils import run_bass_kernel_spmd

F32 = mybir.dt.float32
BF16 = mybir.dt.bfloat16
F16 = mybir.dt.float16
AF = mybir.ActivationFunctionType
ALU = mybir.AluOpType
AX = mybir.AxisListType

NCORE = 8
TOK = 4096
D = 1024
NH = 16
NE = 8
HID = 4096
TILE = 512
NTILE = TOK // TILE
NSUB = TILE // 128
NCH = TOK // 128

DEBUG = []
ABLATE = set()


class Buf:
    __slots__ = ("name", "w", "readers")

    def __init__(self, name):
        self.name = name
        self.w = None
        self.readers = {}


class Op:
    __slots__ = ("eng", "fn", "waits", "dom", "inc", "flagged", "count_after", "i", "eng_seq", "raw")


RAW_K = 4


class Sched:
    def __init__(self):
        self.plan = []
        self.eng_n = {}

    def add(self, eng, fn, reads=(), writes=(), dma=False, cc=False, extra_waits=()):
        op = Op()
        op.eng = eng
        op.fn = fn
        op.dom = eng + "_dma" if dma else ("cc" if cc else eng)
        op.inc = 16 if dma else 1
        op.flagged = False
        op.count_after = None
        op.i = len(self.plan)
        op.eng_seq = self.eng_n.get(eng, 0)
        self.eng_n[eng] = op.eng_seq + 1
        ws = {}
        rw = {}

        def upd(d, w):
            c = d.get(w.dom)
            if c is None or w.i > c.i:
                d[w.dom] = w

        for b in reads:
            if b.w is not None:
                upd(ws, b.w)
                upd(rw, b.w)
        for b in writes:
            for r in b.readers.values():
                upd(ws, r)
            if b.w is not None:
                upd(ws, b.w)
        for w in extra_waits:
            if w is not None:
                upd(ws, w)
        waits = set(ws.values())
        raw = set(rw.values())
        waits.discard(op)
        raw.discard(op)
        for w in raw:
            waits.add(w)
        op.waits = sorted(waits, key=lambda o: o.i)
        op.raw = raw
        for b in reads:
            b.readers[op.dom] = op
        for b in writes:
            b.w = op
            b.readers = {}
        self.plan.append(op)
        return op

    def snapshot(self):
        last = {}
        for op in self.plan:
            last[op.dom] = op
        return list(last.values())

    def emit(self, nc):
        for op in self.plan:
            if op.dom.endswith("_dma") or op.dom == "cc":
                op.flagged = True
        def need_wait(op, w):
            if w.dom != op.eng:
                return True
            if op.eng == "tensor":
                return False
            return (op.eng_seq - w.eng_seq) <= RAW_K

        for op in self.plan:
            for w in op.waits:
                if need_wait(op, w):
                    w.flagged = True
        counts = {}
        for op in self.plan:
            if op.flagged:
                counts[op.dom] = counts.get(op.dom, 0) + op.inc
                op.count_after = counts[op.dom]
        doms = set(counts.keys())
        sems = {}
        guards = []
        for d in sorted(doms):
            g = nc.semaphore("sem_" + d)
            sems[d] = g.__enter__()
            guards.append(g)

        engs = ("tensor", "vector", "scalar", "gpsimd", "sync")
        per_eng = {e: [op for op in self.plan if op.eng == e] for e in engs}

        def mk_body(eng):
            ops = per_eng[eng]

            def body(h):
                hw = {}
                for op in ops:
                    for w in op.waits:
                        if not need_wait(op, w):
                            continue
                        assert w.flagged, (w.i, op.i)
                        tgt = w.count_after
                        if hw.get(w.dom, 0) >= tgt:
                            continue
                        h.wait_ge(sems[w.dom], tgt)
                        hw[w.dom] = tgt
                    inst = op.fn(h)
                    if op.flagged:
                        inst.then_inc(sems[op.dom], op.inc)
            return body

        with nc.Block() as block:
            for eng in engs:
                if per_eng[eng]:
                    getattr(block, eng)(mk_body(eng))
        return guards


DT_SIZE = {F32: 4, BF16: 2, F16: 2}


def build_program(debug=(), ablate=frozenset()):
    nc = bass.Bass("TRN2", target_bir_lowering=False)
    S = Sched()

    x_in = nc.dram_tensor("x", [TOK, D], F32, kind="ExternalInput")
    W_in = {w: nc.dram_tensor(w, [D, D], F32, kind="ExternalInput") for w in ("Wq", "Wk", "Wv", "Wo")}
    bq_t_in = nc.dram_tensor("bq_t", [128, 8], F32, kind="ExternalInput")
    bo_t_in = nc.dram_tensor("bo_t", [128, 8], F32, kind="ExternalInput")
    bk_row_in = nc.dram_tensor("bk_row", [1, D], F32, kind="ExternalInput")
    bo_row_in = nc.dram_tensor("bo_row", [1, D], F32, kind="ExternalInput")
    bv_row_in = nc.dram_tensor("bv_row", [1, D], F32, kind="ExternalInput")
    wgred_in = nc.dram_tensor("wg_red", [16, D], F32, kind="ExternalInput")
    wg_in = nc.dram_tensor("wg", [NE, 16], F32, kind="ExternalInput")
    w1_in = nc.dram_tensor("w1", [NE, D, HID], F32, kind="ExternalInput")
    b1_t_in = nc.dram_tensor("b1_t", [128, 32 * NE], F32, kind="ExternalInput")
    w2_in = nc.dram_tensor("w2", [NE, HID, D], F32, kind="ExternalInput")
    b2row_in = nc.dram_tensor("b2row", [1, NE * D], F32, kind="ExternalInput")
    ident_in = nc.dram_tensor("ident", [128, 128], F32, kind="ExternalInput")
    ones_in = nc.dram_tensor("ones_row", [1, 512], F32, kind="ExternalInput")
    onesc_in = nc.dram_tensor("ones_col", [128, 1], F32, kind="ExternalInput")
    pw2_in = nc.dram_tensor("pw2", [1, NE], F32, kind="ExternalInput")
    bmask_in = nc.dram_tensor("bmask", [1, 2], F32, kind="ExternalInput")

    y_out = nc.dram_tensor("y", [TOK, D], F32, kind="ExternalOutput")

    xn_spill = nc.dram_tensor("xn_spill", [NTILE, D, TILE], BF16)
    q_spill = nc.dram_tensor("q_spill", [NTILE, D, TILE], BF16)
    cc_in = nc.dram_tensor("cc_in", [2 * 128, 520], F32)
    cc_out = nc.dram_tensor("cc_out", [2 * 128, 520], F32)

    dbg = {}
    if "xn" in debug:
        dbg["xn"] = nc.dram_tensor("dbg_xn", [NTILE, D, TILE], BF16, kind="ExternalOutput")
    if "q" in debug:
        dbg["q"] = nc.dram_tensor("dbg_q", [NTILE, D, TILE], BF16, kind="ExternalOutput")
    if "kv" in debug:
        dbg["kv"] = nc.dram_tensor("dbg_kv", [128, 520], F32, kind="ExternalOutput")
    if "kvfull" in debug:
        dbg["kvfull"] = nc.dram_tensor("dbg_kvfull", [128, 1040], F32, kind="ExternalOutput")
    if "C" in debug:
        dbg["C"] = nc.dram_tensor("dbg_C", [128, NCH * NE], F32, kind="ExternalOutput")
    if "Sc" in debug:
        dbg["Sc"] = nc.dram_tensor("dbg_Sc", [128, NCH], F32, kind="ExternalOutput")
    if "gpre" in debug:
        dbg["gpre"] = nc.dram_tensor("dbg_gpre", [NCH, 128, D], F16, kind="ExternalOutput")
    if "xr" in debug:
        dbg["xr"] = nc.dram_tensor("dbg_xr", [NTILE, 16, TILE], F32, kind="ExternalOutput")
    if "lg" in debug:
        dbg["lg"] = nc.dram_tensor("dbg_lg", [128, NCH * NE], F32, kind="ExternalOutput")
    if "lnraw" in debug:
        dbg["lnraw"] = nc.dram_tensor("dbg_lnraw", [128, D], F32, kind="ExternalOutput")
        dbg["lnst"] = nc.dram_tensor("dbg_lnst", [128, 8], F32, kind="ExternalOutput")
    if "ktm" in debug:
        dbg["ktm"] = nc.dram_tensor("dbg_ktm", [128, D], BF16, kind="ExternalOutput")
        dbg["vtm"] = nc.dram_tensor("dbg_vtm", [128, D], BF16, kind="ExternalOutput")

    SB_BASE = (nc.sbuf_base + 63) & ~63
    SB_TOP = nc.sbuf_top

    class Arena:
        def __init__(self, base):
            self.off = base
            self.hi = base

        def take(self, name, shape, dtype):
            nbytes = int(np.prod(shape[1:])) * DT_SIZE[dtype]
            off = (self.off + 63) & ~63
            t = nc.alloc_sbuf_tensor_at(name, list(shape), dtype, offset=off)
            self.off = off + nbytes
            self.hi = max(self.hi, self.off)
            assert self.off <= SB_TOP, (name, self.off, SB_TOP)
            return t, Buf(name)

    pers = Arena(SB_BASE)
    g_pre, b_gpre = pers.take("g_pre", [128, NCH * D], F16)
    b_gpre_ch = [Buf(f"g_pre{g}") for g in range(NCH)]
    C_all, b_C = pers.take("C_all", [128, NCH * NE], F32)
    S_all, b_S = pers.take("S_all", [128, NCH], F32)
    b1t, b_b1t = pers.take("b1t", [128, 32 * NE], F32)
    bqt, b_bqt = pers.take("bqt", [128, 8], F32)
    bot, b_bot = pers.take("bot", [128, 8], F32)
    idf, b_idf = pers.take("idf", [128, 128], F32)
    idb, b_idb = pers.take("idb", [128, 128], BF16)
    onesf, b_onesf = pers.take("onesf", [1, 512], F32)
    onesb, b_onesb = pers.take("onesb", [1, 512], BF16)
    onescb, b_onescb = pers.take("onescb", [128, 1], BF16)
    pw2b, b_pw2 = pers.take("pw2b", [128, NE], F32)
    bmb, b_bmb = pers.take("bmb", [128, 2], F32)
    ARENA_BASE = (pers.off + 63) & ~63

    b_ccin, b_ccout = Buf("cc_in"), Buf("cc_out")
    b_xnsp, b_qsp = Buf("xn_spill"), Buf("q_spill")
    b_y = Buf("y")

    P512 = []
    bP512 = []
    for i in range(6):
        t = nc.alloc_psum_tensor(f"ps{i}", [128, 512], F32)
        P512.append(t)
        bP512.append(Buf(f"ps{i}"))
    P1024 = nc.alloc_psum_tensor("ps1024", [128, 1024], F32)
    bP1024 = Buf("ps1024")

    def dma(eng, dst, src, reads=(), writes=(), extra=()):
        return S.add(eng, lambda h, d=dst, s=src: h.dma_start(out=d, in_=s),
                     reads=reads, writes=writes, dma=True, extra_waits=extra)

    dma("gpsimd", idb[:, :], ident_in[:, :], writes=[b_idb])
    dma("gpsimd", onesb[:, :], ones_in[:, :], writes=[b_onesb])
    dma("gpsimd", onescb[:, :], onesc_in[:, :], writes=[b_onescb])
    dma("sync", b1t[:, :], b1_t_in[:, :], writes=[b_b1t])
    dma("sync", bqt[:, :], bq_t_in[:, :], writes=[b_bqt])
    dma("sync", bot[:, :], bo_t_in[:, :], writes=[b_bot])
    dma("sync", idf[:, :], ident_in[:, :], writes=[b_idf])
    dma("sync", onesf[:, :], ones_in[:, :], writes=[b_onesf])
    dma("sync", pw2b[:, :], pw2_in.ap().broadcast_to([128, NE]), writes=[b_pw2])
    dma("sync", bmb[:, :], bmask_in.ap().broadcast_to([128, 2]), writes=[b_bmb])

    def mm(out, lhsT, rhs, start, stop, reads, writes, skip_gc=True, extra=()):
        return S.add("tensor",
                     lambda h, o=out, l=lhsT, r=rhs, st=start, sp=stop, sg=skip_gc:
                     h.matmul(o, l, r, start=st, stop=sp, skip_group_check=sg),
                     reads=reads, writes=writes, extra_waits=extra)

    def tr(out, in_, ident, reads, writes, extra=()):
        return S.add("tensor", lambda h, o=out, i=in_, d=ident: h.transpose(o, i, d),
                     reads=reads, writes=writes, extra_waits=extra)

    def act(out, in_, func, bias=0.0, scale=1.0, reads=(), writes=(), extra=()):
        return S.add("scalar",
                     lambda h, o=out, i=in_, f=func, b=bias, sc=scale: h.activation(o, i, f, bias=b, scale=sc),
                     reads=reads, writes=writes, extra_waits=extra)

    def vop(name, *aps, reads=(), writes=(), extra=(), eng="vector"):
        def fn(h, n=name, a=aps):
            return getattr(h, n)(*a)
        return S.add(eng, fn, reads=reads, writes=writes, extra_waits=extra)

    a1 = Arena(ARENA_BASE)
    WqT, b_WqT = a1.take("WqT", [128, 8 * D], BF16)
    WkT, b_WkT = a1.take("WkT", [128, 8 * D], BF16)
    WvT, b_WvT = a1.take("WvT", [128, 8 * D], BF16)
    wstage, b_wstage = a1.take("wstage", [128, D], BF16)
    wgredT, b_wgredT = a1.take("wgredT", [128, 8 * 16], F32)
    wgr_st, b_wgr_st = a1.take("wgr_st", [16, D], F32)
    wgu, b_wgu = a1.take("wgu", [NE, 16], F32)
    wguT, b_wguT = a1.take("wguT", [16, NE], F32)
    wgtmp, b_wgtmp = a1.take("wgtmp", [NE, 20], F32)
    bkr, b_bkr = a1.take("bkr", [1, D], BF16)
    bvr, b_bvr = a1.take("bvr", [1, D], BF16)
    xh, _ = a1.take("xh", [128, 2 * D], F32)
    b_xh = [Buf("xh0"), Buf("xh1")]
    ctr2, _ = a1.take("ctr2", [128, 2 * D], F32)
    b_ctr2 = [Buf("ctr0"), Buf("ctr1")]
    scr1, b_scr1 = a1.take("scr1", [128, D], F32)
    scr22, _ = a1.take("scr22", [128, 2 * D], F32)
    b_scr22 = [Buf("xnl0"), Buf("xnl1")]
    lnst2, _ = a1.take("lnst2", [128, 16], F32)
    b_lnst2 = [Buf("lnst0"), Buf("lnst1")]
    xnf_bf2, _ = a1.take("xnf_bf2", [128, 2 * 8 * TILE], BF16)
    b_xnbf2 = [Buf("xnbf0"), Buf("xnbf1")]
    xnf_f32, b_xnf32 = a1.take("xnf_f32", [128, 8 * 256], F32)
    q_tile, b_q = a1.take("q_tile", [128, 8 * TILE], BF16)
    k_tm, b_ktm = a1.take("k_tm", [128, D], BF16)
    v_tm, b_vtm = a1.take("v_tm", [128, D], BF16)
    xr_sb, _ = a1.take("xr_sb", [16, 2 * TILE], F32)
    b_xr2 = [Buf("xr0"), Buf("xr1")]
    gsc, b_gsc = a1.take("gsc", [128, 8 * 8], F32)
    qsc, b_qsc = a1.take("qsc", [128, D], F32)
    if "lg" in debug:
        lgbuf, b_lg = a1.take("lgbuf", [128, NCH * NE], F32)
    kvacc, b_kvacc = a1.take("kvacc", [128, 520], F32)
    contrib, b_contrib = a1.take("contrib", [128, 2 * 520], F32)

    ps_kvh, b_pkvh = P512[0], bP512[0]
    ps_ksum, b_pksum = P512[1], bP512[1]
    ps_trA, b_trA = P512[2], bP512[2]
    ps_trB, b_trB = P512[3], bP512[3]
    ps_mmA, b_mmA = P512[4], bP512[4]
    ps_mmB, b_mmB = P512[5], bP512[5]
    ps_kv2, b_pkv2 = P1024, bP1024

    dma("gpsimd", bkr[:, :], bk_row_in[:, :], writes=[b_bkr])
    dma("gpsimd", bvr[:, :], bv_row_in[:, :], writes=[b_bvr])

    def prep_wT(nm, WT, b_WT):
        for oc in range(8):
            dma("gpsimd", wstage[:, :], W_in[nm][oc * 128:(oc + 1) * 128, :],
                reads=(), writes=[b_wstage])
            for ic in range(8):
                psb, bps = (ps_trA, b_trA) if ic % 2 == 0 else (ps_trB, b_trB)
                pbf = psb.ap().bitcast(BF16)
                tr(pbf[:, 0:128], wstage[:, ic * 128:(ic + 1) * 128], idb[:, :],
                   reads=[b_wstage, b_idb], writes=[bps])
                S.add("scalar",
                      lambda h, w=WT, c=ic * D + oc * 128, p=pbf: h.copy(w[:, c:c + 128], p[:, 0:128]),
                      reads=[bps], writes=[b_WT])

    prep_wT("Wq", WqT, b_WqT)
    prep_wT("Wk", WkT, b_WkT)
    prep_wT("Wv", WvT, b_WvT)

    dma("sync", wgr_st[:, :], wgred_in[:, :], writes=[b_wgr_st])
    for ic in range(8):
        psb, bps = (ps_trA, b_trA) if ic % 2 == 0 else (ps_trB, b_trB)
        tr(psb[0:128, 0:16], wgr_st[0:16, ic * 128:(ic + 1) * 128], idf[0:16, 0:16],
           reads=[b_wgr_st, b_idf], writes=[bps])
        vop("tensor_copy", wgredT[:, ic * 16:(ic + 1) * 16], psb[0:128, 0:16],
            reads=[bps], writes=[b_wgredT])

    dma("sync", wgu[:, :], wg_in[:, :], writes=[b_wgu])
    act(wgtmp[:, 0:16], wgu[:, :], AF.Square, reads=[b_wgu], writes=[b_wgtmp])
    vop("reduce_sum", wgtmp[:, 16:17], wgtmp[:, 0:16], AX.X, reads=[b_wgtmp], writes=[b_wgtmp])
    act(wgtmp[:, 17:18], wgtmp[:, 16:17], AF.Sqrt, reads=[b_wgtmp], writes=[b_wgtmp])
    vop("tensor_scalar_max", wgtmp[:, 17:18], wgtmp[:, 17:18], 1e-4, reads=[b_wgtmp], writes=[b_wgtmp])
    vop("reciprocal", wgtmp[:, 18:19], wgtmp[:, 17:18], reads=[b_wgtmp], writes=[b_wgtmp])
    S.add("vector", lambda h: h.tensor_scalar(wgu[:, :], wgu[:, :], wgtmp[:, 18:19], None, ALU.mult),
          reads=[b_wgtmp], writes=[b_wgu])
    tr(ps_trA[0:16, 0:NE], wgu[:, :], idf[0:NE, 0:NE], reads=[b_wgu, b_idf], writes=[b_trA])
    vop("tensor_copy", wguT[:, :], ps_trA[0:16, 0:NE], reads=[b_trA], writes=[b_wguT])


    def xnf(t):
        return xnf_bf2.ap().rearrange("p (a c f) -> p a c f", a=2, c=8)[:, t % 2, :, :]

    def stage_ln(t):
        bxnbf = b_xnbf2[t % 2]
        if t == 0:
            for k0 in range(2):
                dma("sync", xh[:, (k0 % 2) * D:(k0 % 2 + 1) * D],
                    x_in[k0 * 128:(k0 + 1) * 128, :],
                    reads=[b_xh[k0 % 2]], writes=[b_xh[k0 % 2]])
        for s in range(NSUB):
            kg = t * NSUB + s
            par = kg % 2
            pp = s % 2
            xsub = xh[:, par * D:(par + 1) * D]
            bx = b_xh[par]
            ctr = ctr2[:, pp * D:(pp + 1) * D]
            b_ctr = b_ctr2[pp]
            scr2 = scr22[:, pp * D:(pp + 1) * D]
            b_scr2 = b_scr22[pp]
            lnst = lnst2[:, pp * 8:(pp + 1) * 8]
            b_lnst = b_lnst2[pp]
            vop("reduce_sum", lnst[:, 0:1], xsub, AX.X, reads=[bx], writes=[b_lnst])
            vop("tensor_scalar_mul", lnst[:, 1:2], lnst[:, 0:1], 1.0 / D, reads=[b_lnst], writes=[b_lnst])
            S.add("vector", lambda h, xs=xsub, ctr=ctr, lnst=lnst: h.tensor_scalar(ctr[:, :], xs, lnst[:, 1:2], None, ALU.subtract),
                  reads=[bx, b_lnst], writes=[b_ctr])
            act(scr2[:, :], ctr[:, :], AF.Square, reads=[b_ctr], writes=[b_scr2])
            vop("reduce_sum", lnst[:, 2:3], scr2[:, :], AX.X, reads=[b_scr2], writes=[b_lnst])
            act(lnst[:, 3:4], lnst[:, 2:3], AF.Ln, bias=np.float32(1e-5), scale=1.0 / D, reads=[b_lnst], writes=[b_lnst])
            act(lnst[:, 4:5], lnst[:, 3:4], AF.Exp, scale=-0.5, reads=[b_lnst], writes=[b_lnst])
            S.add("vector", lambda h, ctr=ctr, lnst=lnst, scr2=scr2: h.tensor_scalar(scr2[:, :], ctr[:, :], lnst[:, 4:5], None, ALU.mult),
                  reads=[b_ctr, b_lnst], writes=[b_scr2])
            for ic in range(8):
                psb, bps = (ps_trA, b_trA) if ic % 2 == 0 else (ps_trB, b_trB)
                slot = (ic // 2) % 2
                tr(psb[:, slot * 128:slot * 128 + 128], scr2[:, ic * 128:(ic + 1) * 128], idf[:, :],
                   reads=[b_scr2, b_idf], writes=[bps])
                colf = ic * 256 + pp * 128
                vop("tensor_copy", xnf_f32[:, colf:colf + 128], psb[:, slot * 128:slot * 128 + 128],
                    reads=[bps], writes=[b_xnf32])
                S.add("scalar", lambda h, t=t, c=ic * TILE + s * 128, cf=colf: h.copy(
                      xnf(t)[:, c // TILE, c % TILE: c % TILE + 128], xnf_f32[:, cf:cf + 128]),
                      reads=[b_xnf32], writes=[bxnbf])
            if s % 2 == 1 and "gating" not in ablate:
                hf2 = s // 2
                for ic in range(8):
                    mm(ps_mmA[0:16, hf2 * 256:hf2 * 256 + 256], wgredT[:, ic * 16:(ic + 1) * 16],
                       xnf_f32[:, ic * 256:(ic + 1) * 256],
                       start=(ic == 0), stop=(ic == 7), reads=[b_wgredT, b_xnf32], writes=[b_mmA])
                vop("tensor_copy", xr_sb[:, (t % 2) * TILE + hf2 * 256:(t % 2) * TILE + hf2 * 256 + 256],
                    ps_mmA[0:16, hf2 * 256:hf2 * 256 + 256],
                    reads=[b_mmA], writes=[b_xr2[t % 2]])
            if kg + 2 < NTILE * NSUB:
                k2 = kg + 2
                dma("sync", xh[:, (k2 % 2) * D:(k2 % 2 + 1) * D],
                    x_in[k2 * 128:(k2 + 1) * 128, :],
                    reads=[b_xh[k2 % 2]], writes=[b_xh[k2 % 2]])
        dma("scalar", xn_spill[t].rearrange("(c p) f -> p c f", p=128),
            xnf(t), reads=[bxnbf], writes=[b_xnsp])
        if "xn" in dbg:
            dma("scalar", dbg["xn"][t].rearrange("(c p) f -> p c f", p=128), xnf(t), reads=[bxnbf])

    def stage_proj(t):
        bxnbf = b_xnbf2[t % 2]
        xt = xnf(t)
        for oc in (range(8) if "q" not in ablate else ()):
            psq, bpsq = (ps_mmA, b_mmA) if oc % 2 == 0 else (ps_mmB, b_mmB)
            for ic in range(8):
                mm(psq[:, :], WqT[:, ic * D + oc * 128: ic * D + oc * 128 + 128],
                   xt[:, ic, :],
                   start=(ic == 0), stop=(ic == 7),
                   reads=[b_WqT, bxnbf], writes=[bpsq])
            act(scr1[:, 0:TILE], psq[:, :], AF.Relu, bias=bqt[:, oc:oc + 1],
                reads=[bpsq, b_bqt], writes=[b_scr1])
            act(qsc[:, 0:TILE], psq[:, :], AF.Exp, bias=bqt[:, oc:oc + 1],
                reads=[bpsq, b_bqt], writes=[b_qsc])
            S.add("vector",
                  lambda h, oc=oc: h.scalar_tensor_tensor(
                      q_tile[:, oc * TILE:(oc + 1) * TILE], qsc[:, 0:TILE], 1.0, scr1[:, 0:TILE],
                      ALU.min, ALU.add),
                  reads=[b_qsc, b_scr1], writes=[b_q])
        dma("scalar", q_spill[t].rearrange("(c p) f -> p c f", p=128),
            q_tile.ap().rearrange("p (c f) -> p c f", c=8),
            reads=[b_q], writes=[b_qsp])
        if "q" in dbg:
            dma("scalar", dbg["q"][t].rearrange("(c p) f -> p c f", p=128),
                q_tile.ap().rearrange("p (c f) -> p c f", c=8), reads=[b_q])

        def kv_proj(s):
            for half in range(2):
                for ic in range(8):
                    mm(ps_kv2[:, half * 512:(half + 1) * 512],
                       xt[:, ic, s * 128:s * 128 + 128],
                       WkT[:, ic * D + half * 512: ic * D + (half + 1) * 512],
                       start=(ic == 0), stop=False,
                       reads=[bxnbf, b_WkT], writes=[b_pkv2])
                mm(ps_kv2[:, half * 512:(half + 1) * 512],
                   onesb[0:1, 0:128], bkr[0:1, half * 512:(half + 1) * 512],
                   start=False, stop=True,
                   reads=[b_onesb, b_bkr], writes=[b_pkv2])
            act(scr1[:, :], ps_kv2[:, :], AF.Relu, reads=[b_pkv2], writes=[b_scr1])
            act(qsc[:, :], ps_kv2[:, :], AF.Exp, reads=[b_pkv2], writes=[b_qsc])
            S.add("vector", lambda h: h.scalar_tensor_tensor(
                  k_tm[:, :], qsc[:, :], 1.0, scr1[:, :], ALU.min, ALU.add),
                  reads=[b_qsc, b_scr1], writes=[b_ktm])
            for half in range(2):
                psv, bpsv = (ps_mmA, b_mmA) if half == 0 else (ps_mmB, b_mmB)
                for ic in range(8):
                    mm(psv[:, :],
                       xt[:, ic, s * 128:s * 128 + 128],
                       WvT[:, ic * D + half * 512: ic * D + (half + 1) * 512],
                       start=(ic == 0), stop=False,
                       reads=[bxnbf, b_WvT], writes=[bpsv])
                mm(psv[:, :], onesb[0:1, 0:128], bvr[0:1, half * 512:(half + 1) * 512],
                   start=False, stop=True, reads=[b_onesb, b_bvr], writes=[bpsv])
                vop("tensor_copy", v_tm[:, half * 512:(half + 1) * 512], psv[:, :],
                    reads=[bpsv], writes=[b_vtm])
            if "ktm" in dbg and t == 0 and s == 0:
                dma("scalar", dbg["ktm"][:, :], k_tm[:, :], reads=[b_ktm])
                dma("scalar", dbg["vtm"][:, :], v_tm[:, :], reads=[b_vtm])

        def kv_acc(s):
            for h in range(NH):
                pb, fo = (h % 2) * 64, (h // 2) * 64
                mm(ps_kvh[pb:pb + 64, fo:fo + 64],
                   k_tm[:, h * 64:(h + 1) * 64], v_tm[:, h * 64:(h + 1) * 64],
                   start=(s == 0 and h < 2),
                   stop=(s == NSUB - 1),
                   reads=[b_ktm, b_vtm], writes=[b_pkvh], skip_gc=True)
                mm(ps_ksum[pb:pb + 64, (h // 2):(h // 2) + 1],
                   k_tm[:, h * 64:(h + 1) * 64], onescb[:, 0:1],
                   start=(s == 0 and h < 2), stop=(s == NSUB - 1),
                   reads=[b_ktm, b_onescb], writes=[b_pksum], skip_gc=True)

        if "kv" not in ablate:
            for s in range(NSUB):
                kv_proj(s)
                kv_acc(s)

    def stage_tail(t):
        if "kv" in ablate:
            pass
        elif t == 0:
            vop("tensor_copy", kvacc[:, 0:512], ps_kvh[:, :], reads=[b_pkvh], writes=[b_kvacc])
            vop("tensor_copy", kvacc[:, 512:520], ps_ksum[:, 0:8], reads=[b_pksum], writes=[b_kvacc])
        else:
            vop("tensor_add", kvacc[:, 0:512], kvacc[:, 0:512], ps_kvh[:, :], reads=[b_pkvh, b_kvacc], writes=[b_kvacc])
            vop("tensor_add", kvacc[:, 512:520], kvacc[:, 512:520], ps_ksum[:, 0:8], reads=[b_pksum, b_kvacc], writes=[b_kvacc])
        if "xr" in dbg:
            dma("scalar", dbg["xr"][t], xr_sb[:, (t % 2) * TILE:(t % 2 + 1) * TILE], reads=[b_xr2[t % 2]])
        for s in (range(NSUB) if "gating" not in ablate else ()):
            g = t * NSUB + s
            mm(ps_trB[0:128, 0:NE], xr_sb[:, (t % 2) * TILE + s * 128:(t % 2) * TILE + (s + 1) * 128], wguT[:, :],
               start=True, stop=True, reads=[b_xr2[t % 2], b_wguT], writes=[b_trB])
            if "lg" in dbg:
                vop("tensor_copy", lgbuf[:, g * NE:(g + 1) * NE], ps_trB[0:128, 0:NE], reads=[b_trB], writes=[b_lg])
            vop("reduce_max", gsc[:, 0:1], ps_trB[0:128, 0:NE], AX.X, reads=[b_trB], writes=[b_gsc])
            S.add("vector", lambda h: h.tensor_scalar(gsc[:, 8:16], ps_trB[0:128, 0:NE], gsc[:, 0:1], None, ALU.subtract),
                  reads=[b_trB, b_gsc], writes=[b_gsc])
            act(gsc[:, 16:24], gsc[:, 8:16], AF.Exp, reads=[b_gsc], writes=[b_gsc])
            vop("reduce_sum", gsc[:, 1:2], gsc[:, 16:24], AX.X, reads=[b_gsc], writes=[b_gsc])
            S.add("vector", lambda h, g=g: h.reciprocal(S_all[:, g:g + 1], gsc[:, 1:2]),
                  reads=[b_gsc], writes=[b_S])
            S.add("vector", lambda h: h.tensor_scalar(gsc[:, 24:32], gsc[:, 8:16], 0.0, None, ALU.is_equal),
                  reads=[b_gsc], writes=[b_gsc])
            vop("tensor_mul", gsc[:, 32:40], gsc[:, 24:32], pw2b[:, :], reads=[b_gsc, b_pw2], writes=[b_gsc])
            vop("reduce_max", gsc[:, 2:3], gsc[:, 32:40], AX.X, reads=[b_gsc], writes=[b_gsc])
            S.add("vector", lambda h: h.tensor_scalar(gsc[:, 40:48], gsc[:, 32:40], gsc[:, 2:3], None, ALU.is_equal),
                  reads=[b_gsc], writes=[b_gsc])
            S.add("vector", lambda h, g=g: h.tensor_copy(C_all[:, g * NE:(g + 1) * NE], gsc[:, 40:48]),
                  reads=[b_gsc], writes=[b_C])

    stage_ln(0)
    for t in range(NTILE):
        stage_proj(t)
        if t + 1 < NTILE:
            stage_ln(t + 1)
        stage_tail(t)

    if True:
        S.add("vector", lambda h: h.tensor_scalar(contrib[:, 0:520], kvacc[:, :], bmb[:, 0:1], None, ALU.mult),
          reads=[b_kvacc, b_bmb], writes=[b_contrib])
    S.add("vector", lambda h: h.tensor_scalar(contrib[:, 520:1040], kvacc[:, :], bmb[:, 1:2], None, ALU.mult),
          reads=[b_kvacc, b_bmb], writes=[b_contrib])
    if "kv" in dbg:
        dma("scalar", dbg["kv"][:, :], kvacc[:, :], reads=[b_kvacc])
    dma("gpsimd", cc_in.ap().rearrange("(b p) f -> p b f", p=128),
        contrib.ap().rearrange("p (b f) -> p b f", b=2),
        reads=[b_contrib], writes=[b_ccin])
    fence1 = S.snapshot()
    cc_op = S.add("gpsimd",
                  lambda h: h.collective_compute("AllReduce", ALU.add,
                                                 replica_groups=[list(range(NCORE))],
                                                 ins=[cc_in.ap().opt()], outs=[cc_out.ap().opt()]),
                  reads=[b_ccin], writes=[b_ccout], cc=True)

    a2 = Arena(ARENA_BASE)
    xnr, b_xnr = a2.take("xnr", [128, 8 * 2048], BF16)
    b_xnr4 = [Buf(f"xnr{i}") for i in range(4)]
    xns, _ = a2.take("xns", [128, 2 * 8 * 512], BF16)
    b_xns = [Buf("xns0"), Buf("xns1")]
    w1q, _ = a2.take("w1q", [128, 2 * 8 * 1024], BF16)
    b_w1q = [Buf("w1q0"), Buf("w1q1")]
    w2q, _ = a2.take("w2q", [128, 2 * 8 * 1024], BF16)
    b_w2q = [Buf("w2q0"), Buf("w2q1")]
    b2e, _ = a2.take("b2e", [1, 2 * D], BF16)
    b_b2e = [Buf("b2e0"), Buf("b2e1")]
    h_sb, b_h = a2.take("h_sb", [128, 8 * 512], BF16)

    ps_hA, b_hA = P512[0], bP512[0]
    ps_hB, b_hB = P512[1], bP512[1]
    ps_gA, b_gA = P1024, bP1024
    ps_gB0, b_gB0 = P512[2], bP512[2]
    ps_gB1, b_gB1 = P512[3], bP512[3]

    xnr4 = xnr.ap().rearrange("p (c tt) -> p c tt", c=8)
    xns4 = xns.ap().rearrange("p (a c f) -> p a c f", a=2, c=8)
    w1q4 = w1q.ap().rearrange("p (a c f) -> p a c f", a=2, c=8)
    w2q4 = w2q.ap().rearrange("p (a c f) -> p a c f", a=2, c=8)

    S.add("vector", lambda h: h.memset(g_pre[:, :], 0.0), reads=[b_gpre],
          writes=[b_gpre] + b_gpre_ch, extra_waits=())

    for t in range(4):
        dma("sync" if t == 0 else "scalar", xnr4[:, :, t * TILE:(t + 1) * TILE],
            xn_spill[t].rearrange("(c p) f -> p c f", p=128),
            reads=[b_xnsp], writes=[b_xnr, b_xnr4[t]], extra=fence1)

    b_w1q0h = [Buf("w1q0a"), Buf("w1q0b")]
    b_w2q0h = [Buf("w2q0a"), Buf("w2q0b")]

    def w_load(j):
        e, qtr = divmod(j, 4)
        par = j % 2
        if j == 0:
            for hh in range(2):
                dma("gpsimd", w1q4[:, 0, :, hh * 512:(hh + 1) * 512],
                    w1_in[e][:, hh * 512:(hh + 1) * 512].rearrange("(c p) f -> p c f", p=128),
                    reads=[b_w1q[0]], writes=[b_w1q0h[hh]], extra=fence1)
            for hh in range(2):
                dma("gpsimd", w2q4[:, 0, hh * 4:(hh + 1) * 4, :],
                    w2_in[e][hh * 512:(hh + 1) * 512, :].rearrange("(c p) f -> p c f", p=128),
                    reads=[b_w2q[0]], writes=[b_w2q0h[hh]], extra=fence1)
        else:
            extra_w = [b_w1q0h[0].w, b_w1q0h[1].w] if j == 2 else []
            extra_w2 = [b_w2q0h[0].w, b_w2q0h[1].w] if j == 2 else []
            dma("gpsimd", w1q4[:, par, :, :],
                w1_in[e][:, qtr * 1024:(qtr + 1) * 1024].rearrange("(c p) f -> p c f", p=128),
                reads=[b_w1q[par]] + ([b_w1q0h[0], b_w1q0h[1]] if j == 2 else []),
                writes=[b_w1q[par]] + ([b_w1q0h[0], b_w1q0h[1]] if j == 2 else []), extra=fence1)
            dma("gpsimd", w2q4[:, par, :, :],
                w2_in[e][qtr * 1024:(qtr + 1) * 1024, :].rearrange("(c p) f -> p c f", p=128),
                reads=[b_w2q[par]] + ([b_w2q0h[0], b_w2q0h[1]] if j == 2 else []),
                writes=[b_w2q[par]] + ([b_w2q0h[0], b_w2q0h[1]] if j == 2 else []), extra=fence1)
        if qtr == 0:
            pe_ = e % 2
            dma("gpsimd", b2e[0:1, pe_ * D:(pe_ + 1) * D], b2row_in[0:1, e * D:(e + 1) * D],
                reads=[b_b2e[pe_]], writes=[b_b2e[pe_]], extra=fence1)

    if "ffn" not in ablate:
        w_load(0)
        w_load(1)

    for j in (range(NE * 4) if "ffn" not in ablate else ()):
        e, qtr = divmod(j, 4)
        par = j % 2
        bw1, bw2 = b_w1q[par], b_w2q[par]
        epar = e % 2
        for t in range(NTILE):
            if t < 4:
                rhs_sel = lambda ic, t=t: xnr4[:, ic, t * TILE:(t + 1) * TILE]
                b_rhs = b_xnr4[t]
            else:
                sp = (t - 4) % 2
                rhs_sel = lambda ic, sp=sp: xns4[:, sp, ic, :]
                b_rhs = b_xns[(t - 4) % 2]
            if t == 2:
                for i in range(2):
                    dma("scalar", xns4[:, i, :, :], xn_spill[4 + i].rearrange("(c p) f -> p c f", p=128),
                        reads=[b_xnsp, b_xns[i]], writes=[b_xns[i]], extra=fence1 if j == 0 else ())
            if t == 5:
                dma("scalar", xns4[:, 0, :, :], xn_spill[6].rearrange("(c p) f -> p c f", p=128),
                    reads=[b_xnsp, b_xns[0]], writes=[b_xns[0]])
            if t == 6:
                dma("scalar", xns4[:, 1, :, :], xn_spill[7].rearrange("(c p) f -> p c f", p=128),
                    reads=[b_xnsp, b_xns[1]], writes=[b_xns[1]])

            for hs in range(8):
                psh, bph = (ps_hA, b_hA) if hs % 2 == 0 else (ps_hB, b_hB)
                bw1u = b_w1q0h[hs // 4] if j == 0 else bw1
                for ic in range(8):
                    mm(psh[:, :], w1q4[:, par, ic, hs * 128:(hs + 1) * 128], rhs_sel(ic),
                       start=(ic == 0), stop=(ic == 7),
                       reads=[bw1u, b_rhs], writes=[bph])
                hsg = qtr * 8 + hs
                act(h_sb[:, hs * 512:(hs + 1) * 512], psh[:, :], AF.Gelu,
                    bias=b1t[:, hsg * NE + e: hsg * NE + e + 1],
                    reads=[bph, b_b1t], writes=[b_h])
            for sub in range(NSUB):
                g = t * NSUB + sub
                if sub % 2 == 0:
                    outs = ((ps_gA[:, 0:512], b_gA), (ps_gA[:, 512:1024], b_gA))
                else:
                    outs = ((ps_gB0[:, :], b_gB0), (ps_gB1[:, :], b_gB1))
                for oh in range(2):
                    po, bpo = outs[oh]
                    for hc in range(8):
                        bw2u = b_w2q0h[hc // 4] if j == 0 else bw2
                        mm(po, h_sb[:, hc * 512 + sub * 128: hc * 512 + sub * 128 + 128],
                           w2q4[:, par, hc, oh * 512:(oh + 1) * 512],
                           start=(hc == 0), stop=(hc == 7 and qtr != 0),
                           reads=[b_h, bw2u], writes=[bpo])
                    if qtr == 0:
                        mm(po, onesb[0:1, 0:128], b2e[0:1, epar * D + oh * 512: epar * D + (oh + 1) * 512],
                           start=False, stop=True, reads=[b_onesb, b_b2e[epar]], writes=[bpo])
                if sub % 2 == 0:
                    S.add("vector",
                          lambda h, g=g, e=e: h.scalar_tensor_tensor(
                              g_pre[:, g * D:(g + 1) * D], ps_gA[:, :], C_all[:, g * NE + e: g * NE + e + 1],
                              g_pre[:, g * D:(g + 1) * D], ALU.mult, ALU.add),
                          reads=[b_gA, b_C, b_gpre_ch[g]], writes=[b_gpre_ch[g]])
                else:
                    S.add("vector",
                          lambda h, g=g, e=e: h.scalar_tensor_tensor(
                              g_pre[:, g * D:g * D + 512], ps_gB0[:, :], C_all[:, g * NE + e: g * NE + e + 1],
                              g_pre[:, g * D:g * D + 512], ALU.mult, ALU.add),
                          reads=[b_gB0, b_C, b_gpre_ch[g]], writes=[b_gpre_ch[g]])
                    S.add("vector",
                          lambda h, g=g, e=e: h.scalar_tensor_tensor(
                              g_pre[:, g * D + 512:(g + 1) * D], ps_gB1[:, :], C_all[:, g * NE + e: g * NE + e + 1],
                              g_pre[:, g * D + 512:(g + 1) * D], ALU.mult, ALU.add),
                          reads=[b_gB1, b_C, b_gpre_ch[g]], writes=[b_gpre_ch[g]])
        if j + 2 <= NE * 4 - 1:
            w_load(j + 2)

    if "gpre" in dbg:
        dma("scalar", dbg["gpre"].ap().rearrange("g p f -> p g f"),
            g_pre.ap().rearrange("p (g f) -> p g f", g=NCH), reads=b_gpre_ch)
    if "C" in dbg:
        dma("scalar", dbg["C"][:, :], C_all[:, :], reads=[b_C])
    if "lg" in dbg:
        dma("scalar", dbg["lg"][:, :], lgbuf[:, :], reads=[b_lg])
    if "Sc" in dbg:
        dma("scalar", dbg["Sc"][:, :], S_all[:, :], reads=[b_S])

    fence2 = S.snapshot()

    a3 = Arena(ARENA_BASE)
    WoT, b_WoT = a3.take("WoT", [128, 8 * D], BF16)
    wstage2, b_wstage2 = a3.take("wstage2", [128, D], BF16)
    kvfull, b_kvfull = a3.take("kvfull", [128, 1040], F32)
    kvm, b_kvm = a3.take("kvm", [128, 520], F32)
    bor, b_bor = a3.take("bor", [1, D], BF16)
    kvaug, b_kvaug = a3.take("kvaug", [128, 8 * 66], BF16)
    q_sb, _ = a3.take("q_sb", [128, 2 * 8 * 512], BF16)
    b_qsb = [Buf("qsb0"), Buf("qsb1")]
    xh2, _ = a3.take("xh2", [128, 2 * 2 * D], F32)
    b_xh2 = [Buf("xh2_0"), Buf("xh2_1")]
    aztm, _ = a3.take("aztm", [128, 2 * 1024], BF16)
    b_aztm = [Buf("aztm0"), Buf("aztm1")]
    azf, _ = a3.take("azf", [128, 2 * 8 * 512], BF16)
    b_azf = [Buf("azf0"), Buf("azf1")]
    den_sb, b_den = a3.take("den_sb", [128, 16], F32)
    zs_sb, b_zs = a3.take("zs_sb", [128, 16], F32)
    out_sb, b_out = a3.take("out_sb", [128, 4 * D], F32)
    gs_sb, b_gs = a3.take("gs_sb", [128, D], F32)
    mres, b_mres = a3.take("mres", [128, 512], F32)

    ps_atA, b_atA = P512[0], bP512[0]
    ps_atB, b_atB = P512[1], bP512[1]
    b_tr2 = [bP1024, Buf("ps1024_b1")]

    dma("gpsimd", bor[:, :], bo_row_in[:, :], writes=[b_bor], extra=fence2)
    for oc in range(8):
        dma("gpsimd", wstage2[:, :], W_in["Wo"][oc * 128:(oc + 1) * 128, :],
            reads=[b_wstage2], writes=[b_wstage2], extra=fence2)
        for ic in range(8):
            psb, bps = (ps_atA, b_atA) if ic % 2 == 0 else (ps_atB, b_atB)
            pbf = psb.ap().bitcast(BF16)
            tr(pbf[:, 0:128], wstage2[:, ic * 128:(ic + 1) * 128], idb[:, :],
               reads=[b_wstage2, b_idb], writes=[bps], extra=fence2)
            S.add("scalar",
                  lambda h, c=ic * D + oc * 128, p=pbf: h.copy(WoT[:, c:c + 128], p[:, 0:128]),
                  reads=[bps], writes=[b_WoT])

    dma("gpsimd", kvfull.ap().rearrange("p (b f) -> p b f", b=2),
        cc_out.ap().rearrange("(b p) f -> p b f", p=128),
        reads=[b_ccout], writes=[b_kvfull], extra=[cc_op] + list(fence2))
    if "kvfull" in dbg:
        dma("scalar", dbg["kvfull"][:, :], kvfull[:, :], reads=[b_kvfull])
    S.add("vector", lambda h: h.tensor_scalar(kvm[:, :], kvfull[:, 0:520], bmb[:, 0:1], None, ALU.mult),
          reads=[b_kvfull, b_bmb], writes=[b_kvm])
    S.add("vector", lambda h: h.scalar_tensor_tensor(kvm[:, :], kvfull[:, 520:1040], bmb[:, 1:2], kvm[:, :], ALU.mult, ALU.add),
          reads=[b_kvfull, b_bmb, b_kvm], writes=[b_kvm])
    for h_ in range(NH):
        pb, fo, sl = (h_ % 2) * 64, (h_ // 2) * 64, h_ // 2
        vop("tensor_copy", kvaug[pb:pb + 64, sl * 66:sl * 66 + 64],
            kvm[pb:pb + 64, fo:fo + 64], reads=[b_kvm], writes=[b_kvaug])
        vop("tensor_copy", kvaug[pb:pb + 64, sl * 66 + 64:sl * 66 + 65],
            kvm[pb:pb + 64, 512 + sl:512 + sl + 1], reads=[b_kvm], writes=[b_kvaug])

    q4 = q_sb.ap().rearrange("p (a c f) -> p a c f", a=2, c=8)
    xh2v = xh2.ap().rearrange("p (a u f) -> p a u f", a=2, u=2)

    for t in (range(NTILE) if "attn" not in ablate else ()):
        qpar = t % 2
        azp = t % 2
        if t == 0:
            dma("sync", q4[:, 0, :, :], q_spill[0].rearrange("(c p) f -> p c f", p=128),
                reads=[b_qsp, b_qsb[0]], writes=[b_qsb[0]], extra=fence2)
            for hf in range(2):
                dma("sync", xh2v[:, hf, :, :],
                    x_in[hf * 256:(hf + 1) * 256, :].rearrange("(u p) f -> p u f", p=128),
                    reads=[b_xh2[hf]], writes=[b_xh2[hf]], extra=fence2)

        def p3_att(sub):
            pA, bA = (P512[0], bP512[0]) if sub % 2 == 0 else (P512[2], bP512[2])
            pB, bB = (P512[1], bP512[1]) if sub % 2 == 0 else (P512[3], bP512[3])
            for h_ in range(NH):
                qrow = (h_ % 2) * 64
                sl = h_ // 2
                qT = q4[qrow:qrow + 64, qpar, sl, sub * 128:(sub + 1) * 128]
                po, bo_ = (pA, bA) if h_ % 2 == 0 else (pB, bB)
                mm(po[:, sl * 64:sl * 64 + 64], qT,
                   kvaug[qrow:qrow + 64, sl * 66:sl * 66 + 64],
                   start=True, stop=True, reads=[b_qsb[qpar], b_kvaug], writes=[bo_])

        def p3_den(sub):
            pD0, bD0 = P512[4], bP512[4]
            pD1, bD1 = P512[5], bP512[5]
            for h_ in range(NH):
                qrow = (h_ % 2) * 64
                sl = h_ // 2
                qT = q4[qrow:qrow + 64, qpar, sl, sub * 128:(sub + 1) * 128]
                pd, bd = (pD0, bD0) if h_ % 2 == 0 else (pD1, bD1)
                mm(pd[:, sub * 8 + sl:sub * 8 + sl + 1],
                   qT, kvaug[qrow:qrow + 64, sl * 66 + 64:sl * 66 + 65],
                   start=True, stop=True, reads=[b_qsb[qpar], b_kvaug], writes=[bd])

        def p3_z(sub):
            pA, bA = (P512[0], bP512[0]) if sub % 2 == 0 else (P512[2], bP512[2])
            pB, bB = (P512[1], bP512[1]) if sub % 2 == 0 else (P512[3], bP512[3])
            pD0, bD0 = P512[4], bP512[4]
            pD1, bD1 = P512[5], bP512[5]
            S.add("vector", lambda h, pD0=pD0, sub=sub: h.tensor_scalar_add(den_sb[:, 0:8], pD0[:, sub * 8:sub * 8 + 8], 1e-6),
                  reads=[bD0], writes=[b_den])
            S.add("vector", lambda h, pD1=pD1, sub=sub: h.tensor_scalar_add(den_sb[:, 8:16], pD1[:, sub * 8:sub * 8 + 8], 1e-6),
                  reads=[bD1], writes=[b_den])
            S.add("vector", lambda h: h.reciprocal(zs_sb[:, :], den_sb[:, :]),
                  reads=[b_den], writes=[b_zs])
            for h_ in range(NH):
                sl = h_ // 2
                po, bo_ = (pA, bA) if h_ % 2 == 0 else (pB, bB)
                src = po[:, sl * 64:sl * 64 + 64]
                zcol = sl + 8 * (h_ % 2)
                dst = aztm[:, (sub % 2) * 1024 + h_ * 64:(sub % 2) * 1024 + (h_ + 1) * 64]
                S.add("vector", lambda h, d=dst, sr=src, zc=zcol: h.tensor_scalar(d, sr, zs_sb[:, zc:zc + 1], None, ALU.mult),
                      reads=[bo_, b_zs], writes=[b_aztm[sub % 2]])

        def p3_tr(sub):
            for ic in range(8):
                trbank = ic % 2
                pt = P1024.ap().bitcast(BF16)[:, trbank * 1024 + 0:trbank * 1024 + 128]
                tr(pt, aztm[:, (sub % 2) * 1024 + ic * 128:(sub % 2) * 1024 + (ic + 1) * 128], idb[:, :],
                   reads=[b_aztm[sub % 2], b_idb], writes=[b_tr2[trbank]])
                col = ic * TILE + sub * 128
                S.add("scalar", lambda h, c=col, p=pt, azp=azp: h.copy(
                      azf[:, azp * 4096 + c: azp * 4096 + c + 128], p),
                      reads=[b_tr2[trbank]], writes=[b_azf[azp]])

        if "p3cb" not in ablate:
            p3_att(0)
            p3_den(0)
            p3_att(1)
            for sub in range(NSUB):
                if "p3z" not in ablate:
                    p3_z(sub)
                if sub + 1 < NSUB:
                    p3_den(sub + 1)
                if sub + 2 < NSUB:
                    p3_att(sub + 2)
                if "p3z" not in ablate:
                    p3_tr(sub)

        if t + 1 < NTILE:
            dma("sync", q4[:, (t + 1) % 2, :, :], q_spill[t + 1].rearrange("(c p) f -> p c f", p=128),
                reads=[b_qsp, b_qsb[(t + 1) % 2]], writes=[b_qsb[(t + 1) % 2]])
        for sub in (range(NSUB) if "p3wo" not in ablate else ()):
            g = t * NSUB + sub
            par = (t * 2 + sub // 2) % 2
            S.add("scalar",
                  lambda h, g=g: h.activation(gs_sb[:, :], g_pre[:, g * D:(g + 1) * D], AF.Silu,
                                              scale=S_all[:, g:g + 1]),
                  reads=[b_gpre_ch[g], b_S], writes=[b_gs])
            for oh in range(2):
                pw, bpw = P512[sub], bP512[sub]
                for ic in range(8):
                    mm(pw[:, :], azf[:, azp * 4096 + ic * TILE + sub * 128: azp * 4096 + ic * TILE + sub * 128 + 128],
                       WoT[:, ic * D + oh * 512:(ic) * D + (oh + 1) * 512],
                       start=(ic == 0), stop=False, reads=[b_azf[azp], b_WoT], writes=[bpw])
                mm(pw[:, :], onesb[0:1, 0:128], bor[0:1, oh * 512:(oh + 1) * 512],
                   start=False, stop=True, reads=[b_onesb, b_bor], writes=[bpw])
                S.add("vector", lambda h, pw=pw, oh=oh: h.tensor_mul(
                      mres[:, :], pw[:, :], gs_sb[:, oh * 512:(oh + 1) * 512]),
                      reads=[bpw, b_gs], writes=[b_mres])
                S.add("vector",
                      lambda h, sub=sub, par=par, oh=oh: h.tensor_add(
                          out_sb[:, sub * D + oh * 512: sub * D + (oh + 1) * 512],
                          xh2v[:, par, sub % 2, oh * 512:(oh + 1) * 512], mres[:, :]),
                      reads=[b_xh2[par], b_mres], writes=[b_out])
        dma("sync", y_out[t * TILE:(t + 1) * TILE, :].rearrange("(s p) f -> p s f", p=128),
            out_sb.ap().rearrange("p (s f) -> p s f", s=NSUB),
            reads=[b_out], writes=[b_y])
        if t + 1 < NTILE:
            for hf in range(2):
                dma("sync", xh2v[:, hf, :, :],
                    x_in[(t + 1) * TILE + hf * 256: (t + 1) * TILE + (hf + 1) * 256, :].rearrange("(u p) f -> p u f", p=128),
                    reads=[b_xh2[hf]], writes=[b_xh2[hf]])

    S.add("gpsimd", lambda h: h.nop(), extra_waits=S.snapshot())

    guards = S.emit(nc)
    return nc


def kernel(**inputs):
    x = np.ascontiguousarray(np.asarray(inputs["L0_attn_hidden_states"], dtype=np.float32).reshape(2 * 16384, D))
    Wq = np.ascontiguousarray(np.asarray(inputs["L0_attn_Wq"], dtype=np.float32))
    Wk = np.ascontiguousarray(np.asarray(inputs["L0_attn_Wk"], dtype=np.float32))
    Wv = np.ascontiguousarray(np.asarray(inputs["L0_attn_Wv"], dtype=np.float32))
    Wo = np.ascontiguousarray(np.asarray(inputs["L0_attn_Wo"], dtype=np.float32))
    bq = np.asarray(inputs["L0_attn_bq"], dtype=np.float32)
    bk = np.asarray(inputs["L0_attn_bk"], dtype=np.float32)
    bv = np.asarray(inputs["L0_attn_bv"], dtype=np.float32)
    bo = np.asarray(inputs["L0_attn_bo"], dtype=np.float32)
    wg_red = np.ascontiguousarray(np.asarray(inputs["L0_ffn_wg_red"], dtype=np.float32))
    wg = np.ascontiguousarray(np.asarray(inputs["L0_ffn_wg"], dtype=np.float32))
    w1 = np.ascontiguousarray(np.asarray(inputs["L0_ffn_w1"], dtype=np.float32))
    b1 = np.asarray(inputs["L0_ffn_b1"], dtype=np.float32)
    w2 = np.ascontiguousarray(np.asarray(inputs["L0_ffn_w2"], dtype=np.float32))
    b2 = np.asarray(inputs["L0_ffn_b2"], dtype=np.float32)

    bq_t = np.ascontiguousarray(bq.reshape(8, 128).T)
    bo_t = np.ascontiguousarray(bo.reshape(8, 128).T)
    bk_row = np.ascontiguousarray(bk.reshape(1, D))
    bo_row = np.ascontiguousarray(bo.reshape(1, D))
    bv_row = np.ascontiguousarray(bv.reshape(1, D))
    b1_t = np.ascontiguousarray(b1.reshape(NE, 32, 128).transpose(2, 1, 0).reshape(128, 32 * NE))
    b2row = np.ascontiguousarray(b2.reshape(1, NE * D))
    ident = np.eye(128, dtype=np.float32)
    ones_row = np.ones((1, 512), dtype=np.float32)
    ones_col = np.ones((128, 1), dtype=np.float32)
    pw2 = (2.0 ** -np.arange(NE, dtype=np.float32)).reshape(1, NE)

    nc = build_program(debug=tuple(DEBUG), ablate=frozenset(ABLATE))

    in_maps = []
    for c in range(NCORE):
        bmask = np.zeros((1, 2), dtype=np.float32)
        bmask[0, c // 4] = 1.0
        in_maps.append({
            "x": x[c * TOK:(c + 1) * TOK],
            "Wq": Wq, "Wk": Wk, "Wv": Wv, "Wo": Wo,
            "bq_t": bq_t, "bo_t": bo_t, "bk_row": bk_row, "bv_row": bv_row, "bo_row": bo_row,
            "wg_red": wg_red, "wg": wg, "w1": w1, "b1_t": b1_t, "w2": w2, "b2row": b2row,
            "ident": ident, "ones_row": ones_row, "ones_col": ones_col,
            "pw2": pw2, "bmask": bmask,
        })

    res = run_bass_kernel_spmd(nc, in_maps, core_ids=list(range(NCORE)))
    out = np.concatenate([res.results[c]["y"] for c in range(NCORE)], axis=0)
    if DEBUG:
        kernel.debug_results = res.results
    return out.reshape(2, 16384, D)


if __name__ == "__main__":
    build_program()
    print("build ok")


# version 14 (final)
# speedup vs baseline: 2.4421x; 1.0040x over previous
import numpy as np
import ml_dtypes
import concourse.bass as bass
import concourse.mybir as mybir
from concourse.bass_utils import run_bass_kernel_spmd

F32 = mybir.dt.float32
BF16 = mybir.dt.bfloat16
F16 = mybir.dt.float16
AF = mybir.ActivationFunctionType
ALU = mybir.AluOpType
AX = mybir.AxisListType

NCORE = 8
TOK = 4096
D = 1024
NH = 16
NE = 8
HID = 4096
TILE = 512
NTILE = TOK // TILE
NSUB = TILE // 128
NCH = TOK // 128

DEBUG = []
ABLATE = set()


class Buf:
    __slots__ = ("name", "w", "readers")

    def __init__(self, name):
        self.name = name
        self.w = None
        self.readers = {}


class Op:
    __slots__ = ("eng", "fn", "waits", "dom", "inc", "flagged", "count_after", "i", "eng_seq", "raw")


RAW_K = 4


class Sched:
    def __init__(self):
        self.plan = []
        self.eng_n = {}

    def add(self, eng, fn, reads=(), writes=(), dma=False, cc=False, extra_waits=()):
        op = Op()
        op.eng = eng
        op.fn = fn
        op.dom = eng + "_dma" if dma else ("cc" if cc else eng)
        op.inc = 16 if dma else 1
        op.flagged = False
        op.count_after = None
        op.i = len(self.plan)
        op.eng_seq = self.eng_n.get(eng, 0)
        self.eng_n[eng] = op.eng_seq + 1
        ws = {}
        rw = {}

        def upd(d, w):
            c = d.get(w.dom)
            if c is None or w.i > c.i:
                d[w.dom] = w

        for b in reads:
            if b.w is not None:
                upd(ws, b.w)
                upd(rw, b.w)
        for b in writes:
            for r in b.readers.values():
                upd(ws, r)
            if b.w is not None:
                upd(ws, b.w)
        for w in extra_waits:
            if w is not None:
                upd(ws, w)
        waits = set(ws.values())
        raw = set(rw.values())
        waits.discard(op)
        raw.discard(op)
        for w in raw:
            waits.add(w)
        op.waits = sorted(waits, key=lambda o: o.i)
        op.raw = raw
        for b in reads:
            b.readers[op.dom] = op
        for b in writes:
            b.w = op
            b.readers = {}
        self.plan.append(op)
        return op

    def snapshot(self):
        last = {}
        for op in self.plan:
            last[op.dom] = op
        return list(last.values())

    def emit(self, nc):
        for op in self.plan:
            if op.dom.endswith("_dma") or op.dom == "cc":
                op.flagged = True
        def need_wait(op, w):
            if w.dom != op.eng:
                return True
            if op.eng == "tensor":
                return False
            return (op.eng_seq - w.eng_seq) <= RAW_K

        for op in self.plan:
            for w in op.waits:
                if need_wait(op, w):
                    w.flagged = True
        counts = {}
        for op in self.plan:
            if op.flagged:
                counts[op.dom] = counts.get(op.dom, 0) + op.inc
                op.count_after = counts[op.dom]
        doms = set(counts.keys())
        sems = {}
        guards = []
        for d in sorted(doms):
            g = nc.semaphore("sem_" + d)
            sems[d] = g.__enter__()
            guards.append(g)

        engs = ("tensor", "vector", "scalar", "gpsimd", "sync")
        per_eng = {e: [op for op in self.plan if op.eng == e] for e in engs}

        def mk_body(eng):
            ops = per_eng[eng]

            def body(h):
                hw = {}
                for op in ops:
                    for w in op.waits:
                        if not need_wait(op, w):
                            continue
                        assert w.flagged, (w.i, op.i)
                        tgt = w.count_after
                        if hw.get(w.dom, 0) >= tgt:
                            continue
                        h.wait_ge(sems[w.dom], tgt)
                        hw[w.dom] = tgt
                    inst = op.fn(h)
                    if op.flagged:
                        inst.then_inc(sems[op.dom], op.inc)
            return body

        with nc.Block() as block:
            for eng in engs:
                if per_eng[eng]:
                    getattr(block, eng)(mk_body(eng))
        return guards


DT_SIZE = {F32: 4, BF16: 2, F16: 2}


def build_program(debug=(), ablate=frozenset()):
    nc = bass.Bass("TRN2", target_bir_lowering=False)
    S = Sched()

    x_in = nc.dram_tensor("x", [TOK, D], F32, kind="ExternalInput")
    W_in = {w: nc.dram_tensor(w, [D, D], F32, kind="ExternalInput") for w in ("Wq", "Wk", "Wv", "Wo")}
    bq_t_in = nc.dram_tensor("bq_t", [128, 8], F32, kind="ExternalInput")
    bo_t_in = nc.dram_tensor("bo_t", [128, 8], F32, kind="ExternalInput")
    bk_row_in = nc.dram_tensor("bk_row", [1, D], F32, kind="ExternalInput")
    bo_row_in = nc.dram_tensor("bo_row", [1, D], F32, kind="ExternalInput")
    bv_row_in = nc.dram_tensor("bv_row", [1, D], F32, kind="ExternalInput")
    wgred_in = nc.dram_tensor("wg_red", [16, D], F32, kind="ExternalInput")
    wg_in = nc.dram_tensor("wg", [NE, 16], F32, kind="ExternalInput")
    w1_in = nc.dram_tensor("w1", [NE, D, HID], F32, kind="ExternalInput")
    b1_t_in = nc.dram_tensor("b1_t", [128, 32 * NE], F32, kind="ExternalInput")
    w2_in = nc.dram_tensor("w2", [NE, HID, D], F32, kind="ExternalInput")
    b2row_in = nc.dram_tensor("b2row", [1, NE * D], F32, kind="ExternalInput")
    ident_in = nc.dram_tensor("ident", [128, 128], F32, kind="ExternalInput")
    ones_in = nc.dram_tensor("ones_row", [1, 512], F32, kind="ExternalInput")
    onesc_in = nc.dram_tensor("ones_col", [128, 1], F32, kind="ExternalInput")
    pw2_in = nc.dram_tensor("pw2", [1, NE], F32, kind="ExternalInput")
    bmask_in = nc.dram_tensor("bmask", [1, 2], F32, kind="ExternalInput")

    y_out = nc.dram_tensor("y", [TOK, D], F32, kind="ExternalOutput")

    xn_spill = nc.dram_tensor("xn_spill", [NTILE, D, TILE], BF16)
    q_spill = nc.dram_tensor("q_spill", [NTILE, D, TILE], BF16)
    cc_in = nc.dram_tensor("cc_in", [2 * 128, 520], F32)
    cc_out = nc.dram_tensor("cc_out", [2 * 128, 520], F32)

    dbg = {}
    if "xn" in debug:
        dbg["xn"] = nc.dram_tensor("dbg_xn", [NTILE, D, TILE], BF16, kind="ExternalOutput")
    if "q" in debug:
        dbg["q"] = nc.dram_tensor("dbg_q", [NTILE, D, TILE], BF16, kind="ExternalOutput")
    if "kv" in debug:
        dbg["kv"] = nc.dram_tensor("dbg_kv", [128, 520], F32, kind="ExternalOutput")
    if "kvfull" in debug:
        dbg["kvfull"] = nc.dram_tensor("dbg_kvfull", [128, 1040], F32, kind="ExternalOutput")
    if "C" in debug:
        dbg["C"] = nc.dram_tensor("dbg_C", [128, NCH * NE], F32, kind="ExternalOutput")
    if "Sc" in debug:
        dbg["Sc"] = nc.dram_tensor("dbg_Sc", [128, NCH], F32, kind="ExternalOutput")
    if "gpre" in debug:
        dbg["gpre"] = nc.dram_tensor("dbg_gpre", [NCH, 128, D], F16, kind="ExternalOutput")
    if "xr" in debug:
        dbg["xr"] = nc.dram_tensor("dbg_xr", [NTILE, 16, TILE], F32, kind="ExternalOutput")
    if "lg" in debug:
        dbg["lg"] = nc.dram_tensor("dbg_lg", [128, NCH * NE], F32, kind="ExternalOutput")
    if "lnraw" in debug:
        dbg["lnraw"] = nc.dram_tensor("dbg_lnraw", [128, D], F32, kind="ExternalOutput")
        dbg["lnst"] = nc.dram_tensor("dbg_lnst", [128, 8], F32, kind="ExternalOutput")
    if "ktm" in debug:
        dbg["ktm"] = nc.dram_tensor("dbg_ktm", [128, D], BF16, kind="ExternalOutput")
        dbg["vtm"] = nc.dram_tensor("dbg_vtm", [128, D], BF16, kind="ExternalOutput")

    SB_BASE = (nc.sbuf_base + 63) & ~63
    SB_TOP = nc.sbuf_top

    class Arena:
        def __init__(self, base):
            self.off = base
            self.hi = base

        def take(self, name, shape, dtype):
            nbytes = int(np.prod(shape[1:])) * DT_SIZE[dtype]
            off = (self.off + 63) & ~63
            t = nc.alloc_sbuf_tensor_at(name, list(shape), dtype, offset=off)
            self.off = off + nbytes
            self.hi = max(self.hi, self.off)
            assert self.off <= SB_TOP, (name, self.off, SB_TOP)
            return t, Buf(name)

    pers = Arena(SB_BASE)
    g_pre, b_gpre = pers.take("g_pre", [128, NCH * D], F16)
    b_gpre_ch = [Buf(f"g_pre{g}") for g in range(NCH)]
    C_all, b_C = pers.take("C_all", [128, NCH * NE], F32)
    S_all, b_S = pers.take("S_all", [128, NCH], F32)
    b1t, b_b1t = pers.take("b1t", [128, 32 * NE], F32)
    bqt, b_bqt = pers.take("bqt", [128, 8], F32)
    bot, b_bot = pers.take("bot", [128, 8], F32)
    idf, b_idf = pers.take("idf", [128, 128], F32)
    idb, b_idb = pers.take("idb", [128, 128], BF16)
    onesf, b_onesf = pers.take("onesf", [1, 512], F32)
    onesb, b_onesb = pers.take("onesb", [1, 512], BF16)
    onescb, b_onescb = pers.take("onescb", [128, 1], BF16)
    pw2b, b_pw2 = pers.take("pw2b", [128, NE], F32)
    bmb, b_bmb = pers.take("bmb", [128, 2], F32)
    ARENA_BASE = (pers.off + 63) & ~63

    b_ccin, b_ccout = Buf("cc_in"), Buf("cc_out")
    b_xnsp, b_qsp = Buf("xn_spill"), Buf("q_spill")
    b_y = Buf("y")

    P512 = []
    bP512 = []
    for i in range(6):
        t = nc.alloc_psum_tensor(f"ps{i}", [128, 512], F32)
        P512.append(t)
        bP512.append(Buf(f"ps{i}"))
    P1024 = nc.alloc_psum_tensor("ps1024", [128, 1024], F32)
    bP1024 = Buf("ps1024")

    def dma(eng, dst, src, reads=(), writes=(), extra=()):
        return S.add(eng, lambda h, d=dst, s=src: h.dma_start(out=d, in_=s),
                     reads=reads, writes=writes, dma=True, extra_waits=extra)

    dma("gpsimd", idb[:, :], ident_in[:, :], writes=[b_idb])
    dma("gpsimd", onesb[:, :], ones_in[:, :], writes=[b_onesb])
    dma("gpsimd", onescb[:, :], onesc_in[:, :], writes=[b_onescb])
    dma("sync", b1t[:, :], b1_t_in[:, :], writes=[b_b1t])
    dma("sync", bqt[:, :], bq_t_in[:, :], writes=[b_bqt])
    dma("sync", bot[:, :], bo_t_in[:, :], writes=[b_bot])
    dma("sync", idf[:, :], ident_in[:, :], writes=[b_idf])
    dma("sync", onesf[:, :], ones_in[:, :], writes=[b_onesf])
    dma("sync", pw2b[:, :], pw2_in.ap().broadcast_to([128, NE]), writes=[b_pw2])
    dma("sync", bmb[:, :], bmask_in.ap().broadcast_to([128, 2]), writes=[b_bmb])

    def mm(out, lhsT, rhs, start, stop, reads, writes, skip_gc=True, extra=()):
        return S.add("tensor",
                     lambda h, o=out, l=lhsT, r=rhs, st=start, sp=stop, sg=skip_gc:
                     h.matmul(o, l, r, start=st, stop=sp, skip_group_check=sg),
                     reads=reads, writes=writes, extra_waits=extra)

    def tr(out, in_, ident, reads, writes, extra=()):
        return S.add("tensor", lambda h, o=out, i=in_, d=ident: h.transpose(o, i, d),
                     reads=reads, writes=writes, extra_waits=extra)

    def act(out, in_, func, bias=0.0, scale=1.0, reads=(), writes=(), extra=()):
        return S.add("scalar",
                     lambda h, o=out, i=in_, f=func, b=bias, sc=scale: h.activation(o, i, f, bias=b, scale=sc),
                     reads=reads, writes=writes, extra_waits=extra)

    def vop(name, *aps, reads=(), writes=(), extra=(), eng="vector"):
        def fn(h, n=name, a=aps):
            return getattr(h, n)(*a)
        return S.add(eng, fn, reads=reads, writes=writes, extra_waits=extra)

    a1 = Arena(ARENA_BASE)
    WqT, b_WqT = a1.take("WqT", [128, 8 * D], BF16)
    WkT, b_WkT = a1.take("WkT", [128, 8 * D], BF16)
    WvT, b_WvT = a1.take("WvT", [128, 8 * D], BF16)
    wstage, b_wstage = a1.take("wstage", [128, D], BF16)
    wgredT, b_wgredT = a1.take("wgredT", [128, 8 * 16], F32)
    wgr_st, b_wgr_st = a1.take("wgr_st", [16, D], F32)
    wgu, b_wgu = a1.take("wgu", [NE, 16], F32)
    wguT, b_wguT = a1.take("wguT", [16, NE], F32)
    wgtmp, b_wgtmp = a1.take("wgtmp", [NE, 20], F32)
    bkr, b_bkr = a1.take("bkr", [1, D], BF16)
    bvr, b_bvr = a1.take("bvr", [1, D], BF16)
    xh, _ = a1.take("xh", [128, 2 * D], F32)
    b_xh = [Buf("xh0"), Buf("xh1")]
    ctr2, _ = a1.take("ctr2", [128, 2 * D], F32)
    b_ctr2 = [Buf("ctr0"), Buf("ctr1")]
    scr1, _ = a1.take("scr1", [128, D], F32)
    b_scr1h = [Buf("scr1a"), Buf("scr1b")]
    scr22, _ = a1.take("scr22", [128, 2 * D], F32)
    b_scr22 = [Buf("xnl0"), Buf("xnl1")]
    lnst2, _ = a1.take("lnst2", [128, 16], F32)
    b_lnst2 = [Buf("lnst0"), Buf("lnst1")]
    xnf_bf2, _ = a1.take("xnf_bf2", [128, 2 * 8 * TILE], BF16)
    b_xnbf2 = [Buf("xnbf0"), Buf("xnbf1")]
    xnf_f32, b_xnf32 = a1.take("xnf_f32", [128, 8 * 256], F32)
    q_tile, b_q = a1.take("q_tile", [128, 8 * TILE], BF16)
    k_tm, b_ktm = a1.take("k_tm", [128, D], BF16)
    v_tm, b_vtm = a1.take("v_tm", [128, D], BF16)
    xr_sb, _ = a1.take("xr_sb", [16, 2 * TILE], F32)
    b_xr2 = [Buf("xr0"), Buf("xr1")]
    gsc, b_gsc = a1.take("gsc", [128, 8 * 8], F32)
    qsc, _ = a1.take("qsc", [128, D], F32)
    b_qsch = [Buf("qsca"), Buf("qscb")]
    if "lg" in debug:
        lgbuf, b_lg = a1.take("lgbuf", [128, NCH * NE], F32)
    kvacc, b_kvacc = a1.take("kvacc", [128, 520], F32)
    contrib, b_contrib = a1.take("contrib", [128, 2 * 520], F32)

    ps_kvh, b_pkvh = P512[0], bP512[0]
    ps_ksum, b_pksum = P512[1], bP512[1]
    ps_trA, b_trA = P512[2], bP512[2]
    ps_trB, b_trB = P512[3], bP512[3]
    ps_mmA, b_mmA = P512[4], bP512[4]
    ps_mmB, b_mmB = P512[5], bP512[5]
    ps_kv2, b_pkv2 = P1024, bP1024

    dma("gpsimd", bkr[:, :], bk_row_in[:, :], writes=[b_bkr])
    dma("gpsimd", bvr[:, :], bv_row_in[:, :], writes=[b_bvr])

    def prep_wT(nm, WT, b_WT):
        for oc in range(8):
            dma("gpsimd", wstage[:, :], W_in[nm][oc * 128:(oc + 1) * 128, :],
                reads=(), writes=[b_wstage])
            for ic in range(8):
                psb, bps = (ps_trA, b_trA) if ic % 2 == 0 else (ps_trB, b_trB)
                pbf = psb.ap().bitcast(BF16)
                tr(pbf[:, 0:128], wstage[:, ic * 128:(ic + 1) * 128], idb[:, :],
                   reads=[b_wstage, b_idb], writes=[bps])
                S.add("scalar",
                      lambda h, w=WT, c=ic * D + oc * 128, p=pbf: h.copy(w[:, c:c + 128], p[:, 0:128]),
                      reads=[bps], writes=[b_WT])

    prep_wT("Wq", WqT, b_WqT)
    prep_wT("Wk", WkT, b_WkT)
    prep_wT("Wv", WvT, b_WvT)

    dma("sync", wgr_st[:, :], wgred_in[:, :], writes=[b_wgr_st])
    for ic in range(8):
        psb, bps = (ps_trA, b_trA) if ic % 2 == 0 else (ps_trB, b_trB)
        tr(psb[0:128, 0:16], wgr_st[0:16, ic * 128:(ic + 1) * 128], idf[0:16, 0:16],
           reads=[b_wgr_st, b_idf], writes=[bps])
        vop("tensor_copy", wgredT[:, ic * 16:(ic + 1) * 16], psb[0:128, 0:16],
            reads=[bps], writes=[b_wgredT])

    dma("sync", wgu[:, :], wg_in[:, :], writes=[b_wgu])
    act(wgtmp[:, 0:16], wgu[:, :], AF.Square, reads=[b_wgu], writes=[b_wgtmp])
    vop("reduce_sum", wgtmp[:, 16:17], wgtmp[:, 0:16], AX.X, reads=[b_wgtmp], writes=[b_wgtmp])
    act(wgtmp[:, 17:18], wgtmp[:, 16:17], AF.Sqrt, reads=[b_wgtmp], writes=[b_wgtmp])
    vop("tensor_scalar_max", wgtmp[:, 17:18], wgtmp[:, 17:18], 1e-4, reads=[b_wgtmp], writes=[b_wgtmp])
    vop("reciprocal", wgtmp[:, 18:19], wgtmp[:, 17:18], reads=[b_wgtmp], writes=[b_wgtmp])
    S.add("vector", lambda h: h.tensor_scalar(wgu[:, :], wgu[:, :], wgtmp[:, 18:19], None, ALU.mult),
          reads=[b_wgtmp], writes=[b_wgu])
    tr(ps_trA[0:16, 0:NE], wgu[:, :], idf[0:NE, 0:NE], reads=[b_wgu, b_idf], writes=[b_trA])
    vop("tensor_copy", wguT[:, :], ps_trA[0:16, 0:NE], reads=[b_trA], writes=[b_wguT])


    def xnf(t):
        return xnf_bf2.ap().rearrange("p (a c f) -> p a c f", a=2, c=8)[:, t % 2, :, :]

    def stage_ln(t):
        bxnbf = b_xnbf2[t % 2]
        if t == 0:
            for k0 in range(2):
                dma("sync", xh[:, (k0 % 2) * D:(k0 % 2 + 1) * D],
                    x_in[k0 * 128:(k0 + 1) * 128, :],
                    reads=[b_xh[k0 % 2]], writes=[b_xh[k0 % 2]])
        for s in range(NSUB):
            kg = t * NSUB + s
            par = kg % 2
            pp = s % 2
            xsub = xh[:, par * D:(par + 1) * D]
            bx = b_xh[par]
            ctr = ctr2[:, pp * D:(pp + 1) * D]
            b_ctr = b_ctr2[pp]
            scr2 = scr22[:, pp * D:(pp + 1) * D]
            b_scr2 = b_scr22[pp]
            lnst = lnst2[:, pp * 8:(pp + 1) * 8]
            b_lnst = b_lnst2[pp]
            vop("reduce_sum", lnst[:, 0:1], xsub, AX.X, reads=[bx], writes=[b_lnst])
            vop("tensor_scalar_mul", lnst[:, 1:2], lnst[:, 0:1], 1.0 / D, reads=[b_lnst], writes=[b_lnst])
            S.add("vector", lambda h, xs=xsub, ctr=ctr, lnst=lnst: h.tensor_scalar(ctr[:, :], xs, lnst[:, 1:2], None, ALU.subtract),
                  reads=[bx, b_lnst], writes=[b_ctr])
            act(scr2[:, :], ctr[:, :], AF.Square, reads=[b_ctr], writes=[b_scr2])
            vop("reduce_sum", lnst[:, 2:3], scr2[:, :], AX.X, reads=[b_scr2], writes=[b_lnst])
            act(lnst[:, 3:4], lnst[:, 2:3], AF.Ln, bias=np.float32(1e-5), scale=1.0 / D, reads=[b_lnst], writes=[b_lnst])
            act(lnst[:, 4:5], lnst[:, 3:4], AF.Exp, scale=-0.5, reads=[b_lnst], writes=[b_lnst])
            S.add("vector", lambda h, ctr=ctr, lnst=lnst, scr2=scr2: h.tensor_scalar(scr2[:, :], ctr[:, :], lnst[:, 4:5], None, ALU.mult),
                  reads=[b_ctr, b_lnst], writes=[b_scr2])
            for ic in range(8):
                psb, bps = (ps_trA, b_trA) if ic % 2 == 0 else (ps_trB, b_trB)
                slot = (ic // 2) % 2
                tr(psb[:, slot * 128:slot * 128 + 128], scr2[:, ic * 128:(ic + 1) * 128], idf[:, :],
                   reads=[b_scr2, b_idf], writes=[bps])
                colf = ic * 256 + pp * 128
                vop("tensor_copy", xnf_f32[:, colf:colf + 128], psb[:, slot * 128:slot * 128 + 128],
                    reads=[bps], writes=[b_xnf32])
                S.add("scalar", lambda h, t=t, c=ic * TILE + s * 128, cf=colf: h.copy(
                      xnf(t)[:, c // TILE, c % TILE: c % TILE + 128], xnf_f32[:, cf:cf + 128]),
                      reads=[b_xnf32], writes=[bxnbf])
            if s % 2 == 1 and "gating" not in ablate:
                hf2 = s // 2
                for ic in range(8):
                    mm(ps_mmA[0:16, hf2 * 256:hf2 * 256 + 256], wgredT[:, ic * 16:(ic + 1) * 16],
                       xnf_f32[:, ic * 256:(ic + 1) * 256],
                       start=(ic == 0), stop=(ic == 7), reads=[b_wgredT, b_xnf32], writes=[b_mmA])
                vop("tensor_copy", xr_sb[:, (t % 2) * TILE + hf2 * 256:(t % 2) * TILE + hf2 * 256 + 256],
                    ps_mmA[0:16, hf2 * 256:hf2 * 256 + 256],
                    reads=[b_mmA], writes=[b_xr2[t % 2]])
            if kg + 2 < NTILE * NSUB:
                k2 = kg + 2
                dma("sync", xh[:, (k2 % 2) * D:(k2 % 2 + 1) * D],
                    x_in[k2 * 128:(k2 + 1) * 128, :],
                    reads=[b_xh[k2 % 2]], writes=[b_xh[k2 % 2]])
        dma("scalar", xn_spill[t].rearrange("(c p) f -> p c f", p=128),
            xnf(t), reads=[bxnbf], writes=[b_xnsp])
        if "xn" in dbg:
            dma("scalar", dbg["xn"][t].rearrange("(c p) f -> p c f", p=128), xnf(t), reads=[bxnbf])

    def stage_proj(t):
        bxnbf = b_xnbf2[t % 2]
        xt = xnf(t)
        for oc in (range(8) if "q" not in ablate else ()):
            psq, bpsq = (ps_mmA, b_mmA) if oc % 2 == 0 else (ps_mmB, b_mmB)
            for ic in range(8):
                mm(psq[:, :], WqT[:, ic * D + oc * 128: ic * D + oc * 128 + 128],
                   xt[:, ic, :],
                   start=(ic == 0), stop=(ic == 7),
                   reads=[b_WqT, bxnbf], writes=[bpsq])
            hp = oc % 2
            act(scr1[:, hp * TILE:(hp + 1) * TILE], psq[:, :], AF.Relu, bias=bqt[:, oc:oc + 1],
                reads=[bpsq, b_bqt], writes=[b_scr1h[hp]])
            act(qsc[:, hp * TILE:(hp + 1) * TILE], psq[:, :], AF.Exp, bias=bqt[:, oc:oc + 1],
                reads=[bpsq, b_bqt], writes=[b_qsch[hp]])
            S.add("vector",
                  lambda h, oc=oc, hp=hp: h.scalar_tensor_tensor(
                      q_tile[:, oc * TILE:(oc + 1) * TILE], qsc[:, hp * TILE:(hp + 1) * TILE], 1.0,
                      scr1[:, hp * TILE:(hp + 1) * TILE], ALU.min, ALU.add),
                  reads=[b_qsch[hp], b_scr1h[hp]], writes=[b_q])
        dma("scalar", q_spill[t].rearrange("(c p) f -> p c f", p=128),
            q_tile.ap().rearrange("p (c f) -> p c f", c=8),
            reads=[b_q], writes=[b_qsp])
        if "q" in dbg:
            dma("scalar", dbg["q"][t].rearrange("(c p) f -> p c f", p=128),
                q_tile.ap().rearrange("p (c f) -> p c f", c=8), reads=[b_q])

        def kv_proj(s):
            for half in range(2):
                for ic in range(8):
                    mm(ps_kv2[:, half * 512:(half + 1) * 512],
                       xt[:, ic, s * 128:s * 128 + 128],
                       WkT[:, ic * D + half * 512: ic * D + (half + 1) * 512],
                       start=(ic == 0), stop=False,
                       reads=[bxnbf, b_WkT], writes=[b_pkv2])
                mm(ps_kv2[:, half * 512:(half + 1) * 512],
                   onesb[0:1, 0:128], bkr[0:1, half * 512:(half + 1) * 512],
                   start=False, stop=True,
                   reads=[b_onesb, b_bkr], writes=[b_pkv2])
            act(scr1[:, :], ps_kv2[:, :], AF.Relu, reads=[b_pkv2], writes=b_scr1h)
            act(qsc[:, :], ps_kv2[:, :], AF.Exp, reads=[b_pkv2], writes=b_qsch)
            S.add("vector", lambda h: h.scalar_tensor_tensor(
                  k_tm[:, :], qsc[:, :], 1.0, scr1[:, :], ALU.min, ALU.add),
                  reads=b_qsch + b_scr1h, writes=[b_ktm])
            for half in range(2):
                psv, bpsv = (ps_mmA, b_mmA) if half == 0 else (ps_mmB, b_mmB)
                for ic in range(8):
                    mm(psv[:, :],
                       xt[:, ic, s * 128:s * 128 + 128],
                       WvT[:, ic * D + half * 512: ic * D + (half + 1) * 512],
                       start=(ic == 0), stop=False,
                       reads=[bxnbf, b_WvT], writes=[bpsv])
                mm(psv[:, :], onesb[0:1, 0:128], bvr[0:1, half * 512:(half + 1) * 512],
                   start=False, stop=True, reads=[b_onesb, b_bvr], writes=[bpsv])
                vop("tensor_copy", v_tm[:, half * 512:(half + 1) * 512], psv[:, :],
                    reads=[bpsv], writes=[b_vtm])
            if "ktm" in dbg and t == 0 and s == 0:
                dma("scalar", dbg["ktm"][:, :], k_tm[:, :], reads=[b_ktm])
                dma("scalar", dbg["vtm"][:, :], v_tm[:, :], reads=[b_vtm])

        def kv_acc(s):
            for h in range(NH):
                pb, fo = (h % 2) * 64, (h // 2) * 64
                mm(ps_kvh[pb:pb + 64, fo:fo + 64],
                   k_tm[:, h * 64:(h + 1) * 64], v_tm[:, h * 64:(h + 1) * 64],
                   start=(s == 0 and h < 2),
                   stop=(s == NSUB - 1),
                   reads=[b_ktm, b_vtm], writes=[b_pkvh], skip_gc=True)
                mm(ps_ksum[pb:pb + 64, (h // 2):(h // 2) + 1],
                   k_tm[:, h * 64:(h + 1) * 64], onescb[:, 0:1],
                   start=(s == 0 and h < 2), stop=(s == NSUB - 1),
                   reads=[b_ktm, b_onescb], writes=[b_pksum], skip_gc=True)

        if "kv" not in ablate:
            for s in range(NSUB):
                kv_proj(s)
                kv_acc(s)

    def stage_tail(t):
        if "kv" in ablate:
            pass
        elif t == 0:
            vop("tensor_copy", kvacc[:, 0:512], ps_kvh[:, :], reads=[b_pkvh], writes=[b_kvacc])
            vop("tensor_copy", kvacc[:, 512:520], ps_ksum[:, 0:8], reads=[b_pksum], writes=[b_kvacc])
        else:
            vop("tensor_add", kvacc[:, 0:512], kvacc[:, 0:512], ps_kvh[:, :], reads=[b_pkvh, b_kvacc], writes=[b_kvacc])
            vop("tensor_add", kvacc[:, 512:520], kvacc[:, 512:520], ps_ksum[:, 0:8], reads=[b_pksum, b_kvacc], writes=[b_kvacc])
        if "xr" in dbg:
            dma("scalar", dbg["xr"][t], xr_sb[:, (t % 2) * TILE:(t % 2 + 1) * TILE], reads=[b_xr2[t % 2]])
        for s in (range(NSUB) if "gating" not in ablate else ()):
            g = t * NSUB + s
            mm(ps_trB[0:128, 0:NE], xr_sb[:, (t % 2) * TILE + s * 128:(t % 2) * TILE + (s + 1) * 128], wguT[:, :],
               start=True, stop=True, reads=[b_xr2[t % 2], b_wguT], writes=[b_trB])
            if "lg" in dbg:
                vop("tensor_copy", lgbuf[:, g * NE:(g + 1) * NE], ps_trB[0:128, 0:NE], reads=[b_trB], writes=[b_lg])
            vop("reduce_max", gsc[:, 0:1], ps_trB[0:128, 0:NE], AX.X, reads=[b_trB], writes=[b_gsc])
            S.add("vector", lambda h: h.tensor_scalar(gsc[:, 8:16], ps_trB[0:128, 0:NE], gsc[:, 0:1], None, ALU.subtract),
                  reads=[b_trB, b_gsc], writes=[b_gsc])
            act(gsc[:, 16:24], gsc[:, 8:16], AF.Exp, reads=[b_gsc], writes=[b_gsc])
            vop("reduce_sum", gsc[:, 1:2], gsc[:, 16:24], AX.X, reads=[b_gsc], writes=[b_gsc])
            S.add("vector", lambda h, g=g: h.reciprocal(S_all[:, g:g + 1], gsc[:, 1:2]),
                  reads=[b_gsc], writes=[b_S])
            S.add("vector", lambda h: h.tensor_scalar(gsc[:, 24:32], gsc[:, 8:16], 0.0, None, ALU.is_equal),
                  reads=[b_gsc], writes=[b_gsc])
            vop("tensor_mul", gsc[:, 32:40], gsc[:, 24:32], pw2b[:, :], reads=[b_gsc, b_pw2], writes=[b_gsc])
            vop("reduce_max", gsc[:, 2:3], gsc[:, 32:40], AX.X, reads=[b_gsc], writes=[b_gsc])
            S.add("vector", lambda h: h.tensor_scalar(gsc[:, 40:48], gsc[:, 32:40], gsc[:, 2:3], None, ALU.is_equal),
                  reads=[b_gsc], writes=[b_gsc])
            S.add("vector", lambda h, g=g: h.tensor_copy(C_all[:, g * NE:(g + 1) * NE], gsc[:, 40:48]),
                  reads=[b_gsc], writes=[b_C])

    stage_ln(0)
    for t in range(NTILE):
        stage_proj(t)
        if t + 1 < NTILE:
            stage_ln(t + 1)
        stage_tail(t)

    if True:
        S.add("vector", lambda h: h.tensor_scalar(contrib[:, 0:520], kvacc[:, :], bmb[:, 0:1], None, ALU.mult),
          reads=[b_kvacc, b_bmb], writes=[b_contrib])
    S.add("vector", lambda h: h.tensor_scalar(contrib[:, 520:1040], kvacc[:, :], bmb[:, 1:2], None, ALU.mult),
          reads=[b_kvacc, b_bmb], writes=[b_contrib])
    if "kv" in dbg:
        dma("scalar", dbg["kv"][:, :], kvacc[:, :], reads=[b_kvacc])
    dma("gpsimd", cc_in.ap().rearrange("(b p) f -> p b f", p=128),
        contrib.ap().rearrange("p (b f) -> p b f", b=2),
        reads=[b_contrib], writes=[b_ccin])
    fence1 = S.snapshot()
    cc_op = S.add("gpsimd",
                  lambda h: h.collective_compute("AllReduce", ALU.add,
                                                 replica_groups=[list(range(NCORE))],
                                                 ins=[cc_in.ap().opt()], outs=[cc_out.ap().opt()]),
                  reads=[b_ccin], writes=[b_ccout], cc=True)

    a2 = Arena(ARENA_BASE)
    xnr, b_xnr = a2.take("xnr", [128, 8 * 2048], BF16)
    b_xnr4 = [Buf(f"xnr{i}") for i in range(4)]
    xns, _ = a2.take("xns", [128, 2 * 8 * 512], BF16)
    b_xns = [Buf("xns0"), Buf("xns1")]
    w1q, _ = a2.take("w1q", [128, 2 * 8 * 1024], BF16)
    b_w1q = [Buf("w1q0"), Buf("w1q1")]
    w2q, _ = a2.take("w2q", [128, 2 * 8 * 1024], BF16)
    b_w2q = [Buf("w2q0"), Buf("w2q1")]
    b2e, _ = a2.take("b2e", [1, 2 * D], BF16)
    b_b2e = [Buf("b2e0"), Buf("b2e1")]
    h_sb, b_h = a2.take("h_sb", [128, 8 * 512], BF16)

    ps_hA, b_hA = P512[0], bP512[0]
    ps_hB, b_hB = P512[1], bP512[1]
    ps_gA, b_gA = P1024, bP1024
    ps_gB0, b_gB0 = P512[2], bP512[2]
    ps_gB1, b_gB1 = P512[3], bP512[3]

    xnr4 = xnr.ap().rearrange("p (c tt) -> p c tt", c=8)
    xns4 = xns.ap().rearrange("p (a c f) -> p a c f", a=2, c=8)
    w1q4 = w1q.ap().rearrange("p (a c f) -> p a c f", a=2, c=8)
    w2q4 = w2q.ap().rearrange("p (a c f) -> p a c f", a=2, c=8)

    S.add("vector", lambda h: h.memset(g_pre[:, :], 0.0), reads=[b_gpre],
          writes=[b_gpre] + b_gpre_ch, extra_waits=())

    for t in range(4):
        dma("sync" if t == 0 else "scalar", xnr4[:, :, t * TILE:(t + 1) * TILE],
            xn_spill[t].rearrange("(c p) f -> p c f", p=128),
            reads=[b_xnsp], writes=[b_xnr, b_xnr4[t]], extra=fence1)

    b_w1q0h = [Buf("w1q0a"), Buf("w1q0b")]
    b_w2q0h = [Buf("w2q0a"), Buf("w2q0b")]

    def w_load(j):
        e, qtr = divmod(j, 4)
        par = j % 2
        if j == 0:
            for hh in range(2):
                dma("gpsimd", w1q4[:, 0, :, hh * 512:(hh + 1) * 512],
                    w1_in[e][:, hh * 512:(hh + 1) * 512].rearrange("(c p) f -> p c f", p=128),
                    reads=[b_w1q[0]], writes=[b_w1q0h[hh]], extra=fence1)
            for hh in range(2):
                dma("gpsimd", w2q4[:, 0, hh * 4:(hh + 1) * 4, :],
                    w2_in[e][hh * 512:(hh + 1) * 512, :].rearrange("(c p) f -> p c f", p=128),
                    reads=[b_w2q[0]], writes=[b_w2q0h[hh]], extra=fence1)
        else:
            extra_w = [b_w1q0h[0].w, b_w1q0h[1].w] if j == 2 else []
            extra_w2 = [b_w2q0h[0].w, b_w2q0h[1].w] if j == 2 else []
            dma("gpsimd", w1q4[:, par, :, :],
                w1_in[e][:, qtr * 1024:(qtr + 1) * 1024].rearrange("(c p) f -> p c f", p=128),
                reads=[b_w1q[par]] + ([b_w1q0h[0], b_w1q0h[1]] if j == 2 else []),
                writes=[b_w1q[par]] + ([b_w1q0h[0], b_w1q0h[1]] if j == 2 else []), extra=fence1)
            dma("gpsimd", w2q4[:, par, :, :],
                w2_in[e][qtr * 1024:(qtr + 1) * 1024, :].rearrange("(c p) f -> p c f", p=128),
                reads=[b_w2q[par]] + ([b_w2q0h[0], b_w2q0h[1]] if j == 2 else []),
                writes=[b_w2q[par]] + ([b_w2q0h[0], b_w2q0h[1]] if j == 2 else []), extra=fence1)
        if qtr == 0:
            pe_ = e % 2
            dma("gpsimd", b2e[0:1, pe_ * D:(pe_ + 1) * D], b2row_in[0:1, e * D:(e + 1) * D],
                reads=[b_b2e[pe_]], writes=[b_b2e[pe_]], extra=fence1)

    if "ffn" not in ablate:
        w_load(0)
        w_load(1)

    for j in (range(NE * 4) if "ffn" not in ablate else ()):
        e, qtr = divmod(j, 4)
        par = j % 2
        bw1, bw2 = b_w1q[par], b_w2q[par]
        epar = e % 2
        for t in range(NTILE):
            if t < 4:
                rhs_sel = lambda ic, t=t: xnr4[:, ic, t * TILE:(t + 1) * TILE]
                b_rhs = b_xnr4[t]
            else:
                sp = (t - 4) % 2
                rhs_sel = lambda ic, sp=sp: xns4[:, sp, ic, :]
                b_rhs = b_xns[(t - 4) % 2]
            if t == 2:
                for i in range(2):
                    dma("scalar", xns4[:, i, :, :], xn_spill[4 + i].rearrange("(c p) f -> p c f", p=128),
                        reads=[b_xnsp, b_xns[i]], writes=[b_xns[i]], extra=fence1 if j == 0 else ())
            if t == 5:
                dma("scalar", xns4[:, 0, :, :], xn_spill[6].rearrange("(c p) f -> p c f", p=128),
                    reads=[b_xnsp, b_xns[0]], writes=[b_xns[0]])
            if t == 6:
                dma("scalar", xns4[:, 1, :, :], xn_spill[7].rearrange("(c p) f -> p c f", p=128),
                    reads=[b_xnsp, b_xns[1]], writes=[b_xns[1]])

            for hs in range(8):
                psh, bph = (ps_hA, b_hA) if hs % 2 == 0 else (ps_hB, b_hB)
                bw1u = b_w1q0h[hs // 4] if j == 0 else bw1
                for ic in range(8):
                    mm(psh[:, :], w1q4[:, par, ic, hs * 128:(hs + 1) * 128], rhs_sel(ic),
                       start=(ic == 0), stop=(ic == 7),
                       reads=[bw1u, b_rhs], writes=[bph])
                hsg = qtr * 8 + hs
                act(h_sb[:, hs * 512:(hs + 1) * 512], psh[:, :], AF.Gelu,
                    bias=b1t[:, hsg * NE + e: hsg * NE + e + 1],
                    reads=[bph, b_b1t], writes=[b_h])
            for sub in range(NSUB):
                g = t * NSUB + sub
                if sub % 2 == 0:
                    outs = ((ps_gA[:, 0:512], b_gA), (ps_gA[:, 512:1024], b_gA))
                else:
                    outs = ((ps_gB0[:, :], b_gB0), (ps_gB1[:, :], b_gB1))
                for oh in range(2):
                    po, bpo = outs[oh]
                    for hc in range(8):
                        bw2u = b_w2q0h[hc // 4] if j == 0 else bw2
                        mm(po, h_sb[:, hc * 512 + sub * 128: hc * 512 + sub * 128 + 128],
                           w2q4[:, par, hc, oh * 512:(oh + 1) * 512],
                           start=(hc == 0), stop=(hc == 7 and qtr != 0),
                           reads=[b_h, bw2u], writes=[bpo])
                    if qtr == 0:
                        mm(po, onesb[0:1, 0:128], b2e[0:1, epar * D + oh * 512: epar * D + (oh + 1) * 512],
                           start=False, stop=True, reads=[b_onesb, b_b2e[epar]], writes=[bpo])
                if sub % 2 == 0:
                    S.add("vector",
                          lambda h, g=g, e=e: h.scalar_tensor_tensor(
                              g_pre[:, g * D:(g + 1) * D], ps_gA[:, :], C_all[:, g * NE + e: g * NE + e + 1],
                              g_pre[:, g * D:(g + 1) * D], ALU.mult, ALU.add),
                          reads=[b_gA, b_C, b_gpre_ch[g]], writes=[b_gpre_ch[g]])
                else:
                    S.add("vector",
                          lambda h, g=g, e=e: h.scalar_tensor_tensor(
                              g_pre[:, g * D:g * D + 512], ps_gB0[:, :], C_all[:, g * NE + e: g * NE + e + 1],
                              g_pre[:, g * D:g * D + 512], ALU.mult, ALU.add),
                          reads=[b_gB0, b_C, b_gpre_ch[g]], writes=[b_gpre_ch[g]])
                    S.add("vector",
                          lambda h, g=g, e=e: h.scalar_tensor_tensor(
                              g_pre[:, g * D + 512:(g + 1) * D], ps_gB1[:, :], C_all[:, g * NE + e: g * NE + e + 1],
                              g_pre[:, g * D + 512:(g + 1) * D], ALU.mult, ALU.add),
                          reads=[b_gB1, b_C, b_gpre_ch[g]], writes=[b_gpre_ch[g]])
        if j + 2 <= NE * 4 - 1:
            w_load(j + 2)

    if "gpre" in dbg:
        dma("scalar", dbg["gpre"].ap().rearrange("g p f -> p g f"),
            g_pre.ap().rearrange("p (g f) -> p g f", g=NCH), reads=b_gpre_ch)
    if "C" in dbg:
        dma("scalar", dbg["C"][:, :], C_all[:, :], reads=[b_C])
    if "lg" in dbg:
        dma("scalar", dbg["lg"][:, :], lgbuf[:, :], reads=[b_lg])
    if "Sc" in dbg:
        dma("scalar", dbg["Sc"][:, :], S_all[:, :], reads=[b_S])

    fence2 = S.snapshot()

    a3 = Arena(ARENA_BASE)
    WoT, b_WoT = a3.take("WoT", [128, 8 * D], BF16)
    wstage2, b_wstage2 = a3.take("wstage2", [128, D], BF16)
    kvfull, b_kvfull = a3.take("kvfull", [128, 1040], F32)
    kvm, b_kvm = a3.take("kvm", [128, 520], F32)
    bor, b_bor = a3.take("bor", [1, D], BF16)
    kvaug, b_kvaug = a3.take("kvaug", [128, 8 * 66], BF16)
    q_sb, _ = a3.take("q_sb", [128, 2 * 8 * 512], BF16)
    b_qsb = [Buf("qsb0"), Buf("qsb1")]
    xh2, _ = a3.take("xh2", [128, 2 * 2 * D], F32)
    b_xh2 = [Buf("xh2_0"), Buf("xh2_1")]
    aztm, _ = a3.take("aztm", [128, 2 * 1024], BF16)
    b_aztm = [Buf("aztm0"), Buf("aztm1")]
    azf, _ = a3.take("azf", [128, 2 * 8 * 512], BF16)
    b_azf = [Buf("azf0"), Buf("azf1")]
    den_sb, b_den = a3.take("den_sb", [128, 16], F32)
    zs_sb, b_zs = a3.take("zs_sb", [128, 16], F32)
    out_sb, b_out = a3.take("out_sb", [128, 4 * D], F32)
    gs_sb, b_gs = a3.take("gs_sb", [128, D], F32)
    mres, b_mres = a3.take("mres", [128, 512], F32)

    ps_atA, b_atA = P512[0], bP512[0]
    ps_atB, b_atB = P512[1], bP512[1]
    b_tr2 = [bP1024, Buf("ps1024_b1")]

    dma("gpsimd", bor[:, :], bo_row_in[:, :], writes=[b_bor], extra=fence2)
    for oc in range(8):
        dma("gpsimd", wstage2[:, :], W_in["Wo"][oc * 128:(oc + 1) * 128, :],
            reads=[b_wstage2], writes=[b_wstage2], extra=fence2)
        for ic in range(8):
            psb, bps = (ps_atA, b_atA) if ic % 2 == 0 else (ps_atB, b_atB)
            pbf = psb.ap().bitcast(BF16)
            tr(pbf[:, 0:128], wstage2[:, ic * 128:(ic + 1) * 128], idb[:, :],
               reads=[b_wstage2, b_idb], writes=[bps], extra=fence2)
            S.add("scalar",
                  lambda h, c=ic * D + oc * 128, p=pbf: h.copy(WoT[:, c:c + 128], p[:, 0:128]),
                  reads=[bps], writes=[b_WoT])

    dma("gpsimd", kvfull.ap().rearrange("p (b f) -> p b f", b=2),
        cc_out.ap().rearrange("(b p) f -> p b f", p=128),
        reads=[b_ccout], writes=[b_kvfull], extra=[cc_op] + list(fence2))
    if "kvfull" in dbg:
        dma("scalar", dbg["kvfull"][:, :], kvfull[:, :], reads=[b_kvfull])
    S.add("vector", lambda h: h.tensor_scalar(kvm[:, :], kvfull[:, 0:520], bmb[:, 0:1], None, ALU.mult),
          reads=[b_kvfull, b_bmb], writes=[b_kvm])
    S.add("vector", lambda h: h.scalar_tensor_tensor(kvm[:, :], kvfull[:, 520:1040], bmb[:, 1:2], kvm[:, :], ALU.mult, ALU.add),
          reads=[b_kvfull, b_bmb, b_kvm], writes=[b_kvm])
    for h_ in range(NH):
        pb, fo, sl = (h_ % 2) * 64, (h_ // 2) * 64, h_ // 2
        vop("tensor_copy", kvaug[pb:pb + 64, sl * 66:sl * 66 + 64],
            kvm[pb:pb + 64, fo:fo + 64], reads=[b_kvm], writes=[b_kvaug])
        vop("tensor_copy", kvaug[pb:pb + 64, sl * 66 + 64:sl * 66 + 65],
            kvm[pb:pb + 64, 512 + sl:512 + sl + 1], reads=[b_kvm], writes=[b_kvaug])

    q4 = q_sb.ap().rearrange("p (a c f) -> p a c f", a=2, c=8)
    xh2v = xh2.ap().rearrange("p (a u f) -> p a u f", a=2, u=2)

    for t in (range(NTILE) if "attn" not in ablate else ()):
        qpar = t % 2
        azp = t % 2
        if t == 0:
            dma("sync", q4[:, 0, :, :], q_spill[0].rearrange("(c p) f -> p c f", p=128),
                reads=[b_qsp, b_qsb[0]], writes=[b_qsb[0]], extra=fence2)
            for hf in range(2):
                dma("sync", xh2v[:, hf, :, :],
                    x_in[hf * 256:(hf + 1) * 256, :].rearrange("(u p) f -> p u f", p=128),
                    reads=[b_xh2[hf]], writes=[b_xh2[hf]], extra=fence2)

        def p3_att(sub):
            pA, bA = (P512[0], bP512[0]) if sub % 2 == 0 else (P512[2], bP512[2])
            pB, bB = (P512[1], bP512[1]) if sub % 2 == 0 else (P512[3], bP512[3])
            for h_ in range(NH):
                qrow = (h_ % 2) * 64
                sl = h_ // 2
                qT = q4[qrow:qrow + 64, qpar, sl, sub * 128:(sub + 1) * 128]
                po, bo_ = (pA, bA) if h_ % 2 == 0 else (pB, bB)
                mm(po[:, sl * 64:sl * 64 + 64], qT,
                   kvaug[qrow:qrow + 64, sl * 66:sl * 66 + 64],
                   start=True, stop=True, reads=[b_qsb[qpar], b_kvaug], writes=[bo_])

        def p3_den(sub):
            pD0, bD0 = P512[4], bP512[4]
            pD1, bD1 = P512[5], bP512[5]
            for h_ in range(NH):
                qrow = (h_ % 2) * 64
                sl = h_ // 2
                qT = q4[qrow:qrow + 64, qpar, sl, sub * 128:(sub + 1) * 128]
                pd, bd = (pD0, bD0) if h_ % 2 == 0 else (pD1, bD1)
                mm(pd[:, sub * 8 + sl:sub * 8 + sl + 1],
                   qT, kvaug[qrow:qrow + 64, sl * 66 + 64:sl * 66 + 65],
                   start=True, stop=True, reads=[b_qsb[qpar], b_kvaug], writes=[bd])

        def p3_z(sub):
            pA, bA = (P512[0], bP512[0]) if sub % 2 == 0 else (P512[2], bP512[2])
            pB, bB = (P512[1], bP512[1]) if sub % 2 == 0 else (P512[3], bP512[3])
            pD0, bD0 = P512[4], bP512[4]
            pD1, bD1 = P512[5], bP512[5]
            S.add("vector", lambda h, pD0=pD0, sub=sub: h.tensor_scalar_add(den_sb[:, 0:8], pD0[:, sub * 8:sub * 8 + 8], 1e-6),
                  reads=[bD0], writes=[b_den])
            S.add("vector", lambda h, pD1=pD1, sub=sub: h.tensor_scalar_add(den_sb[:, 8:16], pD1[:, sub * 8:sub * 8 + 8], 1e-6),
                  reads=[bD1], writes=[b_den])
            S.add("vector", lambda h: h.reciprocal(zs_sb[:, :], den_sb[:, :]),
                  reads=[b_den], writes=[b_zs])
            for h_ in range(NH):
                sl = h_ // 2
                po, bo_ = (pA, bA) if h_ % 2 == 0 else (pB, bB)
                src = po[:, sl * 64:sl * 64 + 64]
                zcol = sl + 8 * (h_ % 2)
                dst = aztm[:, (sub % 2) * 1024 + h_ * 64:(sub % 2) * 1024 + (h_ + 1) * 64]
                S.add("vector", lambda h, d=dst, sr=src, zc=zcol: h.tensor_scalar(d, sr, zs_sb[:, zc:zc + 1], None, ALU.mult),
                      reads=[bo_, b_zs], writes=[b_aztm[sub % 2]])

        def p3_tr(sub):
            for ic in range(8):
                trbank = ic % 2
                pt = P1024.ap().bitcast(BF16)[:, trbank * 1024 + 0:trbank * 1024 + 128]
                tr(pt, aztm[:, (sub % 2) * 1024 + ic * 128:(sub % 2) * 1024 + (ic + 1) * 128], idb[:, :],
                   reads=[b_aztm[sub % 2], b_idb], writes=[b_tr2[trbank]])
                col = ic * TILE + sub * 128
                S.add("scalar", lambda h, c=col, p=pt, azp=azp: h.copy(
                      azf[:, azp * 4096 + c: azp * 4096 + c + 128], p),
                      reads=[b_tr2[trbank]], writes=[b_azf[azp]])

        if "p3cb" not in ablate:
            p3_att(0)
            p3_den(0)
            p3_att(1)
            for sub in range(NSUB):
                if "p3z" not in ablate:
                    p3_z(sub)
                if sub + 1 < NSUB:
                    p3_den(sub + 1)
                if sub + 2 < NSUB:
                    p3_att(sub + 2)
                if "p3z" not in ablate:
                    p3_tr(sub)

        if t + 1 < NTILE:
            dma("sync", q4[:, (t + 1) % 2, :, :], q_spill[t + 1].rearrange("(c p) f -> p c f", p=128),
                reads=[b_qsp, b_qsb[(t + 1) % 2]], writes=[b_qsb[(t + 1) % 2]])
        for sub in (range(NSUB) if "p3wo" not in ablate else ()):
            g = t * NSUB + sub
            par = (t * 2 + sub // 2) % 2
            S.add("scalar",
                  lambda h, g=g: h.activation(gs_sb[:, :], g_pre[:, g * D:(g + 1) * D], AF.Silu,
                                              scale=S_all[:, g:g + 1]),
                  reads=[b_gpre_ch[g], b_S], writes=[b_gs])
            for oh in range(2):
                pw, bpw = P512[sub], bP512[sub]
                for ic in range(8):
                    mm(pw[:, :], azf[:, azp * 4096 + ic * TILE + sub * 128: azp * 4096 + ic * TILE + sub * 128 + 128],
                       WoT[:, ic * D + oh * 512:(ic) * D + (oh + 1) * 512],
                       start=(ic == 0), stop=False, reads=[b_azf[azp], b_WoT], writes=[bpw])
                mm(pw[:, :], onesb[0:1, 0:128], bor[0:1, oh * 512:(oh + 1) * 512],
                   start=False, stop=True, reads=[b_onesb, b_bor], writes=[bpw])
                S.add("vector", lambda h, pw=pw, oh=oh: h.tensor_mul(
                      mres[:, :], pw[:, :], gs_sb[:, oh * 512:(oh + 1) * 512]),
                      reads=[bpw, b_gs], writes=[b_mres])
                S.add("vector",
                      lambda h, sub=sub, par=par, oh=oh: h.tensor_add(
                          out_sb[:, sub * D + oh * 512: sub * D + (oh + 1) * 512],
                          xh2v[:, par, sub % 2, oh * 512:(oh + 1) * 512], mres[:, :]),
                      reads=[b_xh2[par], b_mres], writes=[b_out])
        dma("sync", y_out[t * TILE:(t + 1) * TILE, :].rearrange("(s p) f -> p s f", p=128),
            out_sb.ap().rearrange("p (s f) -> p s f", s=NSUB),
            reads=[b_out], writes=[b_y])
        if t + 1 < NTILE:
            for hf in range(2):
                dma("sync", xh2v[:, hf, :, :],
                    x_in[(t + 1) * TILE + hf * 256: (t + 1) * TILE + (hf + 1) * 256, :].rearrange("(u p) f -> p u f", p=128),
                    reads=[b_xh2[hf]], writes=[b_xh2[hf]])

    S.add("gpsimd", lambda h: h.nop(), extra_waits=S.snapshot())

    guards = S.emit(nc)
    return nc


def kernel(**inputs):
    x = np.ascontiguousarray(np.asarray(inputs["L0_attn_hidden_states"], dtype=np.float32).reshape(2 * 16384, D))
    Wq = np.ascontiguousarray(np.asarray(inputs["L0_attn_Wq"], dtype=np.float32))
    Wk = np.ascontiguousarray(np.asarray(inputs["L0_attn_Wk"], dtype=np.float32))
    Wv = np.ascontiguousarray(np.asarray(inputs["L0_attn_Wv"], dtype=np.float32))
    Wo = np.ascontiguousarray(np.asarray(inputs["L0_attn_Wo"], dtype=np.float32))
    bq = np.asarray(inputs["L0_attn_bq"], dtype=np.float32)
    bk = np.asarray(inputs["L0_attn_bk"], dtype=np.float32)
    bv = np.asarray(inputs["L0_attn_bv"], dtype=np.float32)
    bo = np.asarray(inputs["L0_attn_bo"], dtype=np.float32)
    wg_red = np.ascontiguousarray(np.asarray(inputs["L0_ffn_wg_red"], dtype=np.float32))
    wg = np.ascontiguousarray(np.asarray(inputs["L0_ffn_wg"], dtype=np.float32))
    w1 = np.ascontiguousarray(np.asarray(inputs["L0_ffn_w1"], dtype=np.float32))
    b1 = np.asarray(inputs["L0_ffn_b1"], dtype=np.float32)
    w2 = np.ascontiguousarray(np.asarray(inputs["L0_ffn_w2"], dtype=np.float32))
    b2 = np.asarray(inputs["L0_ffn_b2"], dtype=np.float32)

    bq_t = np.ascontiguousarray(bq.reshape(8, 128).T)
    bo_t = np.ascontiguousarray(bo.reshape(8, 128).T)
    bk_row = np.ascontiguousarray(bk.reshape(1, D))
    bo_row = np.ascontiguousarray(bo.reshape(1, D))
    bv_row = np.ascontiguousarray(bv.reshape(1, D))
    b1_t = np.ascontiguousarray(b1.reshape(NE, 32, 128).transpose(2, 1, 0).reshape(128, 32 * NE))
    b2row = np.ascontiguousarray(b2.reshape(1, NE * D))
    ident = np.eye(128, dtype=np.float32)
    ones_row = np.ones((1, 512), dtype=np.float32)
    ones_col = np.ones((128, 1), dtype=np.float32)
    pw2 = (2.0 ** -np.arange(NE, dtype=np.float32)).reshape(1, NE)

    nc = build_program(debug=tuple(DEBUG), ablate=frozenset(ABLATE))

    in_maps = []
    for c in range(NCORE):
        bmask = np.zeros((1, 2), dtype=np.float32)
        bmask[0, c // 4] = 1.0
        in_maps.append({
            "x": x[c * TOK:(c + 1) * TOK],
            "Wq": Wq, "Wk": Wk, "Wv": Wv, "Wo": Wo,
            "bq_t": bq_t, "bo_t": bo_t, "bk_row": bk_row, "bv_row": bv_row, "bo_row": bo_row,
            "wg_red": wg_red, "wg": wg, "w1": w1, "b1_t": b1_t, "w2": w2, "b2row": b2row,
            "ident": ident, "ones_row": ones_row, "ones_col": ones_col,
            "pw2": pw2, "bmask": bmask,
        })

    res = run_bass_kernel_spmd(nc, in_maps, core_ids=list(range(NCORE)))
    out = np.concatenate([res.results[c]["y"] for c in range(NCORE)], axis=0)
    if DEBUG:
        kernel.debug_results = res.results
    return out.reshape(2, 16384, D)


if __name__ == "__main__":
    build_program()
    print("build ok")
